# version 13
# baseline (speedup 1.0000x reference)
import math
from collections import defaultdict
from contextlib import ExitStack

import numpy as np
import ml_dtypes

import concourse.bass as bass
import concourse.mybir as mybir
from concourse.bass_utils import run_bass_kernel_spmd

F32 = mybir.dt.float32
F32R = mybir.dt.float32r
BF16 = mybir.dt.bfloat16
AF = mybir.ActivationFunctionType
ALU = mybir.AluOpType
AX = mybir.AxisListType

D = 1024
S = 2048
B = 32
NCORES = 8
NSEQ = B // NCORES
NH = 16
HD = 64
NLH = 8
LDIM = 512
NLAT = 64
POOLC = 32
WIN = 256
EPS = 1e-5
NT = S // 128
NC = D // 128
NG = 4
HPG = NH // NG
QAW = 272

import os as _os
ENABLE = {"atta": True, "ffn": True, "attb": True}
DBG = _os.environ.get("KDBG", "")



def host_tables():
    t = {}
    d2 = HD // 2
    mel_max = 2595.0 * np.log10(1.0 + 4000.0 / 200.0)
    scale = np.power(10.0, np.linspace(0.0, mel_max, d2, dtype=np.float64) / 2595.0) - 1.0
    fb = 200.0 * scale / 1000.0
    freqs = np.arange(S, dtype=np.float64)[:, None] * fb[None, :]
    c = np.cos(freqs); s = np.sin(freqs)
    C1 = np.zeros((S, HD), np.float32); C2 = np.zeros((S, HD), np.float32)
    CS = np.zeros((S, HD), np.float32)
    C1[:, 0::2] = c; C1[:, 1::2] = s
    C2[:, 0::2] = s; C2[:, 1::2] = c
    CS[:, 0::2] = c + s; CS[:, 1::2] = c - s
    t["tab_c1"] = C1; t["tab_c2"] = C2; t["tab_cs"] = CS
    ki = np.arange(128)[:, None]; qi = np.arange(128)[None, :]
    lm = np.zeros((128, 384), np.float32)
    lm[:, 0:128] = (qi < ki)
    lm[:, 128:256] = 1.0
    lm[:, 256:384] = (ki <= qi)
    t["tab_localmask"] = lm.astype(ml_dtypes.bfloat16)
    n = np.arange(NLAT)[:, None]; tt = np.arange(S)[None, :]
    t["tab_latmask"] = (n <= (tt // POOLC)).astype(np.float32).astype(ml_dtypes.bfloat16)
    t["tab_ident"] = np.eye(128, dtype=np.float32).astype(ml_dtypes.bfloat16)
    t["tab_identf"] = np.eye(128, dtype=np.float32)
    ab = np.arange(D, dtype=np.float64)
    ang = (np.outer(ab, ab) % D) * (2.0 * np.pi / D)
    t["tab_fcos"] = np.cos(ang).astype(np.float32)
    t["tab_fsin"] = np.sin(ang).astype(np.float32)
    e16 = np.zeros((16, 8 * 128), np.float32)
    for cc in range(8):
        e16[2 * cc, cc * 128: cc * 128 + 64] = 1.0
        e16[2 * cc + 1, cc * 128 + 64: cc * 128 + 128] = 1.0
    t["tab_e16"] = e16.astype(ml_dtypes.bfloat16)
    t0 = np.zeros((128, 4), np.float32)
    for j in range(4):
        t0[32 * j:32 * (j + 1), j] = 1.0
    t["tab_t0"] = t0.astype(ml_dtypes.bfloat16)
    return t



class Prog:
    ENGINES = ("tensor", "vector", "scalar", "gpsimd", "sync")
    DMA_Q = ("q0", "q1", "q2", "q3", "q4", "q5")
    DMA_G = ("g0", "g1", "g2", "g3")

    def __init__(self, nc):
        self.nc = nc
        self.q = {e: [] for e in self.ENGINES}
        self.cnt = defaultdict(int)
        self.rr = {"sync": 0, "gpsimd": 0}

    def val(self, sem):
        return self.cnt[sem]

    AUTO = {"vector": "dve", "scalar": "act", "gpsimd": "pool"}

    def op(self, eng, fn, wait=(), inc=None, amt=1, selfsync=True):
        if fn is not None and selfsync and eng in self.AUTO and inc is None:
            inc = self.AUTO[eng]
            amt = 1
        self.q[eng].append((fn, [(s, v) for (s, v) in wait if v > 0], inc, amt))
        if inc is not None:
            self.cnt[inc] += amt
            v = self.cnt[inc]
            if fn is not None and selfsync and eng in self.AUTO:
                self.q[eng].append(("W", [(inc, v)], None, 0))
            return v
        return None

    def dma(self, eng, fn, wait=()):
        pool = self.DMA_G if eng == "gpsimd" else self.DMA_Q
        sem = pool[self.rr[eng] % len(pool)]
        self.rr[eng] += 1
        prev = self.cnt[sem]
        w = [(s, v) for (s, v) in wait if v > 0]
        if prev:
            w.append((sem, prev))
        v = self.op(eng, fn, wait=w, inc=sem, amt=16, selfsync=False)
        return (sem, v)

    def dmark(self):
        return [(s, self.cnt[s]) for s in self.DMA_Q + self.DMA_G if self.cnt[s]]

    def barrier(self):
        vals = {}
        for e in ("tensor", "vector", "scalar", "gpsimd"):
            vals[e] = self.op(e, None, inc="b_" + e)
        for e in self.ENGINES:
            w = [("b_" + o, vals[o]) for o in vals if o != e]
            self.q[e].append(("W", w, None, 0))

    def vop(self, fn):
        vv = self.op("vector", fn, inc="dve")
        self.q["vector"].append(("W", [("dve", vv)], None, 0))
        return vv

    def wait(self, eng, wait):
        self.q[eng].append(("W", [(s, v) for (s, v) in wait if v > 0], None, 0))

    def finalize(self, block, sems):
        for ename in self.ENGINES:
            entries = self.q[ename]

            def run(engine, entries=entries, ename=ename):
                for fn, waits, inc, amt in entries:
                    for smn, v in waits:
                        engine.wait_ge(sems[smn], v)
                    if fn == "W":
                        continue
                    if fn is None:
                        assert inc is not None
                        engine.sem_inc(sems[inc], amt)
                        continue
                    instr = fn(engine)
                    if inc is not None:
                        assert instr is not None, f"{ename} op must return instr for {inc}"
                        instr.then_inc(sems[inc], amt)

            getattr(block, ename)(run)



def build_program(n_seq=NSEQ):
    nc = bass.Bass("TRN2", target_bir_lowering=False)

    xin = nc.dram_tensor("xs", [n_seq, S, D], F32, kind="ExternalInput").ap()
    yout = nc.dram_tensor("ys", [n_seq, S, D], F32, kind="ExternalOutput").ap()
    dbg = {}
    if "dump" in DBG:
        dbg["rows"] = nc.dram_tensor("dbg_rows", [1, 5 * S], BF16, kind="ExternalOutput").ap()
        dbg["rrep"] = nc.dram_tensor("dbg_rrep", [128, S], F32, kind="ExternalOutput").ap()
        dbg["colst"] = nc.dram_tensor("dbg_colst", [128, 64], F32, kind="ExternalOutput").ap()
        dbg["chain"] = nc.dram_tensor("dbg_chain", [128, 64], F32, kind="ExternalOutput").ap()
        dbg["r1"] = nc.dram_tensor("dbg_r1", [128, 512], F32, kind="ExternalOutput").ap()
        dbg["ops"] = nc.dram_tensor("dbg_ops", [128, 512], F32, kind="ExternalOutput").ap()
        dbg["qnf"] = nc.dram_tensor("dbg_qnf", [128, S], BF16, kind="ExternalOutput").ap()
        dbg["knf"] = nc.dram_tensor("dbg_knf", [128, S], BF16, kind="ExternalOutput").ap()
        dbg["vb"] = nc.dram_tensor("dbg_vb", [128, NT * 260], BF16, kind="ExternalOutput").ap()
        dbg["pb"] = nc.dram_tensor("dbg_pb", [128, 1024], BF16, kind="ExternalOutput").ap()
        dbg["stkv"] = nc.dram_tensor("dbg_stkv", [128, 2048], BF16, kind="ExternalOutput").ap()
        dbg["stqw"] = nc.dram_tensor("dbg_stqw", [128, 1040], BF16, kind="ExternalOutput").ap()
        dbg["strows"] = nc.dram_tensor("dbg_strows", [1, 16384], BF16, kind="ExternalOutput").ap()
        dbg["csprep"] = nc.dram_tensor("dbg_csprep", [1, D], BF16, kind="ExternalOutput").ap()

    win = {}
    for nm, shp in [
        ("aa_qn_w", [D]), ("aa_qn_b", [D]), ("aa_q_w", [D, D]), ("aa_q_b", [D]),
        ("aa_kvn_w", [D]), ("aa_kvn_b", [D]), ("aa_kv_w", [D, 2 * D]), ("aa_kv_b", [2 * D]),
        ("aa_out_w", [D, D]), ("aa_out_b", [D]), ("aa_ln_w", [HD]), ("aa_ln_b", [HD]),
        ("aa_rh_w", [D, 1]), ("aa_rh_b", [1]),
        ("ab_wq", [D, D]), ("ab_wk", [D, D]), ("ab_wv", [D, D]),
        ("ab_wgq", [D, LDIM]), ("ab_wgk", [D, LDIM]), ("ab_wgv", [D, LDIM]),
        ("ab_r_out_w", [LDIM, D]), ("ab_glw", [NH, HD]), ("ab_grw", [NH, HD]),
        ("ab_w_out", [D, D]),
        ("ffn_wr", [D, D]), ("ffn_wi", [D, D]), ("ffn_bias", [D]),
    ]:
        win[nm] = nc.dram_tensor(nm, shp, F32, kind="ExternalInput").ap()

    tabs = {}
    for nm, shp, dt_ in [
        ("tab_c1", [S, HD], F32), ("tab_c2", [S, HD], F32), ("tab_cs", [S, HD], F32),
        ("tab_localmask", [128, 384], BF16), ("tab_latmask", [NLAT, S], BF16),
        ("tab_ident", [128, 128], BF16), ("tab_identf", [128, 128], F32),
        ("tab_fcos", [D, D], F32),
        ("tab_fsin", [D, D], F32), ("tab_e16", [16, 8 * 128], BF16),
        ("tab_t0", [128, 4], BF16), ("tab_glwT", [HD, 2 * NH], F32),
    ]:
        tabs[nm] = nc.dram_tensor(nm, shp, dt_, kind="ExternalInput").ap()

    st = {}
    for nm, shp, dt_ in [
        ("st_qw", [NC, 128, 1040], BF16),
        ("st_kv", [NC, 128, 2 * D], BF16),
        ("st_outw", [NC, 128, D], BF16),
        ("st_M", [NC, 128, D], F32R),
        ("st_wq", [NC, 128, 1152], BF16),
        ("st_wk", [NC, 128, D], BF16),
        ("st_wv", [NC, 128, D], BF16),
        ("st_wgq", [NC, 128, LDIM], BF16),
        ("st_wgk", [NC, 128, LDIM], BF16),
        ("st_wgv", [NC, 128, LDIM], BF16),
        ("st_rout", [LDIM // 128, 128, D], BF16),
        ("st_wout", [NC, 128, D], BF16),
        ("st_rows", [1, 16384], BF16),
        ("st_mrow", [1, S], BF16),
        ("st_mrow64", [1, S], BF16),
        ("st_irow64", [1, S], BF16),
        ("st_xbf", [NC, 128, S], BF16),
    ]:
        st[nm] = nc.dram_tensor(nm, shp, dt_, kind="Internal").ap()

    RO = {}
    _o = [0]
    for nm, n in [("colsum_q", 1040), ("qb", 1040), ("colsum_kv", 2 * D),
                  ("kvb", 2 * D), ("colsum_M", D), ("ffn_bias", D),
                  ("colsum_wq", 1152), ("colsum_wk", D), ("colsum_wv", D),
                  ("colsum_wgq", LDIM), ("colsum_wgk", LDIM), ("colsum_wgv", LDIM)]:
        RO[nm] = _o[0]; _o[0] += n
    assert _o[0] <= 16384

    sb = {}
    def salloc(nm, shp, dt_):
        sb[nm] = nc.alloc_sbuf_tensor(nm, shp, dt_)
        return sb[nm]

    salloc("ident", [128, 128], BF16)
    salloc("identf", [128, 128], F32)
    salloc("lmask", [128, 384], BF16)
    salloc("wrep", [128, HD], F32)
    salloc("brep", [128, HD], F32)
    salloc("cbias", [128, 1], F32)
    salloc("rhbrep", [128, 1], F32)
    salloc("onesbf", [2, 128], BF16)
    salloc("onecolbf", [128, 1], BF16)
    salloc("onecolr", [128, 1], F32R)
    salloc("epscol", [128, 1], F32)
    salloc("colst", [128, 64], F32)
    salloc("glwt", [128, 2 * NH], BF16)
    salloc("t0", [128, 4], BF16)
    salloc("onesf32", [33, 64], F32)
    salloc("onesb32", [33, 64], BF16)
    CO = {"R2": 0, "r": 16, "rad": 32, "tmp": 48}
    salloc("rows", [1, 5 * S], BF16)
    RB = {"Rh": 0, "Rl": S, "m": 2 * S, "mR": 3 * S, "invR2": 4 * S}

    X = [salloc(f"X{c}", [128, S], F32) for c in range(NC)]
    AAB = 64 * 1024
    WAB = 56 * 1024
    aarena = nc.alloc_sbuf_tensor("aarena", [128, AAB], mybir.dt.uint8)
    warena = nc.alloc_sbuf_tensor("warena", [128, WAB], mybir.dt.uint8)
    AADDR = nc.lookup_mloc(aarena).addr
    WADDR = nc.lookup_mloc(warena).addr

    _vc = [0]
    _vcache = {}
    def aview(off, shp, dt_, nm="v"):
        key = ("a", off, tuple(shp), dt_)
        if key in _vcache:
            return _vcache[key]
        _vc[0] += 1
        esz = {F32: 4, F32R: 4, BF16: 2, mybir.dt.uint8: 1}[dt_]
        need = int(np.prod(shp[1:])) * esz
        assert off + need <= AAB, f"aview {nm} overflow {off}+{need}"
        t = nc.alloc_sbuf_tensor_at(f"a{_vc[0]}_{nm}", shp, dt_, offset=AADDR + off)
        _vcache[key] = t
        return t

    def wview(off, shp, dt_, nm="v"):
        key = ("w", off, tuple(shp), dt_)
        if key in _vcache:
            return _vcache[key]
        _vc[0] += 1
        esz = {F32: 4, F32R: 4, BF16: 2}[dt_]
        need = int(np.prod(shp[1:])) * esz
        assert off + need <= WAB, f"wview {nm} overflow {off}+{need}"
        t = nc.alloc_sbuf_tensor_at(f"w{_vc[0]}_{nm}", shp, dt_, offset=WADDR + off)
        _vcache[key] = t
        return t

    p = Prog(nc)
    est = ExitStack()
    sems = {}
    for smn in ["q0", "q1", "q2", "q3", "q4", "q5", "g0", "g1", "g2", "g3",
                "pe", "dve", "act", "pool", "free",
                "b_tensor", "b_vector", "b_scalar", "b_gpsimd"]:
        sems[smn] = est.enter_context(nc.semaphore(smn))
    block = est.enter_context(nc.Block())

    def mm(out_ap, lhsT_ap, rhs_ap, start, stop, wait=(), inc=None, tp=None):
        return p.op("tensor",
                    lambda t, o=out_ap, l=lhsT_ap, r=rhs_ap:
                    t.matmul(o, l, r, start=start, stop=stop, tile_position=tp),
                    wait=wait, inc=inc)

    def tr(out_ap, in_ap, wait=(), inc=None):
        np_ = in_ap.shape[0]
        base = sb["ident"] if in_ap.dtype != F32 else sb["identf"]
        idt = base[0:np_, 0:np_]
        return p.op("tensor", lambda t, o=out_ap, i=in_ap, idt=idt: t.transpose(o, i, idt),
                    wait=wait, inc=inc)

    _psn = [0]
    class PsumScope:
        def __init__(self):
            self.guards = []
            self.tensors = []
        def get(self, shape, dt_=F32):
            _psn[0] += 1
            g = nc.psum_tensor(f"ps{_psn[0]}", shape, dt_)
            t = g.__enter__()
            self.guards.append(g)
            self.tensors.append(t)
            return t
        def close(self):
            for g in reversed(self.guards):
                g.__exit__(None, None, None)
            self.guards = []

    def load_x(s):
        pss = PsumScope()
        pst = [pss.get([128, 128]) for _ in range(4)]
        tmb = [aview(0, [128, D], F32, "ldtm0"), aview(4096, [128, D], F32, "ldtm1")]
        ld_done = []
        for t in range(NT):
            bi = t % 2
            w = [("dve", ld_done[t - 2])] if t >= 2 else []
            v_l = p.dma("sync", lambda e, t=t, bi=bi: e.dma_start(
                out=tmb[bi][:], in_=xin[s].rearrange("(t p) d -> t p d", p=128)[t]),
                wait=w)
            for c in range(NC):
                pk = pst[(t * NC + c) % 4]
                wv = [v_l]
                if t * NC + c >= 4:
                    wv.append(("dve", p.val("dve") - 3))
                v_t = tr(pk[:], tmb[bi][:, 128 * c:128 * (c + 1)],
                         wait=wv, inc="pe")
                v_e = p.op("vector", lambda v, c=c, t=t, pk=pk: v.tensor_copy(
                    X[c][:, 128 * t:128 * (t + 1)], pk[:]),
                    wait=[("pe", v_t)], inc="dve")
            ld_done.append(p.val("dve"))
        for c in range(NC):
            p.dma("gpsimd", lambda e, c=c: e.dma_start(
                out=st["st_xbf"][c], in_=X[c][:]),
                wait=[("dve", p.val("dve"))] if c == 0 else ())
        p.barrier()
        pss.close()

    def store_y(s):
        p.barrier()
        pss = PsumScope()
        pst = [pss.get([128, 128]) for _ in range(4)]
        tmb = [aview(0, [128, D], F32, "sttm0"), aview(4096, [128, D], F32, "sttm1")]
        st_dx = []
        for t in range(NT):
            bi = t % 2
            for c in range(NC):
                pk = pst[(t * NC + c) % 4]
                wv = []
                if t >= 2:
                    wv.append(st_dx[t - 2])
                if t * NC + c >= 4:
                    wv.append(("dve", p.val("dve") - 3))
                v_t = tr(pk[:], X[c][:, 128 * t:128 * (t + 1)],
                         wait=wv, inc="pe")
                p.op("vector", lambda v, c=c, bi=bi, pk=pk: v.tensor_copy(
                    tmb[bi][:, 128 * c:128 * (c + 1)], pk[:]),
                    wait=[("pe", v_t)], inc="dve")
            v_s = p.dma("sync", lambda e, t=t, bi=bi: e.dma_start(
                out=yout[s].rearrange("(t p) d -> t p d", p=128)[t], in_=tmb[bi][:]),
                wait=[("dve", p.val("dve"))])
            st_dx.append(v_s)
        pss.close()


    def prep_consts():
        p.dma("sync", lambda e: e.dma_start(out=sb["ident"][:], in_=tabs["tab_ident"]))
        p.dma("sync", lambda e: e.dma_start(out=sb["identf"][:], in_=tabs["tab_identf"]))
        p.dma("sync", lambda e: e.dma_start(out=sb["lmask"][:], in_=tabs["tab_localmask"]))
        p.dma("sync", lambda e: e.dma_start(out=sb["t0"][:], in_=tabs["tab_t0"]))
        vqm = p.dmark()
        def vop(fn):
            vv = p.op("vector", fn, inc="dve")
            p.wait("vector", [("dve", vv)])
            return vv
        vop(lambda v_: v_.memset(sb["colst"][:], 0.0))
        vop(lambda v_: v_.memset(sb["onesf32"][:], 1.0))
        vop(lambda v_: v_.memset(sb["onesb32"][:], 1.0))
        vop(lambda v_: v_.memset(sb["onesbf"][:], 0.0))
        vop(lambda v_: v_.memset(sb["onesbf"][0:1, :], 1.0))
        vop(lambda v_: v_.memset(sb["onecolbf"][:], 1.0))
        vop(lambda v_: v_.memset(sb["epscol"][:], 1.0))
        vop(lambda v_: v_.tensor_copy(sb["onecolr"][:], sb["epscol"][:]))
        vop(lambda v_: v_.memset(sb["epscol"][:], EPS))
        p.op("vector", lambda v_: v_.engine_nop(), inc="dve")
        p.wait("tensor", vqm)
        p.wait("vector", vqm)
        p.barrier()

    def prep_ffn():
        pss = PsumScope()
        ps_a = pss.get([128, 512])
        ps_b = pss.get([128, 512])
        ps_row = pss.get([1, 1024])
        cosr = [wview(c * 4096, [128, D], F32R, "cos") for c in range(NC)]
        sinr = [aview(c * 4096, [128, D], F32R, "sin") for c in range(NC)]
        UTR = [aview(32768 + c * 2048, [128, 512], F32R, "utr") for c in range(NC)]
        UTI = [aview(49152 + c * 2048, [128, 512], F32R, "uti") for c in range(NC)]
        wrsl = [wview(32768 + c * 512, [128, 128], F32R, "wrsl") for c in range(NC)]
        wisl = [wview(36864 + c * 512, [128, 128], F32R, "wisl") for c in range(NC)]
        ldb = wview(40960, [128, D], F32, "mld")
        ldb2 = wview(45056, [128, 128], F32, "mld2")
        evb = wview(45568, [128, 512], F32, "mev")
        evb2 = wview(47616, [128, 512], F32, "mev2")
        mst = wview(49664, [128, 512], F32R, "mst")
        rowbf = wview(51712, [2, 1024], BF16, "mrowbf")

        for nm, dstl in (("tab_fcos", cosr), ("tab_fsin", sinr)):
            for c in range(NC):
                v_l = p.dma("sync", lambda e, c=c, nm=nm: e.dma_start(
                    out=ldb[:], in_=tabs[nm].rearrange("(c p) o -> c p o", p=128)[c]),
                    wait=[("dve", p.val("dve"))])
                p.op("vector", lambda v, c=c, dstl=dstl: v.tensor_copy(dstl[c][:], ldb[:]),
                     wait=[v_l], inc="dve")

        if "mcs" in DBG:
            p.barrier()
            pss.close()
            return
        for h in range(2 if "ms1" not in DBG else 1):
            dsl = slice(512 * h, 512 * (h + 1))
            for jc in range(NC):
                for kc in range(NC):
                    v_l = p.dma("sync", lambda e, jc=jc, kc=kc: e.dma_start(
                        out=ldb2[:], in_=win["ffn_wr"].rearrange(
                            "(kc p) o -> kc p o", p=128)[kc][:, 128 * jc:128 * (jc + 1)]),
                        wait=[("pe", p.val("pe")), ("dve", p.val("dve"))])
                    p.op("vector", lambda v, kc=kc: v.tensor_copy(wrsl[kc][:], ldb2[:]),
                         wait=[v_l], inc="dve")
                    v_l = p.dma("sync", lambda e, jc=jc, kc=kc: e.dma_start(
                        out=ldb2[:], in_=win["ffn_wi"].rearrange(
                            "(kc p) o -> kc p o", p=128)[kc][:, 128 * jc:128 * (jc + 1)]),
                        wait=[("dve", p.val("dve"))])
                    p.op("vector", lambda v, kc=kc: v.tensor_copy(wisl[kc][:], ldb2[:]),
                         wait=[v_l], inc="dve")
                v_w = p.val("dve")
                wv = [("dve", v_w)]
                for kc in range(NC):
                    mm(ps_a[:], wrsl[kc][:], cosr[kc][:, dsl], kc == 0, False, wait=wv)
                    wv = ()
                v1 = None
                for kc in range(NC):
                    v1 = mm(ps_a[:], wisl[kc][:], sinr[kc][:, dsl], False, kc == NC - 1,
                            inc="pe" if kc == NC - 1 else None)
                v_e1 = p.op("vector", lambda v, jc=jc: v.tensor_copy(
                    UTR[jc][:], ps_a[:]), wait=[("pe", v1)], inc="dve")
                v2 = None
                for kc in range(NC):
                    v2 = mm(ps_b[:], wisl[kc][:], cosr[kc][:, dsl], kc == 0, kc == NC - 1,
                            inc="pe" if kc == NC - 1 else None)
                v_e2 = p.op("vector", lambda v: v.tensor_copy(evb[:], ps_b[:]),
                            wait=[("pe", v2)], inc="dve")
                wv = [("dve", v_e1)]
                v3 = None
                for kc in range(NC):
                    v3 = mm(ps_a[:], wrsl[kc][:], sinr[kc][:, dsl], kc == 0, kc == NC - 1,
                            wait=wv, inc="pe" if kc == NC - 1 else None)
                    wv = ()
                v_u = p.op("vector", lambda v: v.scalar_tensor_tensor(
                    evb2[:], ps_a[:], -1.0, evb[:], ALU.mult, ALU.add),
                    wait=[("pe", v3), ("dve", v_e2)], inc="dve")
                p.op("vector", lambda v, jc=jc: v.tensor_copy(UTI[jc][:], evb2[:]),
                     wait=[("dve", v_u)], inc="dve")
            for dci in range(4 if "ms1" not in DBG else 0):
                dc = 4 * h + dci
                for og in range(2):
                    osl = slice(512 * og, 512 * (og + 1))
                    wv = [("dve", p.val("dve"))]
                    v1 = None
                    for jc in range(NC):
                        v1 = mm(ps_a[:], UTR[jc][:, 128 * dci:128 * (dci + 1)],
                                cosr[jc][:, osl], jc == 0, jc == NC - 1, wait=wv,
                                inc="pe" if jc == NC - 1 else None)
                        wv = ()
                    v_e = p.op("vector", lambda v: v.tensor_scalar(
                        evb[:], ps_a[:], 1.0 / D, None, ALU.mult),
                        wait=[("pe", v1)], inc="dve")
                    v2 = None
                    for jc in range(NC):
                        v2 = mm(ps_b[:], UTI[jc][:, 128 * dci:128 * (dci + 1)],
                                sinr[jc][:, osl], jc == 0, jc == NC - 1,
                                inc="pe" if jc == NC - 1 else None)
                    v_m0 = p.op("vector", lambda v: v.scalar_tensor_tensor(
                        evb2[:], ps_b[:], -1.0 / D, evb[:], ALU.mult, ALU.add),
                        wait=[("pe", v2), ("dve", v_e)], inc="dve")
                    v_m = p.op("vector", lambda v: v.tensor_copy(mst[:], evb2[:]),
                               wait=[("dve", v_m0)], inc="dve")
                    v_st = p.dma("sync", lambda e, dc=dc, osl=osl: e.dma_start(
                        out=st["st_M"][dc][:, osl], in_=mst[:]), wait=[("dve", v_m)])
                    p.wait("vector", [v_st])
        if "ms1" not in DBG:
            mld3 = wview(0, [128, D], F32R, "csmld")
            v_cs = None
            for dc in range(NC):
                v_l = p.dma("sync", lambda e, dc=dc: e.dma_start(
                    out=mld3[:], in_=st["st_M"][dc]),
                    wait=[("pe", v_cs)] if v_cs else [("pe", p.val("pe"))])
                for og in range(2):
                    osl = slice(512 * og, 512 * (og + 1))
                    v_cs = mm(ps_row[0:1, osl], sb["onecolr"][:], mld3[:, osl],
                              dc == 0, dc == NC - 1, wait=[v_l],
                              inc="pe" if og == 1 else None)
            p.wait("vector", [("pe", p.val("pe"))])
            v_n = p.op("vector", lambda v: v.tensor_scalar(
                rowbf[0:1, :], ps_row[0:1, :], -1.0 / 128.0, None, ALU.mult), inc="dve")
            p.wait("vector", [("dve", v_n)])
            v_fin = p.dma("sync", lambda e: e.dma_start(
                out=st["st_rows"][:, RO["colsum_M"]:RO["colsum_M"] + D],
                in_=rowbf[0:1, :]), wait=[("dve", v_n)])
            p.wait("gpsimd", [v_fin])
            if "dump" in DBG:
                p.dma("sync", lambda e: e.dma_start(out=dbg["csprep"],
                                                    in_=rowbf[0:1, :]))
        p.wait("sync", p.dmark())
        p.wait("gpsimd", p.dmark())
        p.barrier()
        pss.close()

    AO_RREP = 57344

    def ln_stats(s, attn_a_mode, rrep_flag=None):
        if rrep_flag is None:
            rrep_flag = not attn_a_mode
        """Compute col stats (colst) + bf16 rows (sb.rows) + RREP for sublayer."""
        pss = PsumScope()
        ps_s1 = pss.get([1, S])
        ps_s2 = pss.get([1, S])
        xbf = [aview(0, [128, 512], BF16, "stx0"), aview(1024, [128, 512], BF16, "stx1")]
        x2 = [aview(2048, [128, 512], BF16, "stq0"), aview(3072, [128, 512], BF16, "stq1")]
        rowsc = aview(4096, [1, S], F32, "strow")
        rowsc2 = aview(12288, [1, S], F32, "strow2")
        mcol = aview(20480, [128, 16], F32, "mcol")
        e2col = aview(20544, [128, 16], F32, "e2col")
        tcol = aview(20608, [128, 16], F32, "tcol")
        t2col = aview(20672, [128, 16], F32, "t2col")
        tbf = aview(20736, [128, 16], BF16, "tbfcol")
        mR2col = aview(20800, [128, 16], F32, "mR2col")
        invR2col = aview(20864, [128, 16], F32, "invR2col")
        mRcol = aview(20928, [128, 16], F32, "mRcol")
        rlcol = aview(20992, [128, 16], F32, "rlcol")
        i = 0
        prev_pe = {}
        for tg in range(4):
            tsl = slice(512 * tg, 512 * (tg + 1))
            for c in range(NC):
                bi = i % 2
                wv = []
                if i >= 2 and (i - 2) in prev_pe:
                    wv.append(("pe", prev_pe[i - 2]))
                v_cast = p.op("vector", lambda v, c=c, tsl=tsl, bi=bi: v.tensor_copy(
                    xbf[bi][:], X[c][:, tsl]), wait=wv, inc="dve")
                v_sq = p.op("gpsimd", lambda g, bi=bi: g.tensor_mul(
                    x2[bi][:], xbf[bi][:], xbf[bi][:]), wait=[("dve", v_cast)],
                    inc="pool")
                mm(ps_s1[0:1, tsl], sb["onecolbf"][:], xbf[bi][:], c == 0, c == NC - 1,
                   wait=[("dve", v_cast)])
                v_pe = mm(ps_s2[0:1, tsl], sb["onecolbf"][:], x2[bi][:],
                          c == 0, c == NC - 1, wait=[("pool", v_sq)], inc="pe")
                prev_pe[i] = v_pe
                i += 1
        v_done = p.val("pe")
        v_r1 = p.op("vector", lambda v: v.tensor_copy(rowsc[0:1, :], ps_s1[0:1, :]),
                    wait=[("pe", v_done)], inc="dve")
        v_r2 = p.op("vector", lambda v: v.tensor_copy(rowsc2[0:1, :], ps_s2[0:1, :]),
                    inc="dve")
        pss.close()
        pss = PsumScope()
        colps = pss.get([128, 16])
        colps2 = pss.get([128, 16])
        rowps = pss.get([1, S])
        v_t1 = None
        for t in range(NT):
            v_t1 = tr(colps[:, t:t + 1], rowsc[0:1, 128 * t:128 * (t + 1)],
                      wait=[("dve", v_r1)] if t == 0 else (),
                      inc="pe" if t == NT - 1 else None)
        v_t2 = None
        for t in range(NT):
            v_t2 = tr(colps2[:, t:t + 1], rowsc2[0:1, 128 * t:128 * (t + 1)],
                      wait=[("dve", v_r2)] if t == 0 else (),
                      inc="pe" if t == NT - 1 else None)
        p.op("vector", lambda v: v.tensor_scalar(mcol[:], colps[:], 1.0 / D, None,
                                                 ALU.mult), wait=[("pe", v_t1)])
        v_c2 = p.op("vector", lambda v: v.tensor_scalar(e2col[:], colps2[:], 1.0 / D,
                                                        None, ALU.mult),
                    wait=[("pe", v_t2)], inc="dve")
        p.op("vector", lambda v: v.tensor_copy(sb["colst"][:, CO["tmp"]:CO["tmp"] + 16],
                                               mcol[:]))
        if "dump" in DBG:
            chm = aview(21184, [128, 16], F32, "chainm")
            che = aview(21248, [128, 16], F32, "chaine")
            p.op("vector", lambda v: v.tensor_copy(chm[:], mcol[:]))
            p.op("vector", lambda v: v.tensor_copy(che[:], e2col[:]))
        def vchain(fn):
            vv = p.op("vector", fn, inc="dve")
            p.wait("vector", [("dve", vv)])
            return vv
        vchain(lambda v: v.tensor_mul(tcol[:], mcol[:], mcol[:]))
        v_var = vchain(lambda v: v.tensor_sub(t2col[:], e2col[:], tcol[:]))
        v_ln = p.op("scalar", lambda s_: s_.activation(tcol[:], t2col[:], AF.Ln,
                                                       bias=sb["epscol"][:]),
                    wait=[("dve", v_var)], inc="act")
        p.wait("scalar", [("act", v_ln)])
        if "dump" in DBG:
            lndump = aview(21056, [128, 16], F32, "lndump")
            vardump = aview(21120, [128, 16], F32, "vardump")
            p.op("vector", lambda v: v.tensor_copy(vardump[:], t2col[:]),
                 wait=[("dve", v_var)])
            p.op("vector", lambda v: v.tensor_copy(lndump[:], tcol[:]),
                 wait=[("act", v_ln)])
        v_r = p.op("scalar", lambda s_: s_.activation(sb["colst"][:, CO["r"]:CO["r"] + 16],
                                                      tcol[:], AF.Exp, scale=-0.5),
                   inc="act")
        p.wait("vector", [("act", v_r)])
        rcol_ap = sb["colst"][:, CO["r"]:CO["r"] + 16]
        R2_ap = sb["colst"][:, CO["R2"]:CO["R2"] + 16]
        v_mR = vchain(lambda v: v.tensor_mul(mRcol[:], mcol[:], rcol_ap))
        kinds = [("m", mcol), ("mR", mRcol)]
        if attn_a_mode:
            vchain(lambda v: v.tensor_mul(tcol[:], rcol_ap, rcol_ap))
            v_q = vchain(lambda v: v.tensor_scalar(
                tcol[:], tcol[:], -EPS, 1.0 + EPS, ALU.mult, ALU.add))
            v_l2 = p.op("scalar", lambda s_: s_.activation(t2col[:], tcol[:], AF.Ln),
                        wait=[("dve", v_q)], inc="act")
            p.wait("scalar", [("act", v_l2)])
            v_r2 = p.op("scalar", lambda s_: s_.activation(tcol[:], t2col[:], AF.Exp,
                                                           scale=-0.5), inc="act")
            p.wait("vector", [("act", v_r2)])
            vchain(lambda v: v.tensor_mul(R2_ap, rcol_ap, tcol[:]))
            v_ir = vchain(lambda v: v.reciprocal(invR2col[:], R2_ap))
            kinds += [("invR2", invR2col)]
        else:
            vchain(lambda v: v.tensor_copy(tbf[:], rcol_ap))
            vchain(lambda v: v.tensor_copy(t2col[:], tbf[:]))
            v_rl = vchain(lambda v: v.tensor_sub(rlcol[:], rcol_ap, t2col[:]))
            kinds += [("Rh", t2col), ("Rl", rlcol)]
        v_pre = p.val("dve")
        for nm, colap in kinds:
            v_tt = None
            for t in range(NT):
                v_tt = tr(rowps[0:1, 128 * t:128 * (t + 1)], colap[:, t:t + 1],
                          wait=[("dve", v_pre)] if t == 0 else (),
                          inc="pe" if t == NT - 1 else None)
            v_ev = p.op("vector", lambda v, nm=nm: v.tensor_copy(
                sb["rows"][:, RB[nm]:RB[nm] + S], rowps[0:1, :]),
                wait=[("pe", v_tt)], inc="dve")
            p.wait("tensor", [("dve", v_ev)])
            if nm == "m":
                p.dma("sync", lambda e: e.dma_start(
                    out=st["st_mrow"], in_=sb["rows"][:, RB["m"]:RB["m"] + S]),
                    wait=[("dve", v_ev)])
                if "skipst64" in DBG:
                    continue
                sc64 = aview(21504, [1, S], BF16, "sc64row")
                v_s64 = p.vop(lambda v: v.tensor_scalar(
                    sc64[:], sb["rows"][:, RB["m"]:RB["m"] + S], 1.0 / 64.0, None,
                    ALU.mult))
                v_d64 = p.dma("sync", lambda e: e.dma_start(out=st["st_mrow64"],
                                                             in_=sc64[:]),
                              wait=[("dve", v_s64)])
            if nm == "invR2" and "skipst64" not in DBG:
                sc64b = aview(21504, [1, S], BF16, "sc64rowb")
                v_s64b = p.op("vector", lambda v: v.tensor_scalar(
                    sc64b[:], sb["rows"][:, RB["invR2"]:RB["invR2"] + S], 1.0 / 64.0,
                    None, ALU.mult), wait=[v_d64])
                v_d64b = p.dma("sync", lambda e: e.dma_start(out=st["st_irow64"],
                                                              in_=sc64b[:]),
                               wait=[("dve", v_s64b)])
        if rrep_flag:
            rrep = aview(AO_RREP, [128, S], F32, "rrep")
            v_rows = p.val("dve")
            for tg in range(4):
                tsl = slice(512 * tg, 512 * (tg + 1))
                mm(colps[:, 0:16] if False else rowps[0:1, 0:1], None, None, True, True) \
                    if False else None
                mm_ps = colps
            pss.close()
            pss = PsumScope()
            reps = pss.get([128, 512])
            for tg in range(4):
                tsl = slice(512 * tg, 512 * (tg + 1))
                mm(reps[:], sb["onesbf"][0:1, :], sb["rows"][:, RB["Rh"] + 512 * tg:
                                                            RB["Rh"] + 512 * (tg + 1)],
                   True, False, wait=[("dve", v_rows)] if tg == 0 else
                   [("dve", p.val("dve"))])
                v_mm = mm(reps[:], sb["onesbf"][0:1, :],
                          sb["rows"][:, RB["Rl"] + 512 * tg:RB["Rl"] + 512 * (tg + 1)],
                          False, True, inc="pe")
                p.op("vector", lambda v, tsl=tsl: v.tensor_copy(rrep[:, tsl], reps[:]),
                     wait=[("pe", v_mm)], inc="dve")
        p.wait("sync", p.dmark())
        p.wait("gpsimd", p.dmark())
        p.barrier()
        pss.close()

    def ffn_sub(s):
        ln_stats(s, attn_a_mode=False)
        if "dump" in DBG and s == 0:
            mcolv = aview(20480, [128, 16], F32, "mcoldump")
            e2colv = aview(20544, [128, 16], F32, "e2coldump")
            p.op("vector", lambda v: v.tensor_copy(sb["colst"][:, 32:48], mcolv[:]))
            p.op("vector", lambda v: v.tensor_copy(sb["colst"][:, 48:64], e2colv[:]))
            lndumpv = aview(21056, [128, 16], F32, "lndump2")
            vardumpv = aview(21120, [128, 16], F32, "vardump2")
            p.op("vector", lambda v: v.tensor_copy(sb["colst"][:, 32:48], lndumpv[:]))
            p.op("vector", lambda v: v.tensor_copy(sb["colst"][:, 48:64], vardumpv[:]), inc="dve")
            chainv = aview(21184, [128, 128], F32, "chaindump")
            p.dma("sync", lambda e: e.dma_start(out=dbg["chain"], in_=chainv[:, 0:64]))
            rrepv = aview(AO_RREP, [128, S], F32, "rrepdump")
            p.dma("sync", lambda e: e.dma_start(out=dbg["rows"], in_=sb["rows"][:]))
            p.dma("sync", lambda e: e.dma_start(out=dbg["rrep"], in_=rrepv[:]))
            p.dma("sync", lambda e: e.dma_start(out=dbg["colst"], in_=sb["colst"][:]))
            p.barrier()
        if "statonly" in DBG:
            return
        pss = PsumScope()
        po = [pss.get([128, 512]) for _ in range(2)]
        ps_rep = pss.get([128, 512])
        rrep = aview(AO_RREP, [128, S], F32, "rrepf")
        xr = [aview(c * 2048, [128, 512], F32R, "xr") for c in range(NC)]
        tbuf = [aview(16384, [128, 512], F32, "ft0"), aview(18432, [128, 512], F32, "ft1")]
        biascol = aview(20480, [128, 8], F32, "fbias")
        csMbf = aview(20544, [1, D], BF16, "csMbf")
        mrepr = aview(22592, [128, S], F32R, "mrepr")
        csMrep = aview(30784, [128, D], F32R, "csMrep")
        Mw = [wview(c * 4096, [128, D], F32R, "Mw") for c in range(NC)]
        v_bias = p.dma("gpsimd", lambda e: e.dma_start(
            out=biascol[:], in_=win["ffn_bias"].rearrange("(c p) -> p c", p=128),
            allow_slow_non_contiguous=True))
        v_csM = p.dma("sync", lambda e: e.dma_start(
            out=csMbf[:], in_=st["st_rows"][:, RO["colsum_M"]:RO["colsum_M"] + D]))
        vw = []
        for c in range(NC):
            vw.append(p.dma("sync", lambda e, c=c: e.dma_start(out=Mw[c][:],
                                                               in_=st["st_M"][c])))
        v_rows = p.op("vector", lambda v: v.engine_nop(), wait=[v_csM], inc="dve")
        v_rep1 = p.dma("gpsimd", lambda e: e.dma_start(
            out=csMrep[:], in_=st["st_rows"][0:1, RO["colsum_M"]:RO["colsum_M"] + D]
            .to_broadcast([128, D])))
        v_rep2 = p.dma("gpsimd", lambda e: e.dma_start(
            out=mrepr[:], in_=st["st_mrow"][0:1, :].to_broadcast([128, S])))
        p.wait("tensor", [v_rep1, v_rep2])
        if "dumpr1" in DBG:
            r1ev = aview(34880, [128, 512], F32, "r1ev")
            opev = aview(36928, [128, 512], F32, "opev")
            v_o1 = p.op("vector", lambda v: v.tensor_copy(opev[:, 0:256], mrepr[:, 0:256]),
                        wait=[v_rep2, v_rep1])
            v_o2 = p.op("vector", lambda v: v.tensor_copy(opev[:, 256:512], csMrep[:, 0:256]),
                        inc="dve")
            p.dma("sync", lambda e: e.dma_start(out=dbg["ops"], in_=opev[:]),
                  wait=[("dve", v_o2)])
            v_x = mm(ps_rep[:], csMrep[:, 0:128], mrepr[:, 0:512], True, True,
                     wait=[v_rep2, v_rep1], inc="pe")
            v_xe = p.op("vector", lambda v: v.tensor_copy(r1ev[:], ps_rep[:]),
                        wait=[("pe", v_x)], inc="dve")
            p.dma("sync", lambda e: e.dma_start(out=dbg["r1"], in_=r1ev[:]),
                  wait=[("dve", v_xe)])
        ep_done = {}
        for tg in range(4):
            tsl = slice(512 * tg, 512 * (tg + 1))
            wv = [("pe", p.val("pe"))] if tg > 0 else []
            v_cast = None
            for c in range(NC):
                v_cast = p.op("vector", lambda v, c=c, tsl=tsl: v.tensor_copy(
                    xr[c][:], X[c][:, tsl]), wait=wv, inc="dve")
                wv = []
            for oc in range(NC):
                pk = po[oc % 2]
                idx = tg * NC + oc
                wv = [("dve", v_cast)] + vw
                if idx >= 2:
                    wv.append(("dve", ep_done[idx - 2]))
                skipr1 = "nor1" in DBG
                for c in range(NC):
                    v_mm = mm(pk[:], Mw[c][:, 128 * oc:128 * (oc + 1)], xr[c][:],
                              c == 0, skipr1 and c == NC - 1, wait=wv if c == 0 else (),
                              inc="pe" if (skipr1 and c == NC - 1) else None)
                if not skipr1:
                    v_mm = mm(pk[:], csMrep[:, 128 * oc:128 * (oc + 1)], mrepr[:, tsl],
                              False, True, inc="pe")
                tb = tbuf[oc % 2]
                p.op("vector", lambda v, pk=pk, tb=tb, tsl=tsl: v.tensor_mul(
                    tb[:], pk[:], rrep[:, tsl]), wait=[("pe", v_mm), ("dve", v_rows)])
                v_ep = p.op("vector", lambda v, oc=oc, tb=tb, tsl=tsl:
                            v.scalar_tensor_tensor(X[oc][:, tsl], tb[:],
                                                   biascol[:, oc:oc + 1],
                                                   X[oc][:, tsl], ALU.add, ALU.add),
                            wait=[v_bias], inc="dve")
                ep_done[idx] = v_ep
        p.barrier()
        pss.close()

    def prep_atta():
        pss = PsumScope()
        ps_rep = pss.get([128, 512])
        A = AADDR
        lnw_row = aview(0, [2, 64], F32, "plnw")
        lnb_row = aview(512, [2, 64], F32, "plnb")
        lnw_bf = aview(1024, [2, 64], BF16, "plnwb")
        lnb_bf = aview(1280, [2, 64], BF16, "plnbb")
        scr = aview(1536, [2, 64], F32, "pscr")
        scr2 = aview(2048, [2, 64], F32, "pscr2")
        rhb_row = aview(2560, [2, 1], F32, "prhb")
        rhb_bf = aview(2816, [2, 1], BF16, "prhbb")
        cv_bf = aview(3072, [2, 1], BF16, "pcv")
        p.dma("sync", lambda e: e.dma_start(out=lnw_row[0:1, :], in_=win["aa_ln_w"].unsqueeze(0)))
        p.dma("sync", lambda e: e.dma_start(out=lnb_row[0:1, :], in_=win["aa_ln_b"].unsqueeze(0)))
        v_ln = p.dma("sync", lambda e: e.dma_start(out=rhb_row[0:1, :], in_=win["aa_rh_b"].unsqueeze(0)))
        p.wait("vector", p.dmark())
        p.vop(lambda v: v.tensor_copy(lnw_bf[0:1, :], lnw_row[0:1, :]))
        p.vop(lambda v: v.tensor_copy(lnb_bf[0:1, :], lnb_row[0:1, :]))
        p.vop(lambda v: v.tensor_copy(rhb_bf[0:1, :], rhb_row[0:1, :]))
        v1 = mm(ps_rep[:, 0:64], sb["onesbf"][0:1, :], lnw_bf[0:1, :], True, True,
                wait=[("dve", p.val("dve"))])
        v2 = mm(ps_rep[:, 64:128], sb["onesbf"][0:1, :], lnb_bf[0:1, :], True, True)
        v3 = mm(ps_rep[:, 128:129], sb["onesbf"][0:1, :], rhb_bf[0:1, :], True, True,
                inc="pe")
        p.wait("vector", [("pe", v3)])
        p.vop(lambda v: v.tensor_copy(sb["wrep"][:], ps_rep[:, 0:64]))
        p.vop(lambda v: v.tensor_copy(sb["brep"][:], ps_rep[:, 64:128]))
        p.vop(lambda v: v.tensor_copy(sb["rhbrep"][:], ps_rep[:, 128:129]))
        v1b = mm(ps_rep[:, 192:256], sb["onesbf"][0:1, :], lnw_bf[0:1, :], True, True)
        p.vop(lambda v: v.reduce_max(scr[0:1, 0:1].unsqueeze(2),
                                     lnw_row[0:1, :].unsqueeze(1), AX.X,
                                     apply_absolute_value=True))
        p.vop(lambda v: v.tensor_mul(scr2[0:1, :], lnb_row[0:1, :], lnb_row[0:1, :]))
        p.vop(lambda v: v.reduce_sum(scr[0:1, 1:2].unsqueeze(2),
                                     scr2[0:1, :].unsqueeze(1), AX.X))
        v_sq = p.op("scalar", lambda s_: s_.activation(scr[0:1, 1:2], scr[0:1, 1:2],
                                                       AF.Sqrt),
                    wait=[("dve", p.val("dve"))], inc="act")
        p.wait("vector", [("act", v_sq)])
        p.vop(lambda v: v.tensor_scalar(scr[0:1, 2:3], scr[0:1, 0:1], 8.0, None, ALU.mult))
        p.vop(lambda v: v.tensor_add(scr[0:1, 2:3], scr[0:1, 2:3], scr[0:1, 1:2]))
        p.vop(lambda v: v.tensor_mul(scr[0:1, 3:4], scr[0:1, 2:3], scr[0:1, 2:3]))
        p.vop(lambda v: v.tensor_scalar(scr[0:1, 3:4], scr[0:1, 3:4], -0.125, None, ALU.mult))
        p.vop(lambda v: v.tensor_copy(cv_bf[0:1, :], scr[0:1, 3:4]))
        v4 = mm(ps_rep[:, 129:130], sb["onesbf"][0:1, :], cv_bf[0:1, :], True, True,
                wait=[("dve", p.val("dve"))], inc="pe")
        p.wait("vector", [("pe", v4)])
        p.vop(lambda v: v.tensor_copy(sb["cbias"][:], ps_rep[:, 129:130]))
        qnw_c = aview(3584, [128, 8], F32, "pqnw")
        qnb_c = aview(3616, [128, 8], F32, "pqnb")
        kvnw_c = aview(3648, [128, 8], F32, "pkvnw")
        kvnb_c = aview(3680, [128, 8], F32, "pkvnb")
        qnb_bf = aview(3712, [128, 8], BF16, "pqnbb")
        kvnb_bf = aview(3744, [128, 8], BF16, "pkvnbb")
        for nm, dst in [("aa_qn_w", qnw_c), ("aa_qn_b", qnb_c),
                        ("aa_kvn_w", kvnw_c), ("aa_kvn_b", kvnb_c)]:
            p.dma("gpsimd", lambda e, nm=nm, dst=dst: e.dma_start(
                out=dst[:], in_=win[nm].rearrange("(c p) -> p c", p=128),
                allow_slow_non_contiguous=True))
        p.wait("vector", p.dmark())
        p.vop(lambda v: v.tensor_copy(qnb_bf[:], qnb_c[:]))
        p.vop(lambda v: v.tensor_copy(kvnb_bf[:], kvnb_c[:]))
        pss.close()

        pss = PsumScope()
        ps_bias = pss.get([1, 2 * D])
        ps_cs = pss.get([1, 2 * D])
        wbuf = wview(0, [128, 2048], F32, "pawbuf")
        wbf = wview(8192, [128, 2080], BF16, "pawbf")
        rowsc = wview(16384 if False else 12352, [2, 2048], F32, "parow")
        rowbf2 = wview(20544, [1, 4160], BF16, "parowbf")

        def fold(which):
            wname = "aa_q_w" if which == "q" else "aa_kv_w"
            wide = D if which == "q" else 2 * D
            scol = qnw_c if which == "q" else kvnw_c
            bcol = qnb_bf if which == "q" else kvnb_bf
            stname = "st_qw" if which == "q" else "st_kv"
            v_mmlast = None
            for c in range(NC):
                wv = [("pe", v_mmlast)] if v_mmlast else [("pe", p.val("pe"))]
                wv += [("dve", p.val("dve"))]
                v_l = p.dma("sync", lambda e, c=c, wname=wname: e.dma_start(
                    out=wbuf[:, 0:wide],
                    in_=win[wname].rearrange("(c p) o -> c p o", p=128)[c]), wait=wv)
                ltoks = [v_l]
                if which == "q":
                    ltoks.append(p.dma("sync", lambda e, c=c: e.dma_start(
                        out=wbuf[:, D:D + 1],
                        in_=win["aa_rh_w"].rearrange("(c p) o -> c p o", p=128)[c])))
                p.wait("vector", ltoks)
                v_raw = p.vop(lambda v: v.tensor_copy(wbf[:, 0:wide], wbuf[:, 0:wide]))
                ngg = wide // 512
                v_bias = None
                for gg in range(ngg):
                    v_bias = mm(ps_bias[0:1, 512 * gg:512 * (gg + 1)], bcol[:, c:c + 1],
                                wbf[:, 512 * gg:512 * (gg + 1)], c == 0, c == NC - 1,
                                wait=[("dve", v_raw)] if gg == 0 else (),
                                inc="pe" if gg == ngg - 1 else None)
                p.wait("vector", [("pe", v_bias)])
                v_sc = p.vop(lambda v, c=c, scol=scol: v.tensor_scalar(
                    wbf[:, 0:wide], wbuf[:, 0:wide], scol[:, c:c + 1], None, ALU.mult))
                if which == "q":
                    v_sc = p.vop(lambda v: v.tensor_copy(wbf[:, D:D + 1],
                                                         wbuf[:, D:D + 1]))
                    if c == 0:
                        v_sc = p.vop(lambda v: v.memset(wbf[:, D + 1:D + 16], 0.0))
                stwide = 1040 if which == "q" else 2048
                v_st = p.dma("sync", lambda e, c=c, stname=stname, stwide=stwide:
                             e.dma_start(out=st[stname][c][:, 0:stwide],
                                         in_=wbf[:, 0:stwide]), wait=[("dve", v_sc)])
                v_mmlast = v_bias
                p.wait("sync", [v_st])
            stw0 = 1040 if which == "q" else 2048
            v_cl = None
            for c in range(NC):
                wv = [("pe", v_cl)] if v_cl else [("pe", p.val("pe"))]
                v_l = p.dma("sync", lambda e, c=c, stname=stname, stw0=stw0: e.dma_start(
                    out=wbf[:, 0:stw0], in_=st[stname][c][:, 0:stw0]), wait=wv)
                nfull = stw0 // 512
                w_ = stw0 - nfull * 512
                for gg in range(nfull):
                    v_cl = mm(ps_cs[0:1, 512 * gg:512 * (gg + 1)], sb["onecolbf"][:],
                              wbf[:, 512 * gg:512 * (gg + 1)], c == 0, c == NC - 1,
                              wait=[v_l] if gg == 0 else (),
                              inc="pe" if (gg == nfull - 1 and not w_) else None)
                if w_:
                    v_cl = mm(ps_cs[0:1, stw0 - w_:stw0], sb["onecolbf"][:],
                              wbf[:, stw0 - w_:stw0], c == 0, c == NC - 1, inc="pe")
            p.wait("vector", [("pe", p.val("pe"))])
            bname = "aa_q_b" if which == "q" else "aa_kv_b"
            v_bl = p.dma("sync", lambda e, bname=bname, wide=wide: e.dma_start(
                out=rowsc[0:1, 0:wide], in_=win[bname].unsqueeze(0)))
            p.wait("vector", [v_bl, ("pe", p.val("pe"))])
            p.vop(lambda v, wide=wide: v.tensor_add(rowsc[0:1, 0:wide],
                                                    rowsc[0:1, 0:wide],
                                                    ps_bias[0:1, 0:wide]))
            stw = 1040 if which == "q" else 2048
            if which == "q":
                p.vop(lambda v: v.memset(rowsc[0:1, D:D + 16], 0.0))
            p.vop(lambda v, stw=stw: v.tensor_copy(rowbf2[0:1, 0:stw],
                                                   rowsc[0:1, 0:stw]))
            p.vop(lambda v, stw=stw: v.tensor_scalar(rowbf2[0:1, 2080:2080 + stw],
                                                     ps_cs[0:1, 0:stw], -1.0, None,
                                                     ALU.mult))
            roq = RO["qb"] if which == "q" else RO["kvb"]
            roc = RO["colsum_q"] if which == "q" else RO["colsum_kv"]
            v_s1 = p.dma("sync", lambda e, roq=roq, stw=stw: e.dma_start(
                out=st["st_rows"][:, roq:roq + stw], in_=rowbf2[0:1, 0:stw]),
                wait=[("dve", p.val("dve"))])
            v_s2 = p.dma("sync", lambda e, roc=roc, stw=stw: e.dma_start(
                out=st["st_rows"][:, roc:roc + stw], in_=rowbf2[0:1, 2080:2080 + stw]))
            p.wait("gpsimd", [v_s1, v_s2])

        fold("q")
        p.barrier()
        fold("kv")
        p.barrier()
        if "dumpst" in DBG:
            bb = wview(24576, [128, 2048], BF16, "dumpb")
            v_l = p.dma("sync", lambda e: e.dma_start(out=bb[:], in_=st["st_kv"][0]))
            t1 = p.dma("sync", lambda e: e.dma_start(out=dbg["stkv"], in_=bb[:]),
                       wait=[v_l])
            v_l2 = p.dma("sync", lambda e: e.dma_start(out=bb[:, 0:1040], in_=st["st_qw"][0]),
                         wait=[t1])
            p.dma("sync", lambda e: e.dma_start(out=dbg["stqw"], in_=bb[:, 0:1040]),
                  wait=[v_l2])
            p.dma("sync", lambda e: e.dma_start(out=dbg["strows"], in_=st["st_rows"]))
            p.barrier()
        v_mmlast = None
        for c in range(NC):
            wv = [("dve", p.val("dve"))]
            v_l = p.dma("sync", lambda e, c=c: e.dma_start(
                out=wbuf[:, 0:D],
                in_=win["aa_out_w"].rearrange("(c p) o -> c p o", p=128)[c]), wait=wv)
            p.wait("vector", [v_l])
            v_c = p.vop(lambda v: v.tensor_copy(wbf[:, 0:D], wbuf[:, 0:D]))
            v_st = p.dma("sync", lambda e, c=c: e.dma_start(out=st["st_outw"][c],
                                                            in_=wbf[:, 0:D]),
                         wait=[("dve", v_c)])
            p.wait("vector", [v_st])
        p.wait("sync", p.dmark())
        p.wait("gpsimd", p.dmark())
        p.barrier()
        pss.close()

    AO_TAB = 0
    AO_QNF = 12288
    AO_KNF = 20480
    AO_VBF = 28672
    AO_ABF = 36992
    AO_PBF = 45184
    AO_Q32 = 49280
    AO_K32 = 50304
    AO_SCR = 51328
    AO_ZRB = 55424
    AO_COLS = 59520
    AO_XBT = 59776
    AO_ROWQ = 60416

    def atta_sub(s):
        ln_stats(s, attn_a_mode=True)
        for nm, off in [("tab_c1", 0), ("tab_c2", 4096), ("tab_cs", 8192)]:
            tv = aview(off, [128, NT * HD], F32, nm + "v")
            p.dma("sync", lambda e, nm=nm, tv=tv: e.dma_start(
                out=tv[:].rearrange("p (t d) -> p t d", d=HD),
                in_=tabs[nm].rearrange("(t p) d -> p t d", p=128)))
        c1t = aview(0, [128, NT * HD], F32, "c1t")
        c2t = aview(4096, [128, NT * HD], F32, "c2t")
        cst = aview(8192, [128, NT * HD], F32, "cst")
        vtm = p.dmark()
        p.wait("vector", vtm)
        p.wait("gpsimd", vtm)

        for g in range(NG):
            atta_group(s, g)

    def atta_group(s, g):
        c1t = aview(0, [128, NT * HD], F32, "gc1")
        c2t = aview(4096, [128, NT * HD], F32, "gc2")
        cst = aview(8192, [128, NT * HD], F32, "gcs")
        qnf = [aview(AO_QNF + i * 4096, [128, S], BF16, "qnf") for i in range(2)]
        knf = [aview(AO_KNF + i * 4096, [128, S], BF16, "knf") for i in range(2)]
        vbuf = aview(AO_VBF, [128, NT * 260], BF16, "vbuf")
        abf = [aview(AO_ABF + i * 4096, [128, S], BF16, "abf") for i in range(2)]
        pbf = [aview(AO_PBF + i * 2048, [128, 1024], BF16, "pbf") for i in range(2)]
        q32 = aview(AO_Q32, [128, 256], F32, "q32")
        k32 = aview(AO_K32, [128, 256], F32, "k32")
        scr = [aview(AO_SCR + i * 1024, [128, 256], F32, "scrA") for i in range(4)]
        zrb = aview(AO_ZRB, [64, 1024], F32, "zrb")
        cols = [aview(AO_COLS + i * 32, [128, 4], F32, "colA") for i in range(12)]
        xbt = [aview(AO_XBT + i * 256, [128, 128], BF16, "xbt") for i in range(2)]
        zrow = aview(AO_ROWQ, [1, 1024], F32, "zrow")
        WS = (g % 2) * 24576
        zbf = wview(WS + 22528, [1, 1024], BF16, "zbf")
        wqkv = [wview(WS + c * 1568, [128, 784], BF16, "wqkv") for c in range(NC)]
        outwg = [wview(WS + 12800 + i * 2048, [128, 1024], BF16, "outwg")
                 for i in range(2)]
        mi64 = wview(49152, [128, S], BF16, "mi64")
        csqkv = wview(WS + 16896, [128, 784], BF16, "csqkv")
        outbc = wview(WS + 21504, [128, 8], F32, "outbc")

        for c in range(NC):
            p.dma("sync", lambda e, c=c: e.dma_start(
                out=wqkv[c][:, 0:256], in_=st["st_qw"][c][:, 256 * g:256 * (g + 1)]))
            p.dma("sync", lambda e, c=c: e.dma_start(
                out=wqkv[c][:, 256:272], in_=st["st_qw"][c][:, 1024:1040]))
            p.dma("sync", lambda e, c=c: e.dma_start(
                out=wqkv[c][:, 272:528], in_=st["st_kv"][c][:, 256 * g:256 * (g + 1)]))
            p.dma("sync", lambda e, c=c: e.dma_start(
                out=wqkv[c][:, 528:784],
                in_=st["st_kv"][c][:, 1024 + 256 * g:1024 + 256 * (g + 1)]))
        for i in range(2):
            p.dma("sync", lambda e, i=i: e.dma_start(
                out=outwg[i][:], in_=st["st_outw"][2 * g + i]))
        if g == 0 and "skipl" not in DBG:
            p.dma("gpsimd", lambda e: e.dma_start(
                out=mi64[0:64, :], in_=st["st_mrow64"][0:1, :].to_broadcast([64, S])))
            p.dma("gpsimd", lambda e: e.dma_start(
                out=mi64[64:128, :], in_=st["st_irow64"][0:1, :].to_broadcast([64, S])))
        for dst, ro, off, n in ([
            (csqkv[0:64, 0:256], "colsum_q", 256 * g, 256),
            (csqkv[64:128, 0:256], "qb", 256 * g, 256),
            (csqkv[0:64, 256:272], "colsum_q", 1024, 16),
            (csqkv[64:128, 256:272], "qb", 1024, 16),
            (csqkv[0:64, 272:528], "colsum_kv", 256 * g, 256),
            (csqkv[64:128, 272:528], "kvb", 256 * g, 256),
            (csqkv[0:64, 528:784], "colsum_kv", 1024 + 256 * g, 256),
            (csqkv[64:128, 528:784], "kvb", 1024 + 256 * g, 256),
        ]):
            p.dma("gpsimd", lambda e, dst=dst, ro=ro, off=off, n=n: e.dma_start(
                out=dst, in_=st["st_rows"][0:1, RO[ro] + off:RO[ro] + off + n]
                .to_broadcast([64, n])))
        if g == 0 and "skipl" not in DBG:
            p.dma("gpsimd", lambda e: e.dma_start(
                out=outbc[:], in_=win["aa_out_b"].rearrange("(c p) -> p c", p=128),
                allow_slow_non_contiguous=True))
        gml = p.dmark() + [("dve", p.val("dve"))]
        p.wait("tensor", gml)
        p.wait("vector", gml)

        pss = PsumScope()
        ps_p2 = [pss.get([128, 1024]) for _ in range(2)]
        ps_tr = [pss.get([128, 128], BF16) for _ in range(2)]

        p.vop(lambda v: v.memset(
            vbuf[:].rearrange("p (t h c) -> p t h c", h=4, c=65)[:, :, :, 64:65], 1.0))

        rcol_all = sb["colst"][:, CO["r"]:CO["r"] + 16]
        R2_all = sb["colst"][:, CO["R2"]:CO["R2"] + 16]
        rad_all = sb["colst"][:, CO["rad"]:CO["rad"] + 16]

        xsl = [aview(AO_ABF + sl * 2048, [128, 1024], BF16, "xslab")
               for sl in range(2)]

        def xb_load(t, wv):
            return p.dma("sync", lambda e, t=t: e.dma_start(
                out=xsl[t % 2][:].rearrange("p (c u) -> p c u", u=128),
                in_=st["st_xbf"][:, :, 128 * t:128 * (t + 1)]
                .rearrange("c p u -> p c u")), wait=wv)

        xtok = {0: xb_load(0, ()), 1: xb_load(1, ())}
        vev_done = {}
        tr_done = {}
        pe_done = {}
        for t in range(NT):
            bi = t % 2
            psp = ps_p2[bi]
            ps_q = psp[:, 0:512]
            ps_k = psp[:, 272:784]
            ps_v = psp[:, 528:784]
            tsl = slice(128 * t, 128 * (t + 1))
            lsl = mi64[:, 128 * t:128 * (t + 1)]
            wv0 = [xtok[t]]
            if t - 2 in vev_done:
                wv0.append(("dve", vev_done[t - 2]))
            for c in range(NC):
                mm(psp[:, 0:512], xsl[bi][:, 128 * c:128 * (c + 1)],
                   wqkv[c][:, 0:512], c == 0, False, wait=wv0 if c == 0 else ())
                mm(psp[:, 512:784], xsl[bi][:, 128 * c:128 * (c + 1)],
                   wqkv[c][:, 512:784], c == 0, False)
            mm(psp[:, 0:512], lsl, csqkv[:, 0:512], False, True)
            v_proj = mm(psp[:, 512:784], lsl, csqkv[:, 512:784], False, True,
                        inc="pe")
            pe_done[t] = v_proj
            if t + 2 < NT:
                xtok[t + 2] = xb_load(t + 2, [("pe", v_proj)])
            p.wait("vector", [("pe", v_proj)])
            qk32 = aview(AO_Q32, [128, 512], F32, "qk32")
            p.vop(lambda v, t=t, psp=psp, qk32=qk32: v.tensor_scalar(
                qk32[:, 0:256], psp[:, 0:256], R2_all[:, t:t + 1], None, ALU.mult))
            p.vop(lambda v, t=t, psp=psp, qk32=qk32: v.tensor_scalar(
                qk32[:, 256:512], psp[:, 272:528], R2_all[:, t:t + 1], None,
                ALU.mult))
            if g == 0:
                p.vop(lambda v, t=t, psp=psp: v.tensor_scalar(
                    cols[8][:, 0:1], psp[:, 256:257], rcol_all[:, t:t + 1],
                    sb["rhbrep"][:], ALU.mult, ALU.add))
                p.vop(lambda v: v.tensor_scalar(cols[8][:, 0:1], cols[8][:, 0:1],
                                                0.01, 0.5, ALU.mult, ALU.max))
                p.vop(lambda v, t=t: v.tensor_scalar(
                    rad_all[:, t:t + 1], cols[8][:, 0:1], 2.0, None, ALU.min))
            v_vev = p.vop(lambda v, t=t, psp=psp: v.tensor_scalar(
                vbuf[:].rearrange("p (tt h c) -> p tt h c", h=4, c=65)[:, t, :, 0:64],
                psp[:, 528:784].rearrange("p (h d) -> p h d", d=64),
                R2_all[:, t:t + 1], None, ALU.mult))
            vev_done[t] = v_vev
            if t - 1 in tr_done:
                p.wait("vector", [("pe", tr_done[t - 1])])
            scr4 = [aview(AO_SCR + i_ * 2048, [128, 512], F32, "scr4")
                    for i_ in range(4)]
            cW = [aview(AO_COLS + i_ * 32, [128, 8], F32, "colW")
                  for i_ in range(12)]
            csv = cst[:].rearrange("p (tt d) -> p tt d", d=HD)[:, t, :] \
                .unsqueeze(1).to_broadcast([128, 8, HD])
            c1v = c1t[:].rearrange("p (tt d) -> p tt d", d=HD)[:, t, :] \
                .unsqueeze(1).to_broadcast([128, 8, HD])
            c2v = c2t[:].rearrange("p (tt d) -> p tt d", d=HD)[:, t, :] \
                .unsqueeze(1).to_broadcast([128, 8, HD])
            s3 = qk32[:].rearrange("p (h d) -> p h d", d=HD)
            p.vop(lambda v, s3=s3, csv=csv: v.tensor_tensor(
                scr4[0][:].rearrange("p (h d) -> p h d", d=HD), s3, csv, ALU.mult))
            p.vop(lambda v: v.reduce_sum(cW[0][:].unsqueeze(2),
                                         scr4[0][:].rearrange("p (h d) -> p h d",
                                                              d=HD), AX.X))
            p.vop(lambda v, s3=s3: v.tensor_tensor(
                scr4[1][:].rearrange("p (h d) -> p h d", d=HD), s3, s3, ALU.mult))
            p.vop(lambda v: v.reduce_sum(cW[1][:].unsqueeze(2),
                                         scr4[1][:].rearrange("p (h d) -> p h d",
                                                              d=HD), AX.X))
            p.vop(lambda v, t=t: v.tensor_mul(cW[2][:, 0:1], rad_all[:, t:t + 1],
                                              rad_all[:, t:t + 1]))
            p.vop(lambda v, t=t: v.tensor_scalar(
                cW[3][:], cW[0][:], rad_all[:, t:t + 1], 1.0 / HD,
                ALU.mult, ALU.mult))
            p.vop(lambda v: v.tensor_scalar(cW[4][:], cW[1][:], cW[2][:, 0:1],
                                            1.0 / HD, ALU.mult, ALU.mult))
            p.vop(lambda v: v.tensor_mul(cW[5][:], cW[3][:], cW[3][:]))
            p.vop(lambda v: v.tensor_sub(cW[5][:], cW[4][:], cW[5][:]))
            v_lnq = p.op("scalar", lambda s_: s_.activation(
                cW[6][:], cW[5][:], AF.Ln, bias=sb["epscol"][:]),
                wait=[("dve", p.val("dve"))], inc="act")
            v_eq = p.op("scalar", lambda s_: s_.activation(
                cW[6][:], cW[6][:], AF.Exp, scale=-0.5), inc="act")
            p.wait("vector", [("act", v_eq)])
            p.vop(lambda v, t=t: v.tensor_scalar(cW[7][:], cW[6][:],
                                                 rad_all[:, t:t + 1], None,
                                                 ALU.mult))
            p.vop(lambda v: v.tensor_mul(cW[9][:], cW[3][:], cW[6][:]))
            p.vop(lambda v, s3=s3, c1v=c1v: v.tensor_tensor(
                scr4[0][:].rearrange("p (h d) -> p h d", d=HD), s3, c1v, ALU.mult))
            p.vop(lambda v, s3=s3, c2v=c2v: v.tensor_tensor(
                scr4[1][:].rearrange("p (h d) -> p h d", d=HD), s3, c2v, ALU.mult))
            s0e = scr4[0][:].rearrange("p (h j two) -> p h j two", two=2, j=32)
            s1e = scr4[1][:].rearrange("p (h j two) -> p h j two", two=2, j=32)
            u0 = scr4[2][:].rearrange("p (h j two) -> p h j two", two=2, j=32)
            p.vop(lambda v, s0e=s0e, u0=u0: v.tensor_tensor(
                u0[:, :, :, 0], s0e[:, :, :, 0], s0e[:, :, :, 1], ALU.subtract))
            p.vop(lambda v, s1e=s1e, u0=u0: v.tensor_tensor(
                u0[:, :, :, 1], s1e[:, :, :, 0], s1e[:, :, :, 1], ALU.add))
            u3 = scr4[2][:].rearrange("p (h d) -> p h d", d=HD)
            g_b = cW[7][:].unsqueeze(2).to_broadcast([128, 8, HD])
            m2_b = cW[9][:].unsqueeze(2).to_broadcast([128, 8, HD])
            w_b = sb["wrep"][:].unsqueeze(1).to_broadcast([128, 8, HD])
            b_b = sb["brep"][:].unsqueeze(1).to_broadcast([128, 8, HD])
            p.vop(lambda v, u3=u3, g_b=g_b: v.tensor_tensor(
                scr4[3][:].rearrange("p (h d) -> p h d", d=HD), u3, g_b, ALU.mult))
            p.vop(lambda v, m2_b=m2_b: v.tensor_tensor(
                scr4[3][:].rearrange("p (h d) -> p h d", d=HD),
                scr4[3][:].rearrange("p (h d) -> p h d", d=HD), m2_b, ALU.subtract))
            p.vop(lambda v, w_b=w_b: v.tensor_tensor(
                scr4[3][:].rearrange("p (h d) -> p h d", d=HD),
                scr4[3][:].rearrange("p (h d) -> p h d", d=HD), w_b, ALU.mult))
            p.vop(lambda v, b_b=b_b: v.tensor_tensor(
                scr4[0][:].rearrange("p (h d) -> p h d", d=HD),
                scr4[3][:].rearrange("p (h d) -> p h d", d=HD), b_b, ALU.add))
            v_cast = p.vop(lambda v: v.tensor_copy(
                scr4[1][:].bitcast(BF16)[:, 0:512], scr4[0][:]))
            cp_done = {}
            for c2 in range(4):
                dstf = qnf if c2 < 2 else knf
                wvt = [("dve", v_cast)] if c2 < 2 else [("dve", cp_done[c2 - 2])]
                v_tr = tr(ps_tr[c2 % 2][:],
                          scr4[1][:].bitcast(BF16)[:, 128 * c2:128 * (c2 + 1)],
                          wait=wvt, inc="pe")
                tr_done[t] = v_tr
                cp_done[c2] = p.op(
                    "vector", lambda v, c2=c2, t=t, dstf=dstf, pt=ps_tr[c2 % 2]:
                    v.tensor_copy(dstf[c2 % 2][:, 128 * t:128 * (t + 1)], pt[:]),
                    wait=[("pe", v_tr)], inc="dve")
        p.wait("tensor", [("dve", p.val("dve"))])
        pss.close()
        if "dumpa" in DBG and g == 0:
            p.wait("sync", p.dmark() + [("dve", p.val("dve")), ("pe", p.val("pe")),
                                        ("act", p.val("act"))])
            p.dma("sync", lambda e: e.dma_start(out=dbg["stkv"][:, 0:2048],
                                                in_=mi64[:]))
            p.dma("sync", lambda e: e.dma_start(out=dbg["stqw"][:, 0:256],
                                                in_=csqkv[:, 272:528]))
            p.dma("sync", lambda e: e.dma_start(out=dbg["colst"], in_=sb["colst"][:]))
            p.dma("sync", lambda e: e.dma_start(out=dbg["qnf"], in_=qnf[0][:]))
            p.dma("sync", lambda e: e.dma_start(out=dbg["knf"], in_=knf[0][:]))
            p.dma("sync", lambda e: e.dma_start(out=dbg["vb"], in_=vbuf[:]))
            p.barrier()
        if "a1only" in DBG:
            p.barrier()
            return

        pss = PsumScope()
        ps_s4 = [[pss.get([128, 512]) for _ in range(2)] for _ in range(2)]
        ps_av = [pss.get([64, 512]) for _ in range(2)]
        ps_z = [pss.get([1, 512]) for _ in range(2)]
        v_prev_ab = 0
        for hp in range(2):
            for qh2 in range(4):
                qbase = 512 * qh2
                qsl = slice(qbase, qbase + 512)
                v_av = None
                v_z = None
                v_exp = {}
                for kt in range(NT):
                    par = kt % 2
                    ksl = slice(128 * kt, 128 * (kt + 1))
                    wv = [("dve", v_prev_ab)] if v_prev_ab else []
                    if kt >= 2:
                        wv = [("act", v_exp[kt - 2])]
                    for hh in range(2):
                        base = 64 * hh
                        mm(ps_s4[par][hh][:], knf[hp][base:base + 64, ksl],
                           qnf[hp][base:base + 64, qsl], True, True,
                           wait=wv if hh == 0 else ())
                    p.q["tensor"][-1] = (p.q["tensor"][-1][0],
                                         p.q["tensor"][-1][1], "pe", 1)
                    p.cnt["pe"] += 1
                    v_sc = p.cnt["pe"]
                    for hh in range(2):
                        v_e = p.op("scalar", lambda s_, hh=hh, par=par:
                                   s_.activation(pbf[par][:, 512 * hh:512 * (hh + 1)],
                                                 ps_s4[par][hh][:], AF.Exp,
                                                 bias=sb["cbias"][:], scale=0.125),
                                   wait=[("pe", v_sc)] if hh == 0 else (),
                                   inc="act")
                    v_exp[kt] = v_e
                    for hh in range(2):
                        h4 = 2 * hp + hh
                        vsl = vbuf[:].rearrange("p (tt h c) -> p tt h c", h=4, c=65)
                        mm(ps_av[hh][:], vsl[:, kt, h4, 0:64],
                           pbf[par][:, 512 * hh:512 * (hh + 1)],
                           kt == 0, kt == NT - 1,
                           wait=[("act", v_exp[kt])] if hh == 0 else ())
                        ivz = "pe" if (kt == NT - 1 and hh == 1) else None
                        v_z = mm(ps_z[hh][:], sb["onecolbf"][:],
                                 pbf[par][:, 512 * hh:512 * (hh + 1)],
                                 kt == 0, kt == NT - 1, inc=ivz)
                v_av = v_z
                p.wait("vector", [("pe", v_av)])
                for hh in range(2):
                    p.op("scalar", lambda s_, hh=hh: s_.activation(
                        zrow[0:1, 0:512], ps_z[hh][:], AF.Ln),
                        wait=[("pe", v_av), ("dve", p.val("dve"))])
                    v_re = p.op("scalar", lambda s_: s_.activation(
                        zrow[0:1, 0:512], zrow[0:1, 0:512], AF.Exp, scale=-1.0))
                    p.op("vector", lambda v: v.tensor_copy(zbf[0:1, 0:512],
                                                           zrow[0:1, 0:512]),
                         wait=[("act", v_re)])
                    v_z1 = mm(ps_s4[0][hh][0:64, :], sb["onesbf"][0:1, 0:64],
                              zbf[0:1, 0:512], True, True,
                              wait=[("dve", p.val("dve"))], inc="pe")
                    p.wait("vector", [("pe", v_z1)])
                    base = 64 * hh
                    v_zc = p.vop(lambda v, hh=hh: v.tensor_copy(
                        zrb[:, 0:512], ps_s4[0][hh][0:64, :]))
                    p.vop(lambda v, hh=hh, hp=hp, base=base, qsl=qsl:
                          v.tensor_tensor(abf[hp][base:base + 64, qsl],
                                          ps_av[hh][:], zrb[:, 0:512], ALU.mult))
                v_prev_ab = p.val("dve")
        pss.close()
        if "a12" in DBG:
            p.barrier()
            return

        pss = PsumScope()
        po = [pss.get([128, 512]) for _ in range(2)]
        ep_done = {}
        for oc in range(NC):
            for tg in range(4):
                tsl = slice(512 * tg, 512 * (tg + 1))
                idx = oc * 4 + tg
                pk = po[idx % 2]
                wv = [("dve", p.val("dve"))] if idx == 0 else ()
                if idx >= 2:
                    wv = [("dve", ep_done[idx - 2])]
                mm(pk[:], outwg[0][:, 128 * oc:128 * (oc + 1)], abf[0][:, tsl],
                   True, False, wait=wv)
                v_mm = mm(pk[:], outwg[1][:, 128 * oc:128 * (oc + 1)], abf[1][:, tsl],
                          False, True, inc="pe")
                if g == 0:
                    v_ep = p.op("vector", lambda v, oc=oc, tsl=tsl, pk=pk:
                                v.scalar_tensor_tensor(X[oc][:, tsl], pk[:],
                                                       outbc[:, oc:oc + 1],
                                                       X[oc][:, tsl], ALU.add, ALU.add),
                                wait=[("pe", v_mm)], inc="dve")
                else:
                    v_ep = p.op("vector", lambda v, oc=oc, tsl=tsl, pk=pk:
                                v.tensor_add(X[oc][:, tsl], X[oc][:, tsl], pk[:]),
                                wait=[("pe", v_mm)], inc="dve")
                ep_done[idx] = v_ep
        p.barrier()
        pss.close()


    def prep_attb():
        pss = PsumScope()
        ps_cs = pss.get([1, 2048])
        wbuf = wview(0, [128, 2048], F32, "pawbuf")
        wbf = wview(8192, [128, 2080], BF16, "pawbf")
        rowbf2 = wview(20544, [1, 4160], BF16, "parowbf")
        casts = [("ab_wq", "st_wq", D, NC), ("ab_wk", "st_wk", D, NC),
                 ("ab_wv", "st_wv", D, NC), ("ab_wgq", "st_wgq", LDIM, NC),
                 ("ab_wgk", "st_wgk", LDIM, NC), ("ab_wgv", "st_wgv", LDIM, NC),
                 ("ab_w_out", "st_wout", D, NC), ("ab_r_out_w", "st_rout", D, LDIM // 128)]
        for wname, stname, wide, nchunk in casts:
            for c in range(nchunk):
                v_l = p.dma("sync", lambda e, c=c, wname=wname, wide=wide: e.dma_start(
                    out=wbuf[:, 0:wide],
                    in_=win[wname].rearrange("(c p) o -> c p o", p=128)[c]),
                    wait=[("dve", p.val("dve"))])
                v_c = p.op("vector", lambda v, wide=wide: v.tensor_copy(
                    wbf[:, 0:wide], wbuf[:, 0:wide]), wait=[v_l])
                v_st = p.dma("sync", lambda e, c=c, stname=stname, wide=wide:
                             e.dma_start(out=st[stname][c][:, 0:wide],
                                         in_=wbf[:, 0:wide]), wait=[("dve", v_c)])
                p.wait("vector", [v_st])
        gbuf = wview(24576, [HD, 2 * NH], F32, "pglw")
        v_l = p.dma("sync", lambda e: e.dma_start(out=gbuf[:], in_=tabs["tab_glwT"]))
        p.op("vector", lambda v: v.tensor_copy(sb["glwt"][0:64, :], gbuf[:]),
             wait=[v_l])
        p.op("vector", lambda v: v.tensor_copy(sb["glwt"][64:128, :], gbuf[:]))
        p.wait("sync", p.dmark())
        for stname, ro, wide in [("st_wq", "colsum_wq", D), ("st_wk", "colsum_wk", D),
                                 ("st_wv", "colsum_wv", D),
                                 ("st_wgq", "colsum_wgq", LDIM)]:
            v_cl = None
            for c in range(NC):
                wv = [("pe", v_cl)] if v_cl else [("pe", p.val("pe")),
                                                 ("dve", p.val("dve"))]
                v_l = p.dma("sync", lambda e, c=c, stname=stname, wide=wide:
                            e.dma_start(out=wbf[:, 0:wide],
                                        in_=st[stname][c][:, 0:wide]), wait=wv)
                ng = wide // 512
                for gg in range(ng):
                    v_cl = mm(ps_cs[0:1, 512 * gg:512 * (gg + 1)], sb["onecolbf"][:],
                              wbf[:, 512 * gg:512 * (gg + 1)], c == 0, c == NC - 1,
                              wait=[v_l] if gg == 0 else (),
                              inc="pe" if gg == ng - 1 else None)
            p.wait("vector", [("pe", v_cl)])
            v_n = p.op("vector", lambda v, wide=wide: v.tensor_scalar(
                rowbf2[0:1, 0:wide], ps_cs[0:1, 0:wide], -1.0, None, ALU.mult))
            v_s = p.dma("sync", lambda e, ro=ro, wide=wide: e.dma_start(
                out=st["st_rows"][:, RO[ro]:RO[ro] + wide], in_=rowbf2[0:1, 0:wide]),
                wait=[("dve", v_n)])
            p.wait("vector", [v_s])
        p.wait("sync", p.dmark())
        p.wait("gpsimd", p.dmark())
        p.barrier()
        pss.close()

    BB_VBL = 0
    BB_QLF = 4096
    BB_KLF = 8192
    BB_EBL = 12288
    BB_SG = 15360
    BB_SG2 = 23552
    BB_ZRL = 31744
    BB_XSL = 39936
    BB_UBF = 44032
    BB_GRBF = 48128
    BB_LOCALF = 52224
    BB_T1 = 56320
    BB_QKVT = 60416

    def attb_sub(s):
        ln_stats(s, attn_a_mode=False, rrep_flag=False)
        for c in range(NC):
            p.dma("gpsimd", lambda e, c=c: e.dma_start(
                out=st["st_xbf"][c], in_=X[c][:]))
        rcol_all = sb["colst"][:, CO["r"]:CO["r"] + 16]
        mcol_all = sb["colst"][:, CO["tmp"]:CO["tmp"] + 16]
        mrcol_all = sb["colst"][:, CO["R2"]:CO["R2"] + 16]
        p.op("vector", lambda v: v.tensor_mul(mrcol_all, mcol_all, rcol_all))
        mi64v = wview(49152, [128, S], BF16, "mi64")
        p.dma("gpsimd", lambda e: e.dma_start(
            out=mi64v[0:64, :], in_=st["st_mrow64"][0:1, :].to_broadcast([64, S])))
        p.wait("sync", p.dmark())
        p.wait("gpsimd", p.dmark())
        p.barrier()
        if "nob0" not in DBG:
            b0_latent(s)
        if "b0only" in DBG:
            return
        npair = NC if "b11" not in DBG else 1
        for hp in range(npair):
            b1_local(s, hp)
        if "nobout" not in DBG:
            b_out(s)

    def b0_latent(s):
        rcol_all = sb["colst"][:, CO["r"]:CO["r"] + 16]
        mrcol_all = sb["colst"][:, CO["R2"]:CO["R2"] + 16]
        mi64v = wview(49152, [128, S], BF16, "mi64")
        latm2 = aview(0, [128, S], BF16, "latm2")
        gqf = [aview(4096 + j * 4096, [128, S], BF16, "gqf") for j in range(4)]
        elat = aview(20480, [128, S], BF16, "elat")
        gkbf = aview(24576, [NLAT, LDIM], BF16, "gkbf")
        gvbf = aview(25600, [128, LDIM], BF16, "gvbf")
        gkt = aview(26624, [128, LDIM], BF16, "gkt")
        poolbf = aview(27648, [NLAT, D], BF16, "poolbf")
        poolT = aview(29696, [128, LDIM], BF16, "poolT")
        rc = aview(34816, [128, 128], BF16, "rcpool")
        onesw = aview(35072, [128, 512], BF16, "onesw")
        zre = aview(36096 if False else BB_ZRL, [33, S], F32, "zre")
        remf = [aview(44288 + j * 4096, [128, S], BF16, "remf") for j in range(4)]
        gqtmp = aview(60672, [128, 512], BF16, "gqtmp")
        rem2b = aview(62720, [128, 1024], BF16, "rem2b")
        xtts = [aview(30720 + i * 2048, [128, D], BF16, "xtts") for i in range(2)]
        xslb = [aview(BB_XSL + sl * 2048, [128, 1024], BF16, "xslab2")
                for sl in range(2)]
        wgql = [wview(c * 1024, [128, LDIM], BF16, "wgql") for c in range(NC)]
        wgkl = [wview(8192 + c * 1024, [128, LDIM], BF16, "wgkl") for c in range(NC)]
        wgvl = [wview(16384 + c * 1024, [128, LDIM], BF16, "wgvl") for c in range(NC)]
        routl = [wview(40960 + j * 2048, [128, D], BF16, "routl") for j in range(4)]
        cswgq = wview(53248, [64, LDIM], BF16, "cswgq")
        rem2f = [wview(c * 4096, [128, S], BF16, "rem2f") for c in range(NC)]

        for c in range(NC):
            p.dma("sync", lambda e, c=c: e.dma_start(
                out=wgql[c][:], in_=st["st_wgq"][c]))
            p.dma("sync", lambda e, c=c: e.dma_start(
                out=wgkl[c][:], in_=st["st_wgk"][c]))
            p.dma("sync", lambda e, c=c: e.dma_start(
                out=wgvl[c][:], in_=st["st_wgv"][c]))
        for j in range(4):
            p.dma("sync", lambda e, j=j: e.dma_start(
                out=routl[j][:], in_=st["st_rout"][j]))
        p.dma("sync", lambda e: e.dma_start(
            out=latm2[0:NLAT, :], in_=tabs["tab_latmask"]))
        p.dma("sync", lambda e: e.dma_start(
            out=latm2[NLAT:128, :], in_=tabs["tab_latmask"]))
        p.dma("gpsimd", lambda e: e.dma_start(
            out=cswgq[:], in_=st["st_rows"]
            [0:1, RO["colsum_wgq"]:RO["colsum_wgq"] + LDIM].to_broadcast([64, LDIM])))
        p.op("vector", lambda v: v.memset(onesw[:], 1.0))
        gml = p.dmark() + [("dve", p.val("dve"))]
        p.wait("tensor", gml)
        p.wait("vector", gml)

        pss = PsumScope()
        ps_gq = [pss.get([128, 512]) for _ in range(2)]
        ps_pool = pss.get([NLAT, D])
        ps_trx = [pss.get([128, 128], BF16) for _ in range(2)]
        ps_trg = [pss.get([128, 128], BF16) for _ in range(2)]

        def xb_load2(t, wv):
            return p.dma("sync", lambda e, t=t: e.dma_start(
                out=xslb[t % 2][:].rearrange("p (c u) -> p c u", u=128),
                in_=st["st_xbf"][:, :, 128 * t:128 * (t + 1)]
                .rearrange("c p u -> p c u")), wait=wv)

        xtok = {0: xb_load2(0, ()), 1: xb_load2(1, ())}
        ev2 = {}
        for t in range(NT):
            bi = t % 2
            tsl = slice(128 * t, 128 * (t + 1))
            pg = ps_gq[bi]
            wv0 = [xtok[t]]
            if t - 2 in ev2:
                wv0.append(("dve", ev2[t - 2]))
            for c in range(NC):
                mm(pg[:, 0:512], xslb[bi][:, 128 * c:128 * (c + 1)],
                   wgql[c][:], c == 0, False, wait=wv0 if c == 0 else ())
            v_gq = mm(pg[:, 0:512], mi64v[0:64, tsl], cswgq[:], False, True, inc="pe")
            vtr = {}
            for c in range(NC):
                v_t = tr(ps_trx[c % 2][:], xslb[bi][:, 128 * c:128 * (c + 1)],
                         wait=[("act", p.val("act"))] if c >= 2 else (), inc="pe")
                vtr[c] = p.op("scalar", lambda s_, c=c, bi=bi: s_.copy(
                    xtts[bi][:, 128 * c:128 * (c + 1)], ps_trx[c % 2][:]),
                    wait=[("pe", v_t)])
            p.op("vector", lambda v: v.memset(rc[:], 0.0),
                 wait=[("pe", p.val("pe"))])
            p.op("vector", lambda v, t=t: v.tensor_scalar(
                rc[:, 4 * t:4 * t + 4], sb["t0"][:], rcol_all[:, t:t + 1],
                1.0 / 32.0, ALU.mult, ALU.mult))
            v_rc2 = p.op("vector", lambda v, t=t: v.tensor_scalar(
                rc[:, 64 + 4 * t:64 + 4 * t + 4], sb["t0"][:], mrcol_all[:, t:t + 1],
                -1.0 / 32.0, ALU.mult, ALU.mult))
            for c in range(NC):
                mm(ps_pool[:, 128 * c:128 * (c + 1)], rc[:, 0:64],
                   xtts[bi][:, 128 * c:128 * (c + 1)],
                   t == 0 and c in (0, 4), False,
                   wait=[("act", vtr[NC - 1]), ("dve", v_rc2)] if c == 0 else ())
            v_pl = None
            for gg in range(2):
                v_pl = mm(ps_pool[:, 512 * gg:512 * (gg + 1)], rc[:, 64:128],
                          onesw[:], False, t == NT - 1,
                          inc="pe" if (t == NT - 1 and gg == 1) else None)
            v_ev = p.op("vector", lambda v, t=t, pg=pg: v.tensor_scalar(
                gqtmp[:], pg[:], rcol_all[:, t:t + 1], None, ALU.mult),
                wait=[("pe", v_gq)])
            ev2[t] = v_ev
            for j in range(4):
                wvt = [("dve", v_ev)] if j == 0 else []
                if j >= 2 or t >= 1:
                    wvt.append(("act", p.val("act")))
                v_t = tr(ps_trg[j % 2][:], gqtmp[:, 128 * j:128 * (j + 1)],
                         wait=wvt, inc="pe")
                p.op("scalar", lambda s_, j=j, t=t: s_.copy(
                    gqf[j][:, 128 * t:128 * (t + 1)], ps_trg[j % 2][:]),
                    wait=[("pe", v_t)])
            if t + 2 < NT:
                xtok[t + 2] = xb_load2(t + 2, [("pe", p.val("pe"))])
        p.wait("vector", [("pe", p.val("pe")), ("act", p.val("act"))])
        v_pb = p.op("vector", lambda v: v.tensor_copy(poolbf[:], ps_pool[:]))
        vtp = None
        for c in range(NC):
            wvt = [("dve", v_pb)] if c == 0 else []
            if c >= 2:
                wvt.append(("act", p.val("act")))
            v_t = tr(ps_trx[c % 2][0:NLAT, :].bitcast(BF16)[:, 0:NLAT]
                     if False else ps_trx[c % 2][:, 0:NLAT],
                     poolbf[:, 128 * c:128 * (c + 1)], wait=wvt, inc="pe")
            vtp = p.op("scalar", lambda s_, c=c: s_.copy(
                poolT[:, 64 * c:64 * (c + 1)], ps_trx[c % 2][:, 0:NLAT]),
                wait=[("pe", v_t)])
        pss.close()
        pss = PsumScope()
        ps_gk = pss.get([NLAT, LDIM])
        ps_gv = pss.get([NLAT, LDIM])
        ps_trk = [pss.get([NLAT, 64], BF16) for _ in range(2)]
        for c in range(NC):
            mm(ps_gk[:], poolT[:, 64 * c:64 * (c + 1)], wgkl[c][:], c == 0,
               c == NC - 1, wait=[("act", vtp)] if c == 0 else ())
        v_gv = None
        for c in range(NC):
            v_gv = mm(ps_gv[:], poolT[:, 64 * c:64 * (c + 1)], wgvl[c][:], c == 0,
                      c == NC - 1, inc="pe" if c == NC - 1 else None)
        p.wait("vector", [("pe", v_gv)])
        v_ek = p.op("vector", lambda v: v.tensor_copy(gkbf[:], ps_gk[:]))
        p.op("vector", lambda v: v.tensor_copy(gvbf[0:64, :], ps_gv[:]))
        v_evv = p.op("vector", lambda v: v.tensor_copy(gvbf[64:128, :], ps_gv[:]))
        for h in range(8):
            wvt = [("dve", v_ek)] if h == 0 else []
            if h >= 2:
                wvt.append(("act", p.val("act")))
            v_t = tr(ps_trk[h % 2][:], gkbf[:, 64 * h:64 * (h + 1)],
                     wait=wvt, inc="pe")
            p.op("scalar", lambda s_, h=h: s_.copy(
                gkt[0:64, 64 * h:64 * (h + 1)], ps_trk[h % 2][:]),
                wait=[("pe", v_t)])
            p.op("scalar", lambda s_, h=h: s_.copy(
                gkt[64:128, 64 * h:64 * (h + 1)], ps_trk[h % 2][:]))
        p.wait("tensor", [("act", p.val("act")), ("dve", v_evv)])
        pss.close()
        for lhp in range(4):
            pss = PsumScope()
            ps_lsc = pss.get([128, S])
            ps_zl = pss.get([33, S])
            v_last = None
            for hh in range(2):
                h = 2 * lhp + hh
                for ch in range(4):
                    csl = slice(512 * ch, 512 * (ch + 1))
                    v_last = mm(ps_lsc[64 * hh:64 * hh + 64, csl],
                                gkt[64 * hh:64 * hh + 64, 64 * h:64 * (h + 1)],
                                gqf[lhp][64 * hh:64 * hh + 64, csl], True, True,
                                wait=[("dve", p.val("dve"))]
                                if hh == 0 and ch == 0 else (),
                                inc="pe" if hh == 1 and ch == 3 else None)
            v_e = None
            for ch in range(4):
                csl = slice(512 * ch, 512 * (ch + 1))
                for hh in range(2):
                    hsl = slice(64 * hh, 64 * hh + 64)
                    v_e = p.op("scalar", lambda s_, csl=csl, hsl=hsl, pl=ps_lsc:
                               s_.activation(elat[hsl, csl], pl[hsl, csl],
                                             AF.Exp, scale=0.125),
                               wait=[("pe", v_last)] if ch == 0 and hh == 0 else ())
            v_m = p.op("vector", lambda v: v.tensor_mul(elat[:], elat[:], latm2[:]),
                       wait=[("act", v_e)])
            v_z = None
            for hh in range(2):
                for ch in range(4):
                    csl = slice(512 * ch, 512 * (ch + 1))
                    v_z = mm(ps_zl[32 * hh:32 * hh + 1, csl],
                             sb["onecolbf"][64 * hh:64 * hh + 64, :],
                             elat[64 * hh:64 * hh + 64, csl], True, True,
                             wait=[("dve", v_m)] if hh == 0 and ch == 0 else (),
                             inc="pe" if hh == 1 and ch == 3 else None)
            for r0 in (0, 32):
                p.op("scalar", lambda s_, pz=ps_zl, r0=r0: s_.activation(
                    zre[r0:r0 + 1, :], pz[r0:r0 + 1, :], AF.Ln),
                    wait=[("pe", v_z)] if r0 == 0 else ())
                v_rec = p.op("scalar", lambda s_, r0=r0: s_.activation(
                    zre[r0:r0 + 1, :], zre[r0:r0 + 1, :], AF.Exp, scale=-1.0))
            pss.close()
            pss = PsumScope()
            ps_zrep = pss.get([128, S])
            ps_rem = pss.get([128, S])
            v_rp = None
            for hh in range(2):
                for ch in range(4):
                    csl = slice(512 * ch, 512 * (ch + 1))
                    v_rp = mm(ps_zrep[64 * hh:64 * hh + 64, csl],
                              sb["onesf32"][32 * hh:32 * hh + 1, 0:64],
                              zre[32 * hh:32 * hh + 1, csl], True, True,
                              wait=[("act", v_rec)] if hh == 0 and ch == 0 else (),
                              inc="pe" if hh == 1 and ch == 3 else None)
            v_nrm = p.op("vector", lambda v, pzr=ps_zrep: v.tensor_mul(
                elat[:], elat[:], pzr[:]), wait=[("pe", v_rp)])
            v_rm = None
            for hh in range(2):
                h = 2 * lhp + hh
                for ch in range(4):
                    csl = slice(512 * ch, 512 * (ch + 1))
                    v_rm = mm(ps_rem[64 * hh:64 * hh + 64, csl],
                              gvbf[64 * hh:64 * hh + 64, 64 * h:64 * (h + 1)],
                              elat[64 * hh:64 * hh + 64, csl], True, True,
                              wait=[("dve", v_nrm)] if hh == 0 and ch == 0 else (),
                              inc="pe" if hh == 1 and ch == 3 else None)
            p.op("vector", lambda v, lhp=lhp, pr=ps_rem: v.tensor_copy(
                remf[lhp][:], pr[:]), wait=[("pe", v_rm)])
            pss.close()
        pss = PsumScope()
        ps_r2 = [pss.get([128, 512]) for _ in range(2)]
        ps_trr = [pss.get([128, 128], BF16) for _ in range(2)]
        for t in range(NT):
            tsl = slice(128 * t, 128 * (t + 1))
            v_r2 = None
            for half in range(2):
                pk = ps_r2[half]
                osl = slice(512 * half, 512 * (half + 1))
                for j in range(4):
                    v_r2 = mm(pk[:], remf[j][:, tsl], routl[j][:, osl],
                              j == 0, j == 3,
                              wait=[("dve", p.val("dve"))] if j == 0 else (),
                              inc="pe" if j == 3 else None)
                p.op("vector", lambda v, pk=pk, osl=osl: v.tensor_copy(
                    rem2b[:, osl], pk[:]), wait=[("pe", v_r2)])
            for c in range(NC):
                wvt = [("dve", p.val("dve"))] if c == 0 else []
                if c >= 2 or t >= 1:
                    wvt.append(("act", p.val("act")))
                v_t = tr(ps_trr[c % 2][:], rem2b[:, 128 * c:128 * (c + 1)],
                         wait=wvt, inc="pe")
                p.op("scalar", lambda s_, c=c, t=t: s_.copy(
                    rem2f[c][:, 128 * t:128 * (t + 1)], ps_trr[c % 2][:]),
                    wait=[("pe", v_t)])
        p.wait("tensor", [("act", p.val("act")), ("dve", p.val("dve"))])
        pss.close()
        p.barrier()

    def b1_local(s, hp):
        rcol_all = sb["colst"][:, CO["r"]:CO["r"] + 16]
        mi64v = wview(49152, [128, S], BF16, "mi64")
        rem2f = [wview(c * 4096, [128, S], BF16, "rem2f") for c in range(NC)]
        vbl = aview(BB_VBL, [128, S], BF16, "latm2")
        qlf = aview(BB_QLF, [128, S], BF16, "gqf")
        klf = aview(BB_KLF, [128, S], BF16, "gqf")
        eblb = [aview(BB_EBL + i * 1536, [128, 768], BF16, "eblb") for i in range(2)]
        sg = aview(BB_SG, [33, S], F32, "sgrow")
        sg2 = aview(BB_SG2, [33, S], F32, "sgrow2")
        zrl = aview(BB_ZRL, [33, S], F32, "zre")
        ubf = aview(BB_UBF, [33, S], BF16, "ubf")
        grbf = aview(BB_GRBF, [33, S], BF16, "grbf")
        localf = aview(BB_LOCALF, [128, S], BF16, "localf")
        t1buf = aview(BB_T1, [128, S], BF16, "t1buf")
        qkvtmp = aview(BB_QKVT, [128, 384], BF16, "qkvtmp")
        xslb = [aview(BB_XSL + sl * 2048, [128, 1024], BF16, "xslab2")
                for sl in range(2)]
        wlpk = [wview(32768 + c * 768, [128, 384], BF16, "wlpk") for c in range(NC)]
        cs3 = wview(38912, [64, 384], BF16, "cs3")
        for c in range(NC):
            for k3, stname in enumerate(("st_wq", "st_wk", "st_wv")):
                p.dma("sync", lambda e, c=c, k3=k3, stname=stname:
                      e.dma_start(out=wlpk[c][:, 128 * k3:128 * (k3 + 1)],
                                  in_=st[stname][c][:, 128 * hp:128 * (hp + 1)]))
        for k3, ro in enumerate(("colsum_wq", "colsum_wk", "colsum_wv")):
            p.dma("gpsimd", lambda e, k3=k3, ro=ro: e.dma_start(
                out=cs3[:, 128 * k3:128 * (k3 + 1)],
                in_=st["st_rows"][0:1, RO[ro] + 128 * hp:RO[ro] + 128 * (hp + 1)]
                .to_broadcast([64, 128])))
        gml = p.dmark()
        p.wait("tensor", gml)
        p.wait("vector", gml)
        pss = PsumScope()
        ps_qkv = [pss.get([128, 512]) for _ in range(2)]
        ps_trq = [pss.get([128, 128], BF16) for _ in range(2)]

        def xb_load3(t, wv):
            return p.dma("sync", lambda e, t=t: e.dma_start(
                out=xslb[t % 2][:].rearrange("p (c u) -> p c u", u=128),
                in_=st["st_xbf"][:, :, 128 * t:128 * (t + 1)]
                .rearrange("c p u -> p c u")), wait=wv)

        xtok = {0: xb_load3(0, ()), 1: xb_load3(1, ())}
        ev2 = {}
        for t in range(NT):
            bi = t % 2
            tsl = slice(128 * t, 128 * (t + 1))
            pg = ps_qkv[bi]
            wv0 = [xtok[t]]
            if t - 2 in ev2:
                wv0.append(("dve", ev2[t - 2]))
            for c in range(NC):
                mm(pg[:, 0:384], xslb[bi][:, 128 * c:128 * (c + 1)],
                   wlpk[c][:], c == 0, False, wait=wv0 if c == 0 else ())
            v_pj = mm(pg[:, 0:384], mi64v[0:64, tsl], cs3[:], False, True, inc="pe")
            if t + 2 < NT:
                xtok[t + 2] = xb_load3(t + 2, [("pe", v_pj)])
            v_sc = p.op("vector", lambda v, t=t, pg=pg: v.tensor_scalar(
                qkvtmp[:], pg[:, 0:384], rcol_all[:, t:t + 1], None, ALU.mult),
                wait=[("pe", v_pj)])
            v_vv = p.op("vector", lambda v, t=t: v.tensor_copy(
                vbl[:, 128 * t:128 * (t + 1)], qkvtmp[:, 256:384]))
            ev2[t] = v_vv
            for j, dstf in ((0, qlf), (1, klf)):
                wvt = [("dve", v_vv)] if j == 0 else []
                if t >= 1:
                    wvt.append(("act", p.val("act")))
                v_t = tr(ps_trq[j][:], qkvtmp[:, 128 * j:128 * (j + 1)],
                         wait=wvt, inc="pe")
                p.op("scalar", lambda s_, dstf=dstf, t=t, j=j: s_.copy(
                    dstf[:, 128 * t:128 * (t + 1)], ps_trq[j][:]),
                    wait=[("pe", v_t)])
        p.wait("tensor", [("act", p.val("act"))])
        pss.close()
        if "b1proj" in DBG:
            p.barrier()
            return
        pss = PsumScope()
        ps_g1 = pss.get([33, S])
        ps_g2 = pss.get([33, S])
        v_g = None
        for hh in range(2):
            h = 2 * hp + hh
            for gi, (pgx, coloff) in enumerate(((ps_g1, 0), (ps_g2, NH))):
                for ch in range(4):
                    csl = slice(512 * ch, 512 * (ch + 1))
                    v_g = mm(pgx[32 * hh:32 * hh + 1, csl],
                             sb["glwt"][64 * hh:64 * hh + 64,
                                        coloff + h:coloff + h + 1],
                             qlf[64 * hh:64 * hh + 64, csl], True, True,
                             wait=[("act", p.val("act"))]
                             if hh == 0 and gi == 0 and ch == 0 else (),
                             inc="pe" if (hh == 1 and gi == 1 and ch == 3) else None)
        for r0 in (0, 32):
            p.op("scalar", lambda s_, r0=r0: s_.activation(
                sg[r0:r0 + 1, :], ps_g1[r0:r0 + 1, :], AF.Exp, scale=-1.0),
                wait=[("pe", v_g)] if r0 == 0 else ())
        v_e2 = None
        for r0 in (0, 32):
            v_e2 = p.op("scalar", lambda s_, r0=r0: s_.activation(
                sg2[r0:r0 + 1, :], ps_g2[r0:r0 + 1, :], AF.Exp, scale=-1.0))
        v_p1 = None
        for r0 in (0, 32):
            p.op("vector", lambda v, r0=r0: v.tensor_scalar(
                sg[r0:r0 + 1, :], sg[r0:r0 + 1, :], 1.0, None, ALU.add),
                wait=[("act", v_e2)] if r0 == 0 else ())
            v_p1 = p.op("vector", lambda v, r0=r0: v.tensor_scalar(
                sg2[r0:r0 + 1, :], sg2[r0:r0 + 1, :], 1.0, None, ALU.add))
        v_sig = None
        for r0 in (0, 32):
            for buf in (sg, sg2):
                p.op("scalar", lambda s_, r0=r0, buf=buf: s_.activation(
                    buf[r0:r0 + 1, :], buf[r0:r0 + 1, :], AF.Ln),
                    wait=[("dve", v_p1)] if r0 == 0 and buf is sg else ())
                v_sig = p.op("scalar", lambda s_, r0=r0, buf=buf: s_.activation(
                    buf[r0:r0 + 1, :], buf[r0:r0 + 1, :], AF.Exp, scale=-1.0))
        pss.close()
        if "b1gate" in DBG:
            p.barrier()
            return
        pss = PsumScope()
        ps_sl = [[pss.get([128, 512]) for _ in range(2)] for _ in range(2)]
        ps_av = [pss.get([128, 128]) for _ in range(2)]
        ps_zt = [pss.get([65, 128]) for _ in range(2)]
        for i in range(2):
            p.op("vector", lambda v, i=i: v.memset(eblb[i][:], 0.0))
        v_lf = {}
        v_zev = {}
        for t in range(NT):
            bi = t % 2
            tsl = slice(128 * t, 128 * (t + 1))
            kts = [t - 2, t - 1, t]
            v_s = None
            for ri, kt in enumerate(kts):
                if kt < 0:
                    continue
                ksl = slice(128 * kt, 128 * (kt + 1))
                for hh in range(2):
                    wv = []
                    if v_s is None:
                        wv = [("act", p.val("act")), ("dve", p.val("dve"))]
                        if t - 2 in v_lf:
                            wv.append(("act", v_lf[t - 2]))
                    v_s = mm(ps_sl[bi][hh][:, 128 * ri:128 * (ri + 1)],
                             klf[64 * hh:64 * hh + 64, ksl],
                             qlf[64 * hh:64 * hh + 64, tsl], True, True,
                             wait=wv, inc=None)
            if "nohack" not in DBG:
                p.q["tensor"][-1] = (p.q["tensor"][-1][0], p.q["tensor"][-1][1],
                                     "pe", 1)
                p.cnt["pe"] += 1
                v_s = p.cnt["pe"]
            if "noexp" in DBG:
                continue
            lo = 256 if t == 0 else (128 if t == 1 else 0)
            v_e = None
            for hh in range(2):
                v_e = p.op("scalar", lambda s_, bi=bi, hh=hh, lo=lo: s_.activation(
                    eblb[bi][:, 384 * hh + lo:384 * (hh + 1)],
                    ps_sl[bi][hh][:, lo:384], AF.Exp, scale=0.125),
                    wait=[("pe", v_s)] if hh == 0 else ())
            ee = eblb[bi][:].rearrange("p (h m) -> p h m", h=2)
            lmb = sb["lmask"][:].unsqueeze(1).to_broadcast([128, 2, 384])
            if "nomask" in DBG:
                v_m = p.op("vector", lambda v: v.engine_nop(),
                           wait=[("act", v_e)])
            else:
                v_m = p.op("vector", lambda v, ee=ee, lmb=lmb: v.tensor_tensor(
                    ee, ee, lmb, ALU.mult), wait=[("act", v_e)])
            v_av = None
            v_z = None
            for hh in range(2):
                if "nozmm" in DBG:
                    break
                first = True
                for ri, kt in enumerate(kts):
                    if kt < 0:
                        continue
                    rsl = slice(384 * hh + 128 * ri, 384 * hh + 128 * (ri + 1))
                    wvz = []
                    if hh == 0 and first:
                        wvz = [("dve", v_m)]
                        if t - 2 in v_zev:
                            wvz.append(("act", v_zev[t - 2]))
                    v_z = mm(ps_zt[bi][64 * hh:64 * hh + 1, 0:128],
                             sb["onecolbf"][:], eblb[bi][:, rsl], first, kt == t,
                             wait=wvz,
                             inc="pe" if (hh == 1 and kt == t) else None)
                    first = False
            for hh in range(2):
                if "noav" in DBG:
                    break
                first = True
                for ri, kt in enumerate(kts):
                    if kt < 0:
                        continue
                    rsl = slice(384 * hh + 128 * ri, 384 * hh + 128 * (ri + 1))
                    v_av = mm(ps_av[bi][64 * hh:64 * hh + 64, 0:128],
                              vbl[:, 128 * kt + 64 * hh:128 * kt + 64 * (hh + 1)],
                              eblb[bi][:, rsl], first, kt == t,
                              wait=[("dve", v_m)] if (hh == 0 and first
                                                      and "nozmm" in DBG) else (),
                              inc="pe" if (hh == 1 and kt == t) else None)
                    first = False
            if "nozmm" not in DBG:
                p.op("scalar", lambda s_, bi=bi, tsl=tsl: s_.copy(
                    zrl[0:1, tsl], ps_zt[bi][0:1, :]), wait=[("pe", v_z)])
                v_zev[t] = p.op("scalar", lambda s_, bi=bi, tsl=tsl: s_.copy(
                    zrl[32:33, tsl], ps_zt[bi][64:65, :]))
            if "noav" not in DBG:
                v_lf[t] = p.op("scalar", lambda s_, bi=bi, tsl=tsl: s_.copy(
                    localf[:, tsl], ps_av[bi][:]), wait=[("pe", v_av)])
        p.wait("vector", [("pe", p.val("pe")), ("act", p.val("act"))])
        if "b1attn" in DBG:
            pss.close()
            p.barrier()
            return
        v_zi = None
        for r0 in (0, 32):
            p.op("scalar", lambda s_, r0=r0: s_.activation(
                zrl[r0:r0 + 1, :], zrl[r0:r0 + 1, :], AF.Ln))
            v_zi = p.op("scalar", lambda s_, r0=r0: s_.activation(
                zrl[r0:r0 + 1, :], zrl[r0:r0 + 1, :], AF.Exp, scale=-1.0))
        v_gr = None
        for r0 in (0, 32):
            p.op("vector", lambda v, r0=r0: v.tensor_mul(
                ubf[r0:r0 + 1, :], sg[r0:r0 + 1, :], zrl[r0:r0 + 1, :]),
                wait=[("act", v_zi)] if r0 == 0 else ())
            v_gr = p.op("vector", lambda v, r0=r0: v.tensor_copy(
                grbf[r0:r0 + 1, :], sg2[r0:r0 + 1, :]))
        pss.close()
        pss = PsumScope()
        ps_u = pss.get([128, S])
        ps_gr = pss.get([128, S])
        v_rep = None
        for hh in range(2):
            for ch in range(4):
                csl = slice(512 * ch, 512 * (ch + 1))
                mm(ps_u[64 * hh:64 * hh + 64, csl],
                   sb["onesb32"][32 * hh:32 * hh + 1, 0:64],
                   ubf[32 * hh:32 * hh + 1, csl], True, True,
                   wait=[("dve", v_gr)] if hh == 0 and ch == 0 else ())
                v_rep = mm(ps_gr[64 * hh:64 * hh + 64, csl],
                           sb["onesb32"][32 * hh:32 * hh + 1, 0:64],
                           grbf[32 * hh:32 * hh + 1, csl], True, True,
                           inc="pe" if hh == 1 and ch == 3 else None)
        v_t1 = p.op("vector", lambda v: v.tensor_mul(t1buf[:], localf[:], ps_u[:]),
                    wait=[("pe", v_rep)])
        v_t2 = p.op("vector", lambda v, hp=hp: v.tensor_mul(
            rem2f[hp][:], rem2f[hp][:], ps_gr[:]))
        v_t3 = p.op("vector", lambda v, hp=hp: v.tensor_add(
            rem2f[hp][:], rem2f[hp][:], t1buf[:]))
        p.wait("tensor", [("dve", v_t3)])
        pss.close()
        p.barrier()

    def b_out(s):
        rem2f = [wview(c * 4096, [128, S], BF16, "rem2f") for c in range(NC)]
        woutl = [wview(32768 + i * 2048, [128, D], BF16, "woutl") for i in range(NC)]
        for i in range(NC):
            p.dma("sync", lambda e, i=i: e.dma_start(
                out=woutl[i][:], in_=st["st_wout"][i]))
        gml = p.dmark()
        p.wait("tensor", gml)
        pss = PsumScope()
        po = [pss.get([128, 512]) for _ in range(2)]
        ep_done = {}
        for oc in range(NC):
            for tg in range(4):
                tsl = slice(512 * tg, 512 * (tg + 1))
                idx = oc * 4 + tg
                pk = po[idx % 2]
                wv = [("dve", p.val("dve"))] if idx == 0 else ()
                if idx >= 2:
                    wv = [("dve", ep_done[idx - 2])]
                v_mm = None
                for i in range(NC):
                    v_mm = mm(pk[:], woutl[i][:, 128 * oc:128 * (oc + 1)],
                              rem2f[i][:, tsl], i == 0, i == NC - 1,
                              wait=wv if i == 0 else (),
                              inc="pe" if i == NC - 1 else None)
                v_ep = p.op("vector", lambda v, oc=oc, tsl=tsl, pk=pk:
                            v.tensor_add(X[oc][:, tsl], X[oc][:, tsl], pk[:]),
                            wait=[("pe", v_mm)], inc="dve")
                ep_done[idx] = v_ep
        p.barrier()
        pss.close()

    prep_consts()
    if ENABLE["atta"] and "noaprep" not in DBG:
        prep_atta()
    if ENABLE["ffn"] and "noprep" not in DBG:
        prep_ffn()
    if ENABLE["attb"] and "nobprep" not in DBG:
        prep_attb()
    for s in range(n_seq):
        load_x(s)
        if ENABLE["atta"] and "noasub" not in DBG:
            atta_sub(s)
        if ENABLE["ffn"] and "nosub" not in DBG:
            ffn_sub(s)
        if ENABLE["attb"] and "nobsub" not in DBG:
            attb_sub(s)
        if ENABLE["ffn"] and "nosub" not in DBG and "one" not in DBG:
            ffn_sub(s)
        store_y(s)
    p.wait("sync", p.dmark())
    p.wait("gpsimd", p.dmark())

    p.finalize(block, sems)
    est.close()
    return nc



WMAP = {
    "aa_qn_w": "L0_attn_a_qn_w", "aa_qn_b": "L0_attn_a_qn_b",
    "aa_q_w": "L0_attn_a_q_w", "aa_q_b": "L0_attn_a_q_b",
    "aa_kvn_w": "L0_attn_a_kvn_w", "aa_kvn_b": "L0_attn_a_kvn_b",
    "aa_kv_w": "L0_attn_a_kv_w", "aa_kv_b": "L0_attn_a_kv_b",
    "aa_out_w": "L0_attn_a_out_w", "aa_out_b": "L0_attn_a_out_b",
    "aa_ln_w": "L0_attn_a_ln_w", "aa_ln_b": "L0_attn_a_ln_b",
    "aa_rh_w": "L0_attn_a_rh_w", "aa_rh_b": "L0_attn_a_rh_b",
    "ab_wq": "L0_attn_b_wq", "ab_wk": "L0_attn_b_wk", "ab_wv": "L0_attn_b_wv",
    "ab_wgq": "L0_attn_b_wgq", "ab_wgk": "L0_attn_b_wgk", "ab_wgv": "L0_attn_b_wgv",
    "ab_r_out_w": "L0_attn_b_r_out_w", "ab_glw": "L0_attn_b_gate_local_w",
    "ab_grw": "L0_attn_b_gate_remote_w", "ab_w_out": "L0_attn_b_w_out",
    "ffn_wr": "L0_ffn_w_freq_real", "ffn_wi": "L0_ffn_w_freq_imag",
    "ffn_bias": "L0_ffn_bias",
}


def kernel(**inputs):
    ht = host_tables()
    ht["tab_glwT"] = np.ascontiguousarray(np.concatenate(
        [np.asarray(inputs["L0_attn_b_gate_local_w"]).T,
         np.asarray(inputs["L0_attn_b_gate_remote_w"]).T], axis=1).astype(np.float32))
    nc = build_program()
    x = np.ascontiguousarray(inputs["L0_attn_a_x"])
    in_maps = []
    for core in range(NCORES):
        m = {"xs": np.ascontiguousarray(x[core * NSEQ:(core + 1) * NSEQ])}
        for k, src in WMAP.items():
            m[k] = np.ascontiguousarray(inputs[src])
        m.update(ht)
        in_maps.append(m)
    res = run_bass_kernel_spmd(nc, in_maps, core_ids=list(range(NCORES)))
    return np.concatenate([r["ys"] for r in res.results], axis=0)


# version 14
# speedup vs baseline: 1.1269x; 1.1269x over previous
import math
from collections import defaultdict
from contextlib import ExitStack

import numpy as np
import ml_dtypes

import concourse.bass as bass
import concourse.mybir as mybir
from concourse.bass_utils import run_bass_kernel_spmd

F32 = mybir.dt.float32
F32R = mybir.dt.float32r
BF16 = mybir.dt.bfloat16
AF = mybir.ActivationFunctionType
ALU = mybir.AluOpType
AX = mybir.AxisListType

D = 1024
S = 2048
B = 32
NCORES = 8
NSEQ = B // NCORES
NH = 16
HD = 64
NLH = 8
LDIM = 512
NLAT = 64
POOLC = 32
WIN = 256
EPS = 1e-5
NT = S // 128
NC = D // 128
NG = 4
HPG = NH // NG
QAW = 272

import os as _os
ENABLE = {"atta": True, "ffn": True, "attb": True}
DBG = _os.environ.get("KDBG", "")



def host_tables():
    t = {}
    d2 = HD // 2
    mel_max = 2595.0 * np.log10(1.0 + 4000.0 / 200.0)
    scale = np.power(10.0, np.linspace(0.0, mel_max, d2, dtype=np.float64) / 2595.0) - 1.0
    fb = 200.0 * scale / 1000.0
    freqs = np.arange(S, dtype=np.float64)[:, None] * fb[None, :]
    c = np.cos(freqs); s = np.sin(freqs)
    C1 = np.zeros((S, HD), np.float32); C2 = np.zeros((S, HD), np.float32)
    CS = np.zeros((S, HD), np.float32)
    C1[:, 0::2] = c; C1[:, 1::2] = s
    C2[:, 0::2] = s; C2[:, 1::2] = c
    CS[:, 0::2] = c + s; CS[:, 1::2] = c - s
    t["tab_c1"] = C1; t["tab_c2"] = C2; t["tab_cs"] = CS
    ki = np.arange(128)[:, None]; qi = np.arange(128)[None, :]
    lm = np.zeros((128, 384), np.float32)
    lm[:, 0:128] = (qi < ki)
    lm[:, 128:256] = 1.0
    lm[:, 256:384] = (ki <= qi)
    t["tab_localmask"] = lm.astype(ml_dtypes.bfloat16)
    n = np.arange(NLAT)[:, None]; tt = np.arange(S)[None, :]
    t["tab_latmask"] = (n <= (tt // POOLC)).astype(np.float32).astype(ml_dtypes.bfloat16)
    t["tab_ident"] = np.eye(128, dtype=np.float32).astype(ml_dtypes.bfloat16)
    t["tab_identf"] = np.eye(128, dtype=np.float32)
    ab = np.arange(D, dtype=np.float64)
    ang = (np.outer(ab, ab) % D) * (2.0 * np.pi / D)
    t["tab_fcos"] = np.cos(ang).astype(np.float32)
    t["tab_fsin"] = np.sin(ang).astype(np.float32)
    e16 = np.zeros((16, 8 * 128), np.float32)
    for cc in range(8):
        e16[2 * cc, cc * 128: cc * 128 + 64] = 1.0
        e16[2 * cc + 1, cc * 128 + 64: cc * 128 + 128] = 1.0
    t["tab_e16"] = e16.astype(ml_dtypes.bfloat16)
    t0 = np.zeros((128, 4), np.float32)
    for j in range(4):
        t0[32 * j:32 * (j + 1), j] = 1.0
    t["tab_t0"] = t0.astype(ml_dtypes.bfloat16)
    return t



class Prog:
    ENGINES = ("tensor", "vector", "scalar", "gpsimd", "sync")
    DMA_Q = ("q0", "q1", "q2", "q3", "q4", "q5")
    DMA_G = ("g0", "g1", "g2", "g3")

    def __init__(self, nc):
        self.nc = nc
        self.q = {e: [] for e in self.ENGINES}
        self.cnt = defaultdict(int)
        self.rr = {"sync": 0, "gpsimd": 0}

    def val(self, sem):
        return self.cnt[sem]

    AUTO = {"vector": "dve", "scalar": "act", "gpsimd": "pool"}

    def op(self, eng, fn, wait=(), inc=None, amt=1, selfsync=True):
        if fn is not None and selfsync and eng in self.AUTO and inc is None:
            inc = self.AUTO[eng]
            amt = 1
        self.q[eng].append((fn, [(s, v) for (s, v) in wait if v > 0], inc, amt))
        if inc is not None:
            self.cnt[inc] += amt
            v = self.cnt[inc]
            if fn is not None and selfsync and eng in self.AUTO:
                self.q[eng].append(("W", [(inc, v)], None, 0))
            return v
        return None

    def dma(self, eng, fn, wait=()):
        pool = self.DMA_G if eng == "gpsimd" else self.DMA_Q
        sem = pool[self.rr[eng] % len(pool)]
        self.rr[eng] += 1
        prev = self.cnt[sem]
        w = [(s, v) for (s, v) in wait if v > 0]
        if prev:
            w.append((sem, prev))
        v = self.op(eng, fn, wait=w, inc=sem, amt=16, selfsync=False)
        return (sem, v)

    def dmark(self):
        return [(s, self.cnt[s]) for s in self.DMA_Q + self.DMA_G if self.cnt[s]]

    def barrier(self):
        vals = {}
        for e in ("tensor", "vector", "scalar", "gpsimd"):
            vals[e] = self.op(e, None, inc="b_" + e)
        for e in self.ENGINES:
            w = [("b_" + o, vals[o]) for o in vals if o != e]
            self.q[e].append(("W", w, None, 0))

    def vop(self, fn):
        vv = self.op("vector", fn, inc="dve")
        self.q["vector"].append(("W", [("dve", vv)], None, 0))
        return vv

    def wait(self, eng, wait):
        self.q[eng].append(("W", [(s, v) for (s, v) in wait if v > 0], None, 0))

    def finalize(self, block, sems):
        for ename in self.ENGINES:
            entries = self.q[ename]

            def run(engine, entries=entries, ename=ename):
                for fn, waits, inc, amt in entries:
                    for smn, v in waits:
                        engine.wait_ge(sems[smn], v)
                    if fn == "W":
                        continue
                    if fn is None:
                        assert inc is not None
                        engine.sem_inc(sems[inc], amt)
                        continue
                    instr = fn(engine)
                    if inc is not None:
                        assert instr is not None, f"{ename} op must return instr for {inc}"
                        instr.then_inc(sems[inc], amt)

            getattr(block, ename)(run)



def build_program(n_seq=NSEQ):
    nc = bass.Bass("TRN2", target_bir_lowering=False)

    xin = nc.dram_tensor("xs", [n_seq, S, D], F32, kind="ExternalInput").ap()
    yout = nc.dram_tensor("ys", [n_seq, S, D], F32, kind="ExternalOutput").ap()
    dbg = {}
    if "dump" in DBG:
        dbg["rows"] = nc.dram_tensor("dbg_rows", [1, 5 * S], BF16, kind="ExternalOutput").ap()
        dbg["rrep"] = nc.dram_tensor("dbg_rrep", [128, S], F32, kind="ExternalOutput").ap()
        dbg["colst"] = nc.dram_tensor("dbg_colst", [128, 64], F32, kind="ExternalOutput").ap()
        dbg["chain"] = nc.dram_tensor("dbg_chain", [128, 64], F32, kind="ExternalOutput").ap()
        dbg["r1"] = nc.dram_tensor("dbg_r1", [128, 512], F32, kind="ExternalOutput").ap()
        dbg["ops"] = nc.dram_tensor("dbg_ops", [128, 512], F32, kind="ExternalOutput").ap()
        dbg["qnf"] = nc.dram_tensor("dbg_qnf", [128, S], BF16, kind="ExternalOutput").ap()
        dbg["knf"] = nc.dram_tensor("dbg_knf", [128, S], BF16, kind="ExternalOutput").ap()
        dbg["vb"] = nc.dram_tensor("dbg_vb", [128, NT * 260], BF16, kind="ExternalOutput").ap()
        dbg["pb"] = nc.dram_tensor("dbg_pb", [128, 1024], BF16, kind="ExternalOutput").ap()
        dbg["stkv"] = nc.dram_tensor("dbg_stkv", [128, 2048], BF16, kind="ExternalOutput").ap()
        dbg["stqw"] = nc.dram_tensor("dbg_stqw", [128, 1040], BF16, kind="ExternalOutput").ap()
        dbg["strows"] = nc.dram_tensor("dbg_strows", [1, 16384], BF16, kind="ExternalOutput").ap()
        dbg["csprep"] = nc.dram_tensor("dbg_csprep", [1, D], BF16, kind="ExternalOutput").ap()

    win = {}
    for nm, shp in [
        ("aa_qn_w", [D]), ("aa_qn_b", [D]), ("aa_q_w", [D, D]), ("aa_q_b", [D]),
        ("aa_kvn_w", [D]), ("aa_kvn_b", [D]), ("aa_kv_w", [D, 2 * D]), ("aa_kv_b", [2 * D]),
        ("aa_out_w", [D, D]), ("aa_out_b", [D]), ("aa_ln_w", [HD]), ("aa_ln_b", [HD]),
        ("aa_rh_w", [D, 1]), ("aa_rh_b", [1]),
        ("ab_wq", [D, D]), ("ab_wk", [D, D]), ("ab_wv", [D, D]),
        ("ab_wgq", [D, LDIM]), ("ab_wgk", [D, LDIM]), ("ab_wgv", [D, LDIM]),
        ("ab_r_out_w", [LDIM, D]), ("ab_glw", [NH, HD]), ("ab_grw", [NH, HD]),
        ("ab_w_out", [D, D]),
        ("ffn_wr", [D, D]), ("ffn_wi", [D, D]), ("ffn_bias", [D]),
    ]:
        win[nm] = nc.dram_tensor(nm, shp, F32, kind="ExternalInput").ap()

    tabs = {}
    for nm, shp, dt_ in [
        ("tab_c1", [S, HD], F32), ("tab_c2", [S, HD], F32), ("tab_cs", [S, HD], F32),
        ("tab_localmask", [128, 384], BF16), ("tab_latmask", [NLAT, S], BF16),
        ("tab_ident", [128, 128], BF16), ("tab_identf", [128, 128], F32),
        ("tab_fcos", [D, D], F32),
        ("tab_fsin", [D, D], F32), ("tab_e16", [16, 8 * 128], BF16),
        ("tab_t0", [128, 4], BF16), ("tab_glwT", [HD, 2 * NH], F32),
    ]:
        tabs[nm] = nc.dram_tensor(nm, shp, dt_, kind="ExternalInput").ap()

    st = {}
    for nm, shp, dt_ in [
        ("st_qw", [NC, 128, 1040], BF16),
        ("st_kv", [NC, 128, 2 * D], BF16),
        ("st_outw", [NC, 128, D], BF16),
        ("st_M", [NC, 128, D], F32R),
        ("st_wq", [NC, 128, 1152], BF16),
        ("st_wk", [NC, 128, D], BF16),
        ("st_wv", [NC, 128, D], BF16),
        ("st_wgq", [NC, 128, LDIM], BF16),
        ("st_wgk", [NC, 128, LDIM], BF16),
        ("st_wgv", [NC, 128, LDIM], BF16),
        ("st_rout", [LDIM // 128, 128, D], BF16),
        ("st_wout", [NC, 128, D], BF16),
        ("st_rows", [1, 16384], BF16),
        ("st_mrow", [1, S], BF16),
        ("st_mrow64", [1, S], BF16),
        ("st_irow64", [1, S], BF16),
        ("st_xbf", [NC, 128, S], BF16),
    ]:
        st[nm] = nc.dram_tensor(nm, shp, dt_, kind="Internal").ap()

    RO = {}
    _o = [0]
    for nm, n in [("colsum_q", 1040), ("qb", 1040), ("colsum_kv", 2 * D),
                  ("kvb", 2 * D), ("colsum_M", D), ("ffn_bias", D),
                  ("colsum_wq", 1152), ("colsum_wk", D), ("colsum_wv", D),
                  ("colsum_wgq", LDIM), ("colsum_wgk", LDIM), ("colsum_wgv", LDIM)]:
        RO[nm] = _o[0]; _o[0] += n
    assert _o[0] <= 16384

    sb = {}
    def salloc(nm, shp, dt_):
        sb[nm] = nc.alloc_sbuf_tensor(nm, shp, dt_)
        return sb[nm]

    salloc("ident", [128, 128], BF16)
    salloc("identf", [128, 128], F32)
    salloc("lmask", [128, 384], BF16)
    salloc("wrep", [128, HD], F32)
    salloc("brep", [128, HD], F32)
    salloc("cbias", [128, 1], F32)
    salloc("rhbrep", [128, 1], F32)
    salloc("onesbf", [2, 128], BF16)
    salloc("onecolbf", [128, 1], BF16)
    salloc("onecolr", [128, 1], F32R)
    salloc("epscol", [128, 1], F32)
    salloc("colst", [128, 64], F32)
    salloc("glwt", [128, 2 * NH], BF16)
    salloc("t0", [128, 4], BF16)
    salloc("onesf32", [33, 64], F32)
    salloc("onesb32", [33, 64], BF16)
    CO = {"R2": 0, "r": 16, "rad": 32, "tmp": 48}
    salloc("rows", [1, 5 * S], BF16)
    RB = {"Rh": 0, "Rl": S, "m": 2 * S, "mR": 3 * S, "invR2": 4 * S}

    X = [salloc(f"X{c}", [128, S], F32) for c in range(NC)]
    AAB = 64 * 1024
    WAB = 56 * 1024
    aarena = nc.alloc_sbuf_tensor("aarena", [128, AAB], mybir.dt.uint8)
    warena = nc.alloc_sbuf_tensor("warena", [128, WAB], mybir.dt.uint8)
    AADDR = nc.lookup_mloc(aarena).addr
    WADDR = nc.lookup_mloc(warena).addr

    _vc = [0]
    _vcache = {}
    def aview(off, shp, dt_, nm="v"):
        key = ("a", off, tuple(shp), dt_)
        if key in _vcache:
            return _vcache[key]
        _vc[0] += 1
        esz = {F32: 4, F32R: 4, BF16: 2, mybir.dt.uint8: 1}[dt_]
        need = int(np.prod(shp[1:])) * esz
        assert off + need <= AAB, f"aview {nm} overflow {off}+{need}"
        t = nc.alloc_sbuf_tensor_at(f"a{_vc[0]}_{nm}", shp, dt_, offset=AADDR + off)
        _vcache[key] = t
        return t

    def wview(off, shp, dt_, nm="v"):
        key = ("w", off, tuple(shp), dt_)
        if key in _vcache:
            return _vcache[key]
        _vc[0] += 1
        esz = {F32: 4, F32R: 4, BF16: 2}[dt_]
        need = int(np.prod(shp[1:])) * esz
        assert off + need <= WAB, f"wview {nm} overflow {off}+{need}"
        t = nc.alloc_sbuf_tensor_at(f"w{_vc[0]}_{nm}", shp, dt_, offset=WADDR + off)
        _vcache[key] = t
        return t

    p = Prog(nc)
    est = ExitStack()
    sems = {}
    for smn in ["q0", "q1", "q2", "q3", "q4", "q5", "g0", "g1", "g2", "g3",
                "pe", "dve", "act", "pool", "free",
                "b_tensor", "b_vector", "b_scalar", "b_gpsimd"]:
        sems[smn] = est.enter_context(nc.semaphore(smn))
    block = est.enter_context(nc.Block())

    def mm(out_ap, lhsT_ap, rhs_ap, start, stop, wait=(), inc=None, tp=None):
        return p.op("tensor",
                    lambda t, o=out_ap, l=lhsT_ap, r=rhs_ap:
                    t.matmul(o, l, r, start=start, stop=stop, tile_position=tp),
                    wait=wait, inc=inc)

    def tr(out_ap, in_ap, wait=(), inc=None):
        np_ = in_ap.shape[0]
        base = sb["ident"] if in_ap.dtype != F32 else sb["identf"]
        idt = base[0:np_, 0:np_]
        return p.op("tensor", lambda t, o=out_ap, i=in_ap, idt=idt: t.transpose(o, i, idt),
                    wait=wait, inc=inc)

    _psn = [0]
    class PsumScope:
        def __init__(self):
            self.guards = []
            self.tensors = []
        def get(self, shape, dt_=F32):
            _psn[0] += 1
            g = nc.psum_tensor(f"ps{_psn[0]}", shape, dt_)
            t = g.__enter__()
            self.guards.append(g)
            self.tensors.append(t)
            return t
        def close(self):
            for g in reversed(self.guards):
                g.__exit__(None, None, None)
            self.guards = []

    def load_x(s):
        pss = PsumScope()
        pst = [pss.get([128, 128]) for _ in range(4)]
        tmb = [aview(0, [128, D], F32, "ldtm0"), aview(4096, [128, D], F32, "ldtm1")]
        ld_done = []
        for t in range(NT):
            bi = t % 2
            w = [("dve", ld_done[t - 2])] if t >= 2 else []
            v_l = p.dma("sync", lambda e, t=t, bi=bi: e.dma_start(
                out=tmb[bi][:], in_=xin[s].rearrange("(t p) d -> t p d", p=128)[t]),
                wait=w)
            for c in range(NC):
                pk = pst[(t * NC + c) % 4]
                wv = [v_l]
                if t * NC + c >= 4:
                    wv.append(("dve", p.val("dve") - 3))
                v_t = tr(pk[:], tmb[bi][:, 128 * c:128 * (c + 1)],
                         wait=wv, inc="pe")
                v_e = p.op("vector", lambda v, c=c, t=t, pk=pk: v.tensor_copy(
                    X[c][:, 128 * t:128 * (t + 1)], pk[:]),
                    wait=[("pe", v_t)], inc="dve")
            ld_done.append(p.val("dve"))
        for c in range(NC):
            p.dma("gpsimd", lambda e, c=c: e.dma_start(
                out=st["st_xbf"][c], in_=X[c][:]),
                wait=[("dve", p.val("dve"))] if c == 0 else ())
        p.barrier()
        pss.close()

    def store_y(s):
        p.barrier()
        pss = PsumScope()
        pst = [pss.get([128, 128]) for _ in range(4)]
        tmb = [aview(0, [128, D], F32, "sttm0"), aview(4096, [128, D], F32, "sttm1")]
        st_dx = []
        for t in range(NT):
            bi = t % 2
            for c in range(NC):
                pk = pst[(t * NC + c) % 4]
                wv = []
                if t >= 2:
                    wv.append(st_dx[t - 2])
                if t * NC + c >= 4:
                    wv.append(("dve", p.val("dve") - 3))
                v_t = tr(pk[:], X[c][:, 128 * t:128 * (t + 1)],
                         wait=wv, inc="pe")
                p.op("vector", lambda v, c=c, bi=bi, pk=pk: v.tensor_copy(
                    tmb[bi][:, 128 * c:128 * (c + 1)], pk[:]),
                    wait=[("pe", v_t)], inc="dve")
            v_s = p.dma("sync", lambda e, t=t, bi=bi: e.dma_start(
                out=yout[s].rearrange("(t p) d -> t p d", p=128)[t], in_=tmb[bi][:]),
                wait=[("dve", p.val("dve"))])
            st_dx.append(v_s)
        pss.close()


    def prep_consts():
        p.dma("sync", lambda e: e.dma_start(out=sb["ident"][:], in_=tabs["tab_ident"]))
        p.dma("sync", lambda e: e.dma_start(out=sb["identf"][:], in_=tabs["tab_identf"]))
        p.dma("sync", lambda e: e.dma_start(out=sb["lmask"][:], in_=tabs["tab_localmask"]))
        p.dma("sync", lambda e: e.dma_start(out=sb["t0"][:], in_=tabs["tab_t0"]))
        vqm = p.dmark()
        def vop(fn):
            vv = p.op("vector", fn, inc="dve")
            p.wait("vector", [("dve", vv)])
            return vv
        vop(lambda v_: v_.memset(sb["colst"][:], 0.0))
        vop(lambda v_: v_.memset(sb["onesf32"][:], 1.0))
        vop(lambda v_: v_.memset(sb["onesb32"][:], 1.0))
        vop(lambda v_: v_.memset(sb["onesbf"][:], 0.0))
        vop(lambda v_: v_.memset(sb["onesbf"][0:1, :], 1.0))
        vop(lambda v_: v_.memset(sb["onecolbf"][:], 1.0))
        vop(lambda v_: v_.memset(sb["epscol"][:], 1.0))
        vop(lambda v_: v_.tensor_copy(sb["onecolr"][:], sb["epscol"][:]))
        vop(lambda v_: v_.memset(sb["epscol"][:], EPS))
        p.op("vector", lambda v_: v_.engine_nop(), inc="dve")
        p.wait("tensor", vqm)
        p.wait("vector", vqm)
        p.barrier()

    def prep_ffn():
        pss = PsumScope()
        ps_a = pss.get([128, 512])
        ps_b = pss.get([128, 512])
        ps_row = pss.get([1, 1024])
        cosr = [wview(c * 4096, [128, D], F32R, "cos") for c in range(NC)]
        sinr = [aview(c * 4096, [128, D], F32R, "sin") for c in range(NC)]
        UTR = [aview(32768 + c * 2048, [128, 512], F32R, "utr") for c in range(NC)]
        UTI = [aview(49152 + c * 2048, [128, 512], F32R, "uti") for c in range(NC)]
        wrsl = [wview(32768 + c * 512, [128, 128], F32R, "wrsl") for c in range(NC)]
        wisl = [wview(36864 + c * 512, [128, 128], F32R, "wisl") for c in range(NC)]
        ldb = wview(40960, [128, D], F32, "mld")
        ldb2 = wview(45056, [128, 128], F32, "mld2")
        evb = wview(45568, [128, 512], F32, "mev")
        evb2 = wview(47616, [128, 512], F32, "mev2")
        mst = wview(49664, [128, 512], F32R, "mst")
        rowbf = wview(51712, [2, 1024], BF16, "mrowbf")

        for nm, dstl in (("tab_fcos", cosr), ("tab_fsin", sinr)):
            for c in range(NC):
                v_l = p.dma("sync", lambda e, c=c, nm=nm: e.dma_start(
                    out=ldb[:], in_=tabs[nm].rearrange("(c p) o -> c p o", p=128)[c]),
                    wait=[("dve", p.val("dve"))])
                p.op("vector", lambda v, c=c, dstl=dstl: v.tensor_copy(dstl[c][:], ldb[:]),
                     wait=[v_l], inc="dve")

        if "mcs" in DBG:
            p.barrier()
            pss.close()
            return
        for h in range(2 if "ms1" not in DBG else 1):
            dsl = slice(512 * h, 512 * (h + 1))
            for jc in range(NC):
                for kc in range(NC):
                    v_l = p.dma("sync", lambda e, jc=jc, kc=kc: e.dma_start(
                        out=ldb2[:], in_=win["ffn_wr"].rearrange(
                            "(kc p) o -> kc p o", p=128)[kc][:, 128 * jc:128 * (jc + 1)]),
                        wait=[("pe", p.val("pe")), ("dve", p.val("dve"))])
                    p.op("vector", lambda v, kc=kc: v.tensor_copy(wrsl[kc][:], ldb2[:]),
                         wait=[v_l], inc="dve")
                    v_l = p.dma("sync", lambda e, jc=jc, kc=kc: e.dma_start(
                        out=ldb2[:], in_=win["ffn_wi"].rearrange(
                            "(kc p) o -> kc p o", p=128)[kc][:, 128 * jc:128 * (jc + 1)]),
                        wait=[("dve", p.val("dve"))])
                    p.op("vector", lambda v, kc=kc: v.tensor_copy(wisl[kc][:], ldb2[:]),
                         wait=[v_l], inc="dve")
                v_w = p.val("dve")
                wv = [("dve", v_w)]
                for kc in range(NC):
                    mm(ps_a[:], wrsl[kc][:], cosr[kc][:, dsl], kc == 0, False, wait=wv)
                    wv = ()
                v1 = None
                for kc in range(NC):
                    v1 = mm(ps_a[:], wisl[kc][:], sinr[kc][:, dsl], False, kc == NC - 1,
                            inc="pe" if kc == NC - 1 else None)
                v_e1 = p.op("vector", lambda v, jc=jc: v.tensor_copy(
                    UTR[jc][:], ps_a[:]), wait=[("pe", v1)], inc="dve")
                v2 = None
                for kc in range(NC):
                    v2 = mm(ps_b[:], wisl[kc][:], cosr[kc][:, dsl], kc == 0, kc == NC - 1,
                            inc="pe" if kc == NC - 1 else None)
                v_e2 = p.op("vector", lambda v: v.tensor_copy(evb[:], ps_b[:]),
                            wait=[("pe", v2)], inc="dve")
                wv = [("dve", v_e1)]
                v3 = None
                for kc in range(NC):
                    v3 = mm(ps_a[:], wrsl[kc][:], sinr[kc][:, dsl], kc == 0, kc == NC - 1,
                            wait=wv, inc="pe" if kc == NC - 1 else None)
                    wv = ()
                v_u = p.op("vector", lambda v: v.scalar_tensor_tensor(
                    evb2[:], ps_a[:], -1.0, evb[:], ALU.mult, ALU.add),
                    wait=[("pe", v3), ("dve", v_e2)], inc="dve")
                p.op("vector", lambda v, jc=jc: v.tensor_copy(UTI[jc][:], evb2[:]),
                     wait=[("dve", v_u)], inc="dve")
            for dci in range(4 if "ms1" not in DBG else 0):
                dc = 4 * h + dci
                for og in range(2):
                    osl = slice(512 * og, 512 * (og + 1))
                    wv = [("dve", p.val("dve"))]
                    v1 = None
                    for jc in range(NC):
                        v1 = mm(ps_a[:], UTR[jc][:, 128 * dci:128 * (dci + 1)],
                                cosr[jc][:, osl], jc == 0, jc == NC - 1, wait=wv,
                                inc="pe" if jc == NC - 1 else None)
                        wv = ()
                    v_e = p.op("vector", lambda v: v.tensor_scalar(
                        evb[:], ps_a[:], 1.0 / D, None, ALU.mult),
                        wait=[("pe", v1)], inc="dve")
                    v2 = None
                    for jc in range(NC):
                        v2 = mm(ps_b[:], UTI[jc][:, 128 * dci:128 * (dci + 1)],
                                sinr[jc][:, osl], jc == 0, jc == NC - 1,
                                inc="pe" if jc == NC - 1 else None)
                    v_m0 = p.op("vector", lambda v: v.scalar_tensor_tensor(
                        evb2[:], ps_b[:], -1.0 / D, evb[:], ALU.mult, ALU.add),
                        wait=[("pe", v2), ("dve", v_e)], inc="dve")
                    v_m = p.op("vector", lambda v: v.tensor_copy(mst[:], evb2[:]),
                               wait=[("dve", v_m0)], inc="dve")
                    v_st = p.dma("sync", lambda e, dc=dc, osl=osl: e.dma_start(
                        out=st["st_M"][dc][:, osl], in_=mst[:]), wait=[("dve", v_m)])
                    p.wait("vector", [v_st])
        if "ms1" not in DBG:
            mld3 = wview(0, [128, D], F32R, "csmld")
            v_cs = None
            for dc in range(NC):
                v_l = p.dma("sync", lambda e, dc=dc: e.dma_start(
                    out=mld3[:], in_=st["st_M"][dc]),
                    wait=[("pe", v_cs)] if v_cs else [("pe", p.val("pe"))])
                for og in range(2):
                    osl = slice(512 * og, 512 * (og + 1))
                    v_cs = mm(ps_row[0:1, osl], sb["onecolr"][:], mld3[:, osl],
                              dc == 0, dc == NC - 1, wait=[v_l],
                              inc="pe" if og == 1 else None)
            p.wait("vector", [("pe", p.val("pe"))])
            v_n = p.op("vector", lambda v: v.tensor_scalar(
                rowbf[0:1, :], ps_row[0:1, :], -1.0 / 128.0, None, ALU.mult), inc="dve")
            p.wait("vector", [("dve", v_n)])
            v_fin = p.dma("sync", lambda e: e.dma_start(
                out=st["st_rows"][:, RO["colsum_M"]:RO["colsum_M"] + D],
                in_=rowbf[0:1, :]), wait=[("dve", v_n)])
            p.wait("gpsimd", [v_fin])
            if "dump" in DBG:
                p.dma("sync", lambda e: e.dma_start(out=dbg["csprep"],
                                                    in_=rowbf[0:1, :]))
        p.wait("sync", p.dmark())
        p.wait("gpsimd", p.dmark())
        p.barrier()
        pss.close()

    AO_RREP = 57344

    def ln_stats(s, attn_a_mode, rrep_flag=None):
        if rrep_flag is None:
            rrep_flag = not attn_a_mode
        """Compute col stats (colst) + bf16 rows (sb.rows) + RREP for sublayer."""
        pss = PsumScope()
        ps_s1 = pss.get([1, S])
        ps_s2 = pss.get([1, S])
        xbf = [aview(0, [128, 512], BF16, "stx0"), aview(1024, [128, 512], BF16, "stx1")]
        x2 = [aview(2048, [128, 512], BF16, "stq0"), aview(3072, [128, 512], BF16, "stq1")]
        rowsc = aview(4096, [1, S], F32, "strow")
        rowsc2 = aview(12288, [1, S], F32, "strow2")
        mcol = aview(20480, [128, 16], F32, "mcol")
        e2col = aview(20544, [128, 16], F32, "e2col")
        tcol = aview(20608, [128, 16], F32, "tcol")
        t2col = aview(20672, [128, 16], F32, "t2col")
        tbf = aview(20736, [128, 16], BF16, "tbfcol")
        mR2col = aview(20800, [128, 16], F32, "mR2col")
        invR2col = aview(20864, [128, 16], F32, "invR2col")
        mRcol = aview(20928, [128, 16], F32, "mRcol")
        rlcol = aview(20992, [128, 16], F32, "rlcol")
        i = 0
        prev_pe = {}
        for tg in range(4):
            tsl = slice(512 * tg, 512 * (tg + 1))
            for c in range(NC):
                bi = i % 2
                wv = []
                if i >= 2 and (i - 2) in prev_pe:
                    wv.append(("pe", prev_pe[i - 2]))
                v_cast = p.op("vector", lambda v, c=c, tsl=tsl, bi=bi: v.tensor_copy(
                    xbf[bi][:], X[c][:, tsl]), wait=wv, inc="dve")
                v_sq = p.op("gpsimd", lambda g, bi=bi: g.tensor_mul(
                    x2[bi][:], xbf[bi][:], xbf[bi][:]), wait=[("dve", v_cast)],
                    inc="pool")
                mm(ps_s1[0:1, tsl], sb["onecolbf"][:], xbf[bi][:], c == 0, c == NC - 1,
                   wait=[("dve", v_cast)])
                v_pe = mm(ps_s2[0:1, tsl], sb["onecolbf"][:], x2[bi][:],
                          c == 0, c == NC - 1, wait=[("pool", v_sq)], inc="pe")
                prev_pe[i] = v_pe
                i += 1
        v_done = p.val("pe")
        v_r1 = p.op("vector", lambda v: v.tensor_copy(rowsc[0:1, :], ps_s1[0:1, :]),
                    wait=[("pe", v_done)], inc="dve")
        v_r2 = p.op("vector", lambda v: v.tensor_copy(rowsc2[0:1, :], ps_s2[0:1, :]),
                    inc="dve")
        pss.close()
        pss = PsumScope()
        colps = pss.get([128, 16])
        colps2 = pss.get([128, 16])
        rowps = pss.get([1, S])
        v_t1 = None
        for t in range(NT):
            v_t1 = tr(colps[:, t:t + 1], rowsc[0:1, 128 * t:128 * (t + 1)],
                      wait=[("dve", v_r1)] if t == 0 else (),
                      inc="pe" if t == NT - 1 else None)
        v_t2 = None
        for t in range(NT):
            v_t2 = tr(colps2[:, t:t + 1], rowsc2[0:1, 128 * t:128 * (t + 1)],
                      wait=[("dve", v_r2)] if t == 0 else (),
                      inc="pe" if t == NT - 1 else None)
        p.op("vector", lambda v: v.tensor_scalar(mcol[:], colps[:], 1.0 / D, None,
                                                 ALU.mult), wait=[("pe", v_t1)])
        v_c2 = p.op("vector", lambda v: v.tensor_scalar(e2col[:], colps2[:], 1.0 / D,
                                                        None, ALU.mult),
                    wait=[("pe", v_t2)], inc="dve")
        p.op("vector", lambda v: v.tensor_copy(sb["colst"][:, CO["tmp"]:CO["tmp"] + 16],
                                               mcol[:]))
        if "dump" in DBG:
            chm = aview(21184, [128, 16], F32, "chainm")
            che = aview(21248, [128, 16], F32, "chaine")
            p.op("vector", lambda v: v.tensor_copy(chm[:], mcol[:]))
            p.op("vector", lambda v: v.tensor_copy(che[:], e2col[:]))
        def vchain(fn):
            vv = p.op("vector", fn, inc="dve")
            p.wait("vector", [("dve", vv)])
            return vv
        vchain(lambda v: v.tensor_mul(tcol[:], mcol[:], mcol[:]))
        v_var = vchain(lambda v: v.tensor_sub(t2col[:], e2col[:], tcol[:]))
        v_ln = p.op("scalar", lambda s_: s_.activation(tcol[:], t2col[:], AF.Ln,
                                                       bias=sb["epscol"][:]),
                    wait=[("dve", v_var)], inc="act")
        p.wait("scalar", [("act", v_ln)])
        if "dump" in DBG:
            lndump = aview(21056, [128, 16], F32, "lndump")
            vardump = aview(21120, [128, 16], F32, "vardump")
            p.op("vector", lambda v: v.tensor_copy(vardump[:], t2col[:]),
                 wait=[("dve", v_var)])
            p.op("vector", lambda v: v.tensor_copy(lndump[:], tcol[:]),
                 wait=[("act", v_ln)])
        v_r = p.op("scalar", lambda s_: s_.activation(sb["colst"][:, CO["r"]:CO["r"] + 16],
                                                      tcol[:], AF.Exp, scale=-0.5),
                   inc="act")
        p.wait("vector", [("act", v_r)])
        rcol_ap = sb["colst"][:, CO["r"]:CO["r"] + 16]
        R2_ap = sb["colst"][:, CO["R2"]:CO["R2"] + 16]
        v_mR = vchain(lambda v: v.tensor_mul(mRcol[:], mcol[:], rcol_ap))
        kinds = [("m", mcol), ("mR", mRcol)]
        if attn_a_mode:
            vchain(lambda v: v.tensor_mul(tcol[:], rcol_ap, rcol_ap))
            v_q = vchain(lambda v: v.tensor_scalar(
                tcol[:], tcol[:], -EPS, 1.0 + EPS, ALU.mult, ALU.add))
            v_l2 = p.op("scalar", lambda s_: s_.activation(t2col[:], tcol[:], AF.Ln),
                        wait=[("dve", v_q)], inc="act")
            p.wait("scalar", [("act", v_l2)])
            v_r2 = p.op("scalar", lambda s_: s_.activation(tcol[:], t2col[:], AF.Exp,
                                                           scale=-0.5), inc="act")
            p.wait("vector", [("act", v_r2)])
            vchain(lambda v: v.tensor_mul(R2_ap, rcol_ap, tcol[:]))
            v_ir = vchain(lambda v: v.reciprocal(invR2col[:], R2_ap))
            kinds += [("invR2", invR2col)]
        else:
            vchain(lambda v: v.tensor_copy(tbf[:], rcol_ap))
            vchain(lambda v: v.tensor_copy(t2col[:], tbf[:]))
            v_rl = vchain(lambda v: v.tensor_sub(rlcol[:], rcol_ap, t2col[:]))
            kinds += [("Rh", t2col), ("Rl", rlcol)]
        v_pre = p.val("dve")
        for nm, colap in kinds:
            v_tt = None
            for t in range(NT):
                v_tt = tr(rowps[0:1, 128 * t:128 * (t + 1)], colap[:, t:t + 1],
                          wait=[("dve", v_pre)] if t == 0 else (),
                          inc="pe" if t == NT - 1 else None)
            v_ev = p.op("vector", lambda v, nm=nm: v.tensor_copy(
                sb["rows"][:, RB[nm]:RB[nm] + S], rowps[0:1, :]),
                wait=[("pe", v_tt)], inc="dve")
            p.wait("tensor", [("dve", v_ev)])
            if nm == "m":
                p.dma("sync", lambda e: e.dma_start(
                    out=st["st_mrow"], in_=sb["rows"][:, RB["m"]:RB["m"] + S]),
                    wait=[("dve", v_ev)])
                if "skipst64" in DBG:
                    continue
                sc64 = aview(21504, [1, S], BF16, "sc64row")
                v_s64 = p.vop(lambda v: v.tensor_scalar(
                    sc64[:], sb["rows"][:, RB["m"]:RB["m"] + S], 1.0 / 64.0, None,
                    ALU.mult))
                v_d64 = p.dma("sync", lambda e: e.dma_start(out=st["st_mrow64"],
                                                             in_=sc64[:]),
                              wait=[("dve", v_s64)])
            if nm == "invR2" and "skipst64" not in DBG:
                sc64b = aview(21504, [1, S], BF16, "sc64rowb")
                v_s64b = p.op("vector", lambda v: v.tensor_scalar(
                    sc64b[:], sb["rows"][:, RB["invR2"]:RB["invR2"] + S], 1.0 / 64.0,
                    None, ALU.mult), wait=[v_d64])
                v_d64b = p.dma("sync", lambda e: e.dma_start(out=st["st_irow64"],
                                                              in_=sc64b[:]),
                               wait=[("dve", v_s64b)])
        if rrep_flag:
            rrep = aview(AO_RREP, [128, S], F32, "rrep")
            v_rows = p.val("dve")
            for tg in range(4):
                tsl = slice(512 * tg, 512 * (tg + 1))
                mm(colps[:, 0:16] if False else rowps[0:1, 0:1], None, None, True, True) \
                    if False else None
                mm_ps = colps
            pss.close()
            pss = PsumScope()
            reps = pss.get([128, 512])
            for tg in range(4):
                tsl = slice(512 * tg, 512 * (tg + 1))
                mm(reps[:], sb["onesbf"][0:1, :], sb["rows"][:, RB["Rh"] + 512 * tg:
                                                            RB["Rh"] + 512 * (tg + 1)],
                   True, False, wait=[("dve", v_rows)] if tg == 0 else
                   [("dve", p.val("dve"))])
                v_mm = mm(reps[:], sb["onesbf"][0:1, :],
                          sb["rows"][:, RB["Rl"] + 512 * tg:RB["Rl"] + 512 * (tg + 1)],
                          False, True, inc="pe")
                p.op("vector", lambda v, tsl=tsl: v.tensor_copy(rrep[:, tsl], reps[:]),
                     wait=[("pe", v_mm)], inc="dve")
        p.wait("sync", p.dmark())
        p.wait("gpsimd", p.dmark())
        p.barrier()
        pss.close()

    def ffn_sub(s):
        ln_stats(s, attn_a_mode=False)
        if "dump" in DBG and s == 0:
            mcolv = aview(20480, [128, 16], F32, "mcoldump")
            e2colv = aview(20544, [128, 16], F32, "e2coldump")
            p.op("vector", lambda v: v.tensor_copy(sb["colst"][:, 32:48], mcolv[:]))
            p.op("vector", lambda v: v.tensor_copy(sb["colst"][:, 48:64], e2colv[:]))
            lndumpv = aview(21056, [128, 16], F32, "lndump2")
            vardumpv = aview(21120, [128, 16], F32, "vardump2")
            p.op("vector", lambda v: v.tensor_copy(sb["colst"][:, 32:48], lndumpv[:]))
            p.op("vector", lambda v: v.tensor_copy(sb["colst"][:, 48:64], vardumpv[:]), inc="dve")
            chainv = aview(21184, [128, 128], F32, "chaindump")
            p.dma("sync", lambda e: e.dma_start(out=dbg["chain"], in_=chainv[:, 0:64]))
            rrepv = aview(AO_RREP, [128, S], F32, "rrepdump")
            p.dma("sync", lambda e: e.dma_start(out=dbg["rows"], in_=sb["rows"][:]))
            p.dma("sync", lambda e: e.dma_start(out=dbg["rrep"], in_=rrepv[:]))
            p.dma("sync", lambda e: e.dma_start(out=dbg["colst"], in_=sb["colst"][:]))
            p.barrier()
        if "statonly" in DBG:
            return
        pss = PsumScope()
        po = [pss.get([128, 512]) for _ in range(2)]
        ps_rep = pss.get([128, 512])
        rrep = aview(AO_RREP, [128, S], F32, "rrepf")
        xr = [aview(c * 2048, [128, 512], F32R, "xr") for c in range(NC)]
        tbuf = [aview(16384, [128, 512], F32, "ft0"), aview(18432, [128, 512], F32, "ft1")]
        biascol = aview(20480, [128, 8], F32, "fbias")
        csMbf = aview(20544, [1, D], BF16, "csMbf")
        mrepr = aview(22592, [128, S], F32R, "mrepr")
        csMrep = aview(30784, [128, D], F32R, "csMrep")
        Mw = [wview(c * 4096, [128, D], F32R, "Mw") for c in range(NC)]
        v_bias = p.dma("gpsimd", lambda e: e.dma_start(
            out=biascol[:], in_=win["ffn_bias"].rearrange("(c p) -> p c", p=128),
            allow_slow_non_contiguous=True))
        v_csM = p.dma("sync", lambda e: e.dma_start(
            out=csMbf[:], in_=st["st_rows"][:, RO["colsum_M"]:RO["colsum_M"] + D]))
        vw = []
        for c in range(NC):
            vw.append(p.dma("sync", lambda e, c=c: e.dma_start(out=Mw[c][:],
                                                               in_=st["st_M"][c])))
        v_rows = p.op("vector", lambda v: v.engine_nop(), wait=[v_csM], inc="dve")
        v_rep1 = p.dma("gpsimd", lambda e: e.dma_start(
            out=csMrep[:], in_=st["st_rows"][0:1, RO["colsum_M"]:RO["colsum_M"] + D]
            .to_broadcast([128, D])))
        v_rep2 = p.dma("gpsimd", lambda e: e.dma_start(
            out=mrepr[:], in_=st["st_mrow"][0:1, :].to_broadcast([128, S])))
        p.wait("tensor", [v_rep1, v_rep2])
        if "dumpr1" in DBG:
            r1ev = aview(34880, [128, 512], F32, "r1ev")
            opev = aview(36928, [128, 512], F32, "opev")
            v_o1 = p.op("vector", lambda v: v.tensor_copy(opev[:, 0:256], mrepr[:, 0:256]),
                        wait=[v_rep2, v_rep1])
            v_o2 = p.op("vector", lambda v: v.tensor_copy(opev[:, 256:512], csMrep[:, 0:256]),
                        inc="dve")
            p.dma("sync", lambda e: e.dma_start(out=dbg["ops"], in_=opev[:]),
                  wait=[("dve", v_o2)])
            v_x = mm(ps_rep[:], csMrep[:, 0:128], mrepr[:, 0:512], True, True,
                     wait=[v_rep2, v_rep1], inc="pe")
            v_xe = p.op("vector", lambda v: v.tensor_copy(r1ev[:], ps_rep[:]),
                        wait=[("pe", v_x)], inc="dve")
            p.dma("sync", lambda e: e.dma_start(out=dbg["r1"], in_=r1ev[:]),
                  wait=[("dve", v_xe)])
        ep_done = {}
        for tg in range(4):
            tsl = slice(512 * tg, 512 * (tg + 1))
            wv = [("pe", p.val("pe"))] if tg > 0 else []
            v_cast = None
            for c in range(NC):
                v_cast = p.op("vector", lambda v, c=c, tsl=tsl: v.tensor_copy(
                    xr[c][:], X[c][:, tsl]), wait=wv, inc="dve")
                wv = []
            for oc in range(NC):
                pk = po[oc % 2]
                idx = tg * NC + oc
                wv = [("dve", v_cast)] + vw
                if idx >= 2:
                    wv.append(("dve", ep_done[idx - 2]))
                skipr1 = "nor1" in DBG
                for c in range(NC):
                    v_mm = mm(pk[:], Mw[c][:, 128 * oc:128 * (oc + 1)], xr[c][:],
                              c == 0, skipr1 and c == NC - 1, wait=wv if c == 0 else (),
                              inc="pe" if (skipr1 and c == NC - 1) else None)
                if not skipr1:
                    v_mm = mm(pk[:], csMrep[:, 128 * oc:128 * (oc + 1)], mrepr[:, tsl],
                              False, True, inc="pe")
                tb = tbuf[oc % 2]
                p.op("vector", lambda v, pk=pk, tb=tb, tsl=tsl: v.tensor_mul(
                    tb[:], pk[:], rrep[:, tsl]), wait=[("pe", v_mm), ("dve", v_rows)])
                v_ep = p.op("vector", lambda v, oc=oc, tb=tb, tsl=tsl:
                            v.scalar_tensor_tensor(X[oc][:, tsl], tb[:],
                                                   biascol[:, oc:oc + 1],
                                                   X[oc][:, tsl], ALU.add, ALU.add),
                            wait=[v_bias], inc="dve")
                ep_done[idx] = v_ep
        p.barrier()
        pss.close()

    def prep_atta():
        pss = PsumScope()
        ps_rep = pss.get([128, 512])
        A = AADDR
        lnw_row = aview(0, [2, 64], F32, "plnw")
        lnb_row = aview(512, [2, 64], F32, "plnb")
        lnw_bf = aview(1024, [2, 64], BF16, "plnwb")
        lnb_bf = aview(1280, [2, 64], BF16, "plnbb")
        scr = aview(1536, [2, 64], F32, "pscr")
        scr2 = aview(2048, [2, 64], F32, "pscr2")
        rhb_row = aview(2560, [2, 1], F32, "prhb")
        rhb_bf = aview(2816, [2, 1], BF16, "prhbb")
        cv_bf = aview(3072, [2, 1], BF16, "pcv")
        p.dma("sync", lambda e: e.dma_start(out=lnw_row[0:1, :], in_=win["aa_ln_w"].unsqueeze(0)))
        p.dma("sync", lambda e: e.dma_start(out=lnb_row[0:1, :], in_=win["aa_ln_b"].unsqueeze(0)))
        v_ln = p.dma("sync", lambda e: e.dma_start(out=rhb_row[0:1, :], in_=win["aa_rh_b"].unsqueeze(0)))
        p.wait("vector", p.dmark())
        p.vop(lambda v: v.tensor_copy(lnw_bf[0:1, :], lnw_row[0:1, :]))
        p.vop(lambda v: v.tensor_copy(lnb_bf[0:1, :], lnb_row[0:1, :]))
        p.vop(lambda v: v.tensor_copy(rhb_bf[0:1, :], rhb_row[0:1, :]))
        v1 = mm(ps_rep[:, 0:64], sb["onesbf"][0:1, :], lnw_bf[0:1, :], True, True,
                wait=[("dve", p.val("dve"))])
        v2 = mm(ps_rep[:, 64:128], sb["onesbf"][0:1, :], lnb_bf[0:1, :], True, True)
        v3 = mm(ps_rep[:, 128:129], sb["onesbf"][0:1, :], rhb_bf[0:1, :], True, True,
                inc="pe")
        p.wait("vector", [("pe", v3)])
        p.vop(lambda v: v.tensor_copy(sb["wrep"][:], ps_rep[:, 0:64]))
        p.vop(lambda v: v.tensor_copy(sb["brep"][:], ps_rep[:, 64:128]))
        p.vop(lambda v: v.tensor_copy(sb["rhbrep"][:], ps_rep[:, 128:129]))
        v1b = mm(ps_rep[:, 192:256], sb["onesbf"][0:1, :], lnw_bf[0:1, :], True, True)
        p.vop(lambda v: v.reduce_max(scr[0:1, 0:1].unsqueeze(2),
                                     lnw_row[0:1, :].unsqueeze(1), AX.X,
                                     apply_absolute_value=True))
        p.vop(lambda v: v.tensor_mul(scr2[0:1, :], lnb_row[0:1, :], lnb_row[0:1, :]))
        p.vop(lambda v: v.reduce_sum(scr[0:1, 1:2].unsqueeze(2),
                                     scr2[0:1, :].unsqueeze(1), AX.X))
        v_sq = p.op("scalar", lambda s_: s_.activation(scr[0:1, 1:2], scr[0:1, 1:2],
                                                       AF.Sqrt),
                    wait=[("dve", p.val("dve"))], inc="act")
        p.wait("vector", [("act", v_sq)])
        p.vop(lambda v: v.tensor_scalar(scr[0:1, 2:3], scr[0:1, 0:1], 8.0, None, ALU.mult))
        p.vop(lambda v: v.tensor_add(scr[0:1, 2:3], scr[0:1, 2:3], scr[0:1, 1:2]))
        p.vop(lambda v: v.tensor_mul(scr[0:1, 3:4], scr[0:1, 2:3], scr[0:1, 2:3]))
        p.vop(lambda v: v.tensor_scalar(scr[0:1, 3:4], scr[0:1, 3:4], -0.125, None, ALU.mult))
        p.vop(lambda v: v.tensor_copy(cv_bf[0:1, :], scr[0:1, 3:4]))
        v4 = mm(ps_rep[:, 129:130], sb["onesbf"][0:1, :], cv_bf[0:1, :], True, True,
                wait=[("dve", p.val("dve"))], inc="pe")
        p.wait("vector", [("pe", v4)])
        p.vop(lambda v: v.tensor_copy(sb["cbias"][:], ps_rep[:, 129:130]))
        qnw_c = aview(3584, [128, 8], F32, "pqnw")
        qnb_c = aview(3616, [128, 8], F32, "pqnb")
        kvnw_c = aview(3648, [128, 8], F32, "pkvnw")
        kvnb_c = aview(3680, [128, 8], F32, "pkvnb")
        qnb_bf = aview(3712, [128, 8], BF16, "pqnbb")
        kvnb_bf = aview(3744, [128, 8], BF16, "pkvnbb")
        for nm, dst in [("aa_qn_w", qnw_c), ("aa_qn_b", qnb_c),
                        ("aa_kvn_w", kvnw_c), ("aa_kvn_b", kvnb_c)]:
            p.dma("gpsimd", lambda e, nm=nm, dst=dst: e.dma_start(
                out=dst[:], in_=win[nm].rearrange("(c p) -> p c", p=128),
                allow_slow_non_contiguous=True))
        p.wait("vector", p.dmark())
        p.vop(lambda v: v.tensor_copy(qnb_bf[:], qnb_c[:]))
        p.vop(lambda v: v.tensor_copy(kvnb_bf[:], kvnb_c[:]))
        pss.close()

        pss = PsumScope()
        ps_bias = pss.get([1, 2 * D])
        ps_cs = pss.get([1, 2 * D])
        wbuf = wview(0, [128, 2048], F32, "pawbuf")
        wbf = wview(8192, [128, 2080], BF16, "pawbf")
        rowsc = wview(16384 if False else 12352, [2, 2048], F32, "parow")
        rowbf2 = wview(20544, [1, 4160], BF16, "parowbf")

        def fold(which):
            wname = "aa_q_w" if which == "q" else "aa_kv_w"
            wide = D if which == "q" else 2 * D
            scol = qnw_c if which == "q" else kvnw_c
            bcol = qnb_bf if which == "q" else kvnb_bf
            stname = "st_qw" if which == "q" else "st_kv"
            v_mmlast = None
            for c in range(NC):
                wv = [("pe", v_mmlast)] if v_mmlast else [("pe", p.val("pe"))]
                wv += [("dve", p.val("dve"))]
                v_l = p.dma("sync", lambda e, c=c, wname=wname: e.dma_start(
                    out=wbuf[:, 0:wide],
                    in_=win[wname].rearrange("(c p) o -> c p o", p=128)[c]), wait=wv)
                ltoks = [v_l]
                if which == "q":
                    ltoks.append(p.dma("sync", lambda e, c=c: e.dma_start(
                        out=wbuf[:, D:D + 1],
                        in_=win["aa_rh_w"].rearrange("(c p) o -> c p o", p=128)[c])))
                p.wait("vector", ltoks)
                v_raw = p.vop(lambda v: v.tensor_copy(wbf[:, 0:wide], wbuf[:, 0:wide]))
                ngg = wide // 512
                v_bias = None
                for gg in range(ngg):
                    v_bias = mm(ps_bias[0:1, 512 * gg:512 * (gg + 1)], bcol[:, c:c + 1],
                                wbf[:, 512 * gg:512 * (gg + 1)], c == 0, c == NC - 1,
                                wait=[("dve", v_raw)] if gg == 0 else (),
                                inc="pe" if gg == ngg - 1 else None)
                p.wait("vector", [("pe", v_bias)])
                v_sc = p.vop(lambda v, c=c, scol=scol: v.tensor_scalar(
                    wbf[:, 0:wide], wbuf[:, 0:wide], scol[:, c:c + 1], None, ALU.mult))
                if which == "q":
                    v_sc = p.vop(lambda v: v.tensor_copy(wbf[:, D:D + 1],
                                                         wbuf[:, D:D + 1]))
                    if c == 0:
                        v_sc = p.vop(lambda v: v.memset(wbf[:, D + 1:D + 16], 0.0))
                stwide = 1040 if which == "q" else 2048
                v_st = p.dma("sync", lambda e, c=c, stname=stname, stwide=stwide:
                             e.dma_start(out=st[stname][c][:, 0:stwide],
                                         in_=wbf[:, 0:stwide]), wait=[("dve", v_sc)])
                v_mmlast = v_bias
                p.wait("sync", [v_st])
            stw0 = 1040 if which == "q" else 2048
            v_cl = None
            for c in range(NC):
                wv = [("pe", v_cl)] if v_cl else [("pe", p.val("pe"))]
                v_l = p.dma("sync", lambda e, c=c, stname=stname, stw0=stw0: e.dma_start(
                    out=wbf[:, 0:stw0], in_=st[stname][c][:, 0:stw0]), wait=wv)
                nfull = stw0 // 512
                w_ = stw0 - nfull * 512
                for gg in range(nfull):
                    v_cl = mm(ps_cs[0:1, 512 * gg:512 * (gg + 1)], sb["onecolbf"][:],
                              wbf[:, 512 * gg:512 * (gg + 1)], c == 0, c == NC - 1,
                              wait=[v_l] if gg == 0 else (),
                              inc="pe" if (gg == nfull - 1 and not w_) else None)
                if w_:
                    v_cl = mm(ps_cs[0:1, stw0 - w_:stw0], sb["onecolbf"][:],
                              wbf[:, stw0 - w_:stw0], c == 0, c == NC - 1, inc="pe")
            p.wait("vector", [("pe", p.val("pe"))])
            bname = "aa_q_b" if which == "q" else "aa_kv_b"
            v_bl = p.dma("sync", lambda e, bname=bname, wide=wide: e.dma_start(
                out=rowsc[0:1, 0:wide], in_=win[bname].unsqueeze(0)))
            p.wait("vector", [v_bl, ("pe", p.val("pe"))])
            p.vop(lambda v, wide=wide: v.tensor_add(rowsc[0:1, 0:wide],
                                                    rowsc[0:1, 0:wide],
                                                    ps_bias[0:1, 0:wide]))
            stw = 1040 if which == "q" else 2048
            if which == "q":
                p.vop(lambda v: v.memset(rowsc[0:1, D:D + 16], 0.0))
            p.vop(lambda v, stw=stw: v.tensor_copy(rowbf2[0:1, 0:stw],
                                                   rowsc[0:1, 0:stw]))
            p.vop(lambda v, stw=stw: v.tensor_scalar(rowbf2[0:1, 2080:2080 + stw],
                                                     ps_cs[0:1, 0:stw], -1.0, None,
                                                     ALU.mult))
            roq = RO["qb"] if which == "q" else RO["kvb"]
            roc = RO["colsum_q"] if which == "q" else RO["colsum_kv"]
            v_s1 = p.dma("sync", lambda e, roq=roq, stw=stw: e.dma_start(
                out=st["st_rows"][:, roq:roq + stw], in_=rowbf2[0:1, 0:stw]),
                wait=[("dve", p.val("dve"))])
            v_s2 = p.dma("sync", lambda e, roc=roc, stw=stw: e.dma_start(
                out=st["st_rows"][:, roc:roc + stw], in_=rowbf2[0:1, 2080:2080 + stw]))
            p.wait("gpsimd", [v_s1, v_s2])

        fold("q")
        p.barrier()
        fold("kv")
        p.barrier()
        if "dumpst" in DBG:
            bb = wview(24576, [128, 2048], BF16, "dumpb")
            v_l = p.dma("sync", lambda e: e.dma_start(out=bb[:], in_=st["st_kv"][0]))
            t1 = p.dma("sync", lambda e: e.dma_start(out=dbg["stkv"], in_=bb[:]),
                       wait=[v_l])
            v_l2 = p.dma("sync", lambda e: e.dma_start(out=bb[:, 0:1040], in_=st["st_qw"][0]),
                         wait=[t1])
            p.dma("sync", lambda e: e.dma_start(out=dbg["stqw"], in_=bb[:, 0:1040]),
                  wait=[v_l2])
            p.dma("sync", lambda e: e.dma_start(out=dbg["strows"], in_=st["st_rows"]))
            p.barrier()
        v_mmlast = None
        for c in range(NC):
            wv = [("dve", p.val("dve"))]
            v_l = p.dma("sync", lambda e, c=c: e.dma_start(
                out=wbuf[:, 0:D],
                in_=win["aa_out_w"].rearrange("(c p) o -> c p o", p=128)[c]), wait=wv)
            p.wait("vector", [v_l])
            v_c = p.vop(lambda v: v.tensor_copy(wbf[:, 0:D], wbuf[:, 0:D]))
            v_st = p.dma("sync", lambda e, c=c: e.dma_start(out=st["st_outw"][c],
                                                            in_=wbf[:, 0:D]),
                         wait=[("dve", v_c)])
            p.wait("vector", [v_st])
        p.wait("sync", p.dmark())
        p.wait("gpsimd", p.dmark())
        p.barrier()
        pss.close()

    AO_TAB = 0
    AO_QNF = 12288
    AO_KNF = 20480
    AO_VBF = 28672
    AO_ABF = 36992
    AO_PBF = 45184
    AO_Q32 = 49280
    AO_K32 = 50304
    AO_SCR = 51328
    AO_ZRB = 55424
    AO_COLS = 59520
    AO_XBT = 59776
    AO_ROWQ = 60416

    def atta_sub(s):
        ln_stats(s, attn_a_mode=True)
        for nm, off in [("tab_c1", 0), ("tab_c2", 4096), ("tab_cs", 8192)]:
            tv = aview(off, [128, NT * HD], F32, nm + "v")
            p.dma("sync", lambda e, nm=nm, tv=tv: e.dma_start(
                out=tv[:].rearrange("p (t d) -> p t d", d=HD),
                in_=tabs[nm].rearrange("(t p) d -> p t d", p=128)))
        c1t = aview(0, [128, NT * HD], F32, "c1t")
        c2t = aview(4096, [128, NT * HD], F32, "c2t")
        cst = aview(8192, [128, NT * HD], F32, "cst")
        vtm = p.dmark()
        p.wait("vector", vtm)
        p.wait("gpsimd", vtm)

        gtoks = {0: atta_loads(0, 0), 1: atta_loads(1, 0)}
        for g in range(NG):
            atta_group(s, g, gtoks[g])
            if g + 2 < NG:
                gtoks[g + 2] = atta_loads(g + 2, p.val("pe"))

    def atta_loads(g, war_pe):
        WS = (g % 2) * 24576
        wqkv = [wview(WS + c * 1568, [128, 784], BF16, "wqkv") for c in range(NC)]
        outwg = [wview(WS + 12800 + i * 2048, [128, 1024], BF16, "outwg")
                 for i in range(2)]
        mi64 = wview(49152, [128, S], BF16, "mi64")
        csqkv = wview(WS + 16896, [128, 784], BF16, "csqkv")
        outbc = wview(WS + 21504, [128, 8], F32, "outbc")
        toks = []
        wv = [("pe", war_pe)] if war_pe else []
        for c in range(NC):
            toks.append(p.dma("sync", lambda e, c=c: e.dma_start(
                out=wqkv[c][:, 0:256], in_=st["st_qw"][c][:, 256 * g:256 * (g + 1)]),
                wait=wv if c == 0 else ()))
            toks.append(p.dma("sync", lambda e, c=c: e.dma_start(
                out=wqkv[c][:, 256:272], in_=st["st_qw"][c][:, 1024:1040])))
            toks.append(p.dma("sync", lambda e, c=c: e.dma_start(
                out=wqkv[c][:, 272:528],
                in_=st["st_kv"][c][:, 256 * g:256 * (g + 1)])))
            toks.append(p.dma("sync", lambda e, c=c: e.dma_start(
                out=wqkv[c][:, 528:784],
                in_=st["st_kv"][c][:, 1024 + 256 * g:1024 + 256 * (g + 1)])))
        for i in range(2):
            toks.append(p.dma("sync", lambda e, i=i: e.dma_start(
                out=outwg[i][:], in_=st["st_outw"][2 * g + i])))
        if g == 0 and "skipl" not in DBG:
            toks.append(p.dma("gpsimd", lambda e: e.dma_start(
                out=mi64[0:64, :], in_=st["st_mrow64"][0:1, :].to_broadcast([64, S])),
                wait=wv))
            toks.append(p.dma("gpsimd", lambda e: e.dma_start(
                out=mi64[64:128, :],
                in_=st["st_irow64"][0:1, :].to_broadcast([64, S]))))
        for dst_off, ro, off, n in ([
            (0, "colsum_q", 256 * g, 256),
            (1, "qb", 256 * g, 256),
            (2, "colsum_q", 1024, 16),
            (3, "qb", 1024, 16),
            (4, "colsum_kv", 256 * g, 256),
            (5, "kvb", 256 * g, 256),
            (6, "colsum_kv", 1024 + 256 * g, 256),
            (7, "kvb", 1024 + 256 * g, 256),
        ]):
            half = 64 * (dst_off % 2)
            col0 = [0, 0, 256, 256, 272, 272, 528, 528][dst_off]
            dst = csqkv[half:half + 64, col0:col0 + n]
            toks.append(p.dma("gpsimd", lambda e, dst=dst, ro=ro, off=off, n=n:
                              e.dma_start(out=dst, in_=st["st_rows"]
                                          [0:1, RO[ro] + off:RO[ro] + off + n]
                                          .to_broadcast([64, n])),
                              wait=wv if dst_off == 0 and g != 0 else ()))
        if g == 0 and "skipl" not in DBG:
            toks.append(p.dma("gpsimd", lambda e: e.dma_start(
                out=outbc[:], in_=win["aa_out_b"].rearrange("(c p) -> p c", p=128),
                allow_slow_non_contiguous=True)))
        return toks

    def atta_group(s, g, ltoks):
        c1t = aview(0, [128, NT * HD], F32, "gc1")
        c2t = aview(4096, [128, NT * HD], F32, "gc2")
        cst = aview(8192, [128, NT * HD], F32, "gcs")
        qnf = [aview(AO_QNF + i * 4096, [128, S], BF16, "qnf") for i in range(2)]
        knf = [aview(AO_KNF + i * 4096, [128, S], BF16, "knf") for i in range(2)]
        vbuf = aview(AO_VBF, [128, NT * 260], BF16, "vbuf")
        abf = [aview(AO_ABF + i * 4096, [128, S], BF16, "abf") for i in range(2)]
        pbf = [aview(AO_PBF + i * 2048, [128, 1024], BF16, "pbf") for i in range(2)]
        q32 = aview(AO_Q32, [128, 256], F32, "q32")
        k32 = aview(AO_K32, [128, 256], F32, "k32")
        scr = [aview(AO_SCR + i * 1024, [128, 256], F32, "scrA") for i in range(4)]
        zrb = aview(AO_ZRB, [64, 1024], F32, "zrb")
        cols = [aview(AO_COLS + i * 32, [128, 4], F32, "colA") for i in range(12)]
        xbt = [aview(AO_XBT + i * 256, [128, 128], BF16, "xbt") for i in range(2)]
        zrow = aview(AO_ROWQ, [1, 1024], F32, "zrow")
        WS = (g % 2) * 24576
        zbf = wview(WS + 22528, [1, 1024], BF16, "zbf")
        wqkv = [wview(WS + c * 1568, [128, 784], BF16, "wqkv") for c in range(NC)]
        outwg = [wview(WS + 12800 + i * 2048, [128, 1024], BF16, "outwg")
                 for i in range(2)]
        mi64 = wview(49152, [128, S], BF16, "mi64")
        csqkv = wview(WS + 16896, [128, 784], BF16, "csqkv")
        outbc = wview(WS + 21504, [128, 8], F32, "outbc")

        gml = list(ltoks) + [("dve", p.val("dve"))]

        p.wait("tensor", gml)
        p.wait("vector", gml)

        pss = PsumScope()
        ps_p2 = [pss.get([128, 1024]) for _ in range(2)]
        ps_tr = [pss.get([128, 128], BF16) for _ in range(2)]

        p.vop(lambda v: v.memset(
            vbuf[:].rearrange("p (t h c) -> p t h c", h=4, c=65)[:, :, :, 64:65], 1.0))

        rcol_all = sb["colst"][:, CO["r"]:CO["r"] + 16]
        R2_all = sb["colst"][:, CO["R2"]:CO["R2"] + 16]
        rad_all = sb["colst"][:, CO["rad"]:CO["rad"] + 16]

        xsl = [aview(AO_ABF + sl * 2048, [128, 1024], BF16, "xslab")
               for sl in range(2)]

        def xb_load(t, wv):
            return p.dma("sync", lambda e, t=t: e.dma_start(
                out=xsl[t % 2][:].rearrange("p (c u) -> p c u", u=128),
                in_=st["st_xbf"][:, :, 128 * t:128 * (t + 1)]
                .rearrange("c p u -> p c u")), wait=wv)

        xtok = {0: xb_load(0, ()), 1: xb_load(1, ())}
        vev_done = {}
        tr_done = {}
        pe_done = {}
        for t in range(NT):
            bi = t % 2
            psp = ps_p2[bi]
            ps_q = psp[:, 0:512]
            ps_k = psp[:, 272:784]
            ps_v = psp[:, 528:784]
            tsl = slice(128 * t, 128 * (t + 1))
            lsl = mi64[:, 128 * t:128 * (t + 1)]
            wv0 = [xtok[t]]
            if t - 2 in vev_done:
                wv0.append(("dve", vev_done[t - 2]))
            for c in range(NC):
                mm(psp[:, 0:512], xsl[bi][:, 128 * c:128 * (c + 1)],
                   wqkv[c][:, 0:512], c == 0, False, wait=wv0 if c == 0 else ())
                mm(psp[:, 512:784], xsl[bi][:, 128 * c:128 * (c + 1)],
                   wqkv[c][:, 512:784], c == 0, False)
            mm(psp[:, 0:512], lsl, csqkv[:, 0:512], False, True)
            v_proj = mm(psp[:, 512:784], lsl, csqkv[:, 512:784], False, True,
                        inc="pe")
            pe_done[t] = v_proj
            if t + 2 < NT:
                xtok[t + 2] = xb_load(t + 2, [("pe", v_proj)])
            p.wait("vector", [("pe", v_proj)])
            qk32 = aview(AO_Q32, [128, 512], F32, "qk32")
            p.vop(lambda v, t=t, psp=psp, qk32=qk32: v.tensor_scalar(
                qk32[:, 0:256], psp[:, 0:256], R2_all[:, t:t + 1], None, ALU.mult))
            p.vop(lambda v, t=t, psp=psp, qk32=qk32: v.tensor_scalar(
                qk32[:, 256:512], psp[:, 272:528], R2_all[:, t:t + 1], None,
                ALU.mult))
            if g == 0:
                p.vop(lambda v, t=t, psp=psp: v.tensor_scalar(
                    cols[8][:, 0:1], psp[:, 256:257], rcol_all[:, t:t + 1],
                    sb["rhbrep"][:], ALU.mult, ALU.add))
                p.vop(lambda v: v.tensor_scalar(cols[8][:, 0:1], cols[8][:, 0:1],
                                                0.01, 0.5, ALU.mult, ALU.max))
                p.vop(lambda v, t=t: v.tensor_scalar(
                    rad_all[:, t:t + 1], cols[8][:, 0:1], 2.0, None, ALU.min))
            v_vev = p.vop(lambda v, t=t, psp=psp: v.tensor_scalar(
                vbuf[:].rearrange("p (tt h c) -> p tt h c", h=4, c=65)[:, t, :, 0:64],
                psp[:, 528:784].rearrange("p (h d) -> p h d", d=64),
                R2_all[:, t:t + 1], None, ALU.mult))
            vev_done[t] = v_vev
            if t - 1 in tr_done:
                p.wait("vector", [("pe", tr_done[t - 1])])
            scr4 = [aview(AO_SCR + i_ * 2048, [128, 512], F32, "scr4")
                    for i_ in range(4)]
            cW = [aview(AO_COLS + i_ * 32, [128, 8], F32, "colW")
                  for i_ in range(12)]
            csv = cst[:].rearrange("p (tt d) -> p tt d", d=HD)[:, t, :] \
                .unsqueeze(1).to_broadcast([128, 8, HD])
            c1v = c1t[:].rearrange("p (tt d) -> p tt d", d=HD)[:, t, :] \
                .unsqueeze(1).to_broadcast([128, 8, HD])
            c2v = c2t[:].rearrange("p (tt d) -> p tt d", d=HD)[:, t, :] \
                .unsqueeze(1).to_broadcast([128, 8, HD])
            s3 = qk32[:].rearrange("p (h d) -> p h d", d=HD)
            p.vop(lambda v, s3=s3, csv=csv: v.tensor_tensor(
                scr4[0][:].rearrange("p (h d) -> p h d", d=HD), s3, csv, ALU.mult))
            p.vop(lambda v: v.reduce_sum(cW[0][:].unsqueeze(2),
                                         scr4[0][:].rearrange("p (h d) -> p h d",
                                                              d=HD), AX.X))
            p.vop(lambda v, s3=s3: v.tensor_tensor(
                scr4[1][:].rearrange("p (h d) -> p h d", d=HD), s3, s3, ALU.mult))
            p.vop(lambda v: v.reduce_sum(cW[1][:].unsqueeze(2),
                                         scr4[1][:].rearrange("p (h d) -> p h d",
                                                              d=HD), AX.X))
            p.vop(lambda v, t=t: v.tensor_mul(cW[2][:, 0:1], rad_all[:, t:t + 1],
                                              rad_all[:, t:t + 1]))
            p.vop(lambda v, t=t: v.tensor_scalar(
                cW[3][:], cW[0][:], rad_all[:, t:t + 1], 1.0 / HD,
                ALU.mult, ALU.mult))
            p.vop(lambda v: v.tensor_scalar(cW[4][:], cW[1][:], cW[2][:, 0:1],
                                            1.0 / HD, ALU.mult, ALU.mult))
            p.vop(lambda v: v.tensor_mul(cW[5][:], cW[3][:], cW[3][:]))
            p.vop(lambda v: v.tensor_sub(cW[5][:], cW[4][:], cW[5][:]))
            v_lnq = p.op("scalar", lambda s_: s_.activation(
                cW[6][:], cW[5][:], AF.Ln, bias=sb["epscol"][:]),
                wait=[("dve", p.val("dve"))], inc="act")
            v_eq = p.op("scalar", lambda s_: s_.activation(
                cW[6][:], cW[6][:], AF.Exp, scale=-0.5), inc="act")
            p.wait("vector", [("act", v_eq)])
            p.vop(lambda v, t=t: v.tensor_scalar(cW[7][:], cW[6][:],
                                                 rad_all[:, t:t + 1], None,
                                                 ALU.mult))
            p.vop(lambda v: v.tensor_mul(cW[9][:], cW[3][:], cW[6][:]))
            p.vop(lambda v, s3=s3, c1v=c1v: v.tensor_tensor(
                scr4[0][:].rearrange("p (h d) -> p h d", d=HD), s3, c1v, ALU.mult))
            p.vop(lambda v, s3=s3, c2v=c2v: v.tensor_tensor(
                scr4[1][:].rearrange("p (h d) -> p h d", d=HD), s3, c2v, ALU.mult))
            s0e = scr4[0][:].rearrange("p (h j two) -> p h j two", two=2, j=32)
            s1e = scr4[1][:].rearrange("p (h j two) -> p h j two", two=2, j=32)
            u0 = scr4[2][:].rearrange("p (h j two) -> p h j two", two=2, j=32)
            p.vop(lambda v, s0e=s0e, u0=u0: v.tensor_tensor(
                u0[:, :, :, 0], s0e[:, :, :, 0], s0e[:, :, :, 1], ALU.subtract))
            p.vop(lambda v, s1e=s1e, u0=u0: v.tensor_tensor(
                u0[:, :, :, 1], s1e[:, :, :, 0], s1e[:, :, :, 1], ALU.add))
            u3 = scr4[2][:].rearrange("p (h d) -> p h d", d=HD)
            g_b = cW[7][:].unsqueeze(2).to_broadcast([128, 8, HD])
            m2_b = cW[9][:].unsqueeze(2).to_broadcast([128, 8, HD])
            w_b = sb["wrep"][:].unsqueeze(1).to_broadcast([128, 8, HD])
            b_b = sb["brep"][:].unsqueeze(1).to_broadcast([128, 8, HD])
            p.vop(lambda v, u3=u3, g_b=g_b: v.tensor_tensor(
                scr4[3][:].rearrange("p (h d) -> p h d", d=HD), u3, g_b, ALU.mult))
            p.vop(lambda v, m2_b=m2_b: v.tensor_tensor(
                scr4[3][:].rearrange("p (h d) -> p h d", d=HD),
                scr4[3][:].rearrange("p (h d) -> p h d", d=HD), m2_b, ALU.subtract))
            p.vop(lambda v, w_b=w_b: v.tensor_tensor(
                scr4[3][:].rearrange("p (h d) -> p h d", d=HD),
                scr4[3][:].rearrange("p (h d) -> p h d", d=HD), w_b, ALU.mult))
            p.vop(lambda v, b_b=b_b: v.tensor_tensor(
                scr4[0][:].rearrange("p (h d) -> p h d", d=HD),
                scr4[3][:].rearrange("p (h d) -> p h d", d=HD), b_b, ALU.add))
            v_cast = p.vop(lambda v: v.tensor_copy(
                scr4[1][:].bitcast(BF16)[:, 0:512], scr4[0][:]))
            cp_done = {}
            for c2 in range(4):
                dstf = qnf if c2 < 2 else knf
                wvt = [("dve", v_cast)] if c2 < 2 else [("dve", cp_done[c2 - 2])]
                v_tr = tr(ps_tr[c2 % 2][:],
                          scr4[1][:].bitcast(BF16)[:, 128 * c2:128 * (c2 + 1)],
                          wait=wvt, inc="pe")
                tr_done[t] = v_tr
                cp_done[c2] = p.op(
                    "vector", lambda v, c2=c2, t=t, dstf=dstf, pt=ps_tr[c2 % 2]:
                    v.tensor_copy(dstf[c2 % 2][:, 128 * t:128 * (t + 1)], pt[:]),
                    wait=[("pe", v_tr)], inc="dve")
        p.wait("tensor", [("dve", p.val("dve"))])
        pss.close()
        if "dumpa" in DBG and g == 0:
            p.wait("sync", p.dmark() + [("dve", p.val("dve")), ("pe", p.val("pe")),
                                        ("act", p.val("act"))])
            p.dma("sync", lambda e: e.dma_start(out=dbg["stkv"][:, 0:2048],
                                                in_=mi64[:]))
            p.dma("sync", lambda e: e.dma_start(out=dbg["stqw"][:, 0:256],
                                                in_=csqkv[:, 272:528]))
            p.dma("sync", lambda e: e.dma_start(out=dbg["colst"], in_=sb["colst"][:]))
            p.dma("sync", lambda e: e.dma_start(out=dbg["qnf"], in_=qnf[0][:]))
            p.dma("sync", lambda e: e.dma_start(out=dbg["knf"], in_=knf[0][:]))
            p.dma("sync", lambda e: e.dma_start(out=dbg["vb"], in_=vbuf[:]))
            p.barrier()
        if "a1only" in DBG:
            p.barrier()
            return

        pss = PsumScope()
        ps_s = [pss.get([128, 1024]) for _ in range(2)]
        ps_av = [pss.get([65, 1024]) for _ in range(2)]
        v_prev_ab = 0
        for hp in range(2):
            for qh in range(2):
                qsl = slice(1024 * qh, 1024 * (qh + 1))
                v_av = None
                v_exp = {}
                for kt in range(NT):
                    ksl = slice(128 * kt, 128 * (kt + 1))
                    wv = [("dve", v_prev_ab)] if v_prev_ab else []
                    if kt >= 1:
                        wv = [("act", v_exp[kt - 1])]
                    for half in range(2):
                        for hh in range(2):
                            base = 64 * hh
                            mm(ps_s[hh][:, 512 * half:512 * (half + 1)],
                               knf[hp][base:base + 64, ksl],
                               qnf[hp][base:base + 64,
                                       1024 * qh + 512 * half:
                                       1024 * qh + 512 * (half + 1)],
                               True, True,
                               wait=wv if half == 0 and hh == 0 else ())
                    v_sc = p.op("tensor", None, inc="pe") if False else None
                    p.q["tensor"][-1] = (p.q["tensor"][-1][0], p.q["tensor"][-1][1], "pe", 1)
                    p.cnt["pe"] += 1
                    v_sc = p.cnt["pe"]
                    for hh in range(2):
                        v_e = p.op("scalar", lambda s_, hh=hh: s_.activation(
                            pbf[hh][:], ps_s[hh][:], AF.Exp, bias=sb["cbias"][:],
                            scale=0.125), wait=[("pe", v_sc)] if hh == 0 else (),
                            inc="act")
                    v_exp[kt] = v_e
                    for hh in range(2):
                        h4 = 2 * hp + hh
                        vsl = vbuf[:].rearrange("p (tt h c) -> p tt h c", h=4, c=65)
                        mm(ps_av[hh][:, 0:512], vsl[:, kt, h4, :],
                           pbf[hh][:, 0:512], kt == 0, kt == NT - 1,
                           wait=[("act", v_exp[kt])] if hh == 0 else ())
                        v_av = mm(ps_av[hh][:, 512:1024],
                                  vsl[:, kt, h4, :],
                                  pbf[hh][:, 512:1024], kt == 0, kt == NT - 1,
                                  inc="pe" if (kt == NT - 1 and hh == 1) else None)
                p.wait("vector", [("pe", v_av)])
                for hh in range(2):
                    p.op("scalar", lambda s_, hh=hh: s_.activation(
                        zrow[0:1, :], ps_av[hh][64:65, :], AF.Ln),
                        wait=[("pe", v_av), ("dve", p.val("dve"))])
                    v_re = p.op("scalar", lambda s_: s_.activation(
                        zrow[0:1, :], zrow[0:1, :], AF.Exp, scale=-1.0))
                    p.op("vector", lambda v: v.tensor_copy(zbf[:], zrow[:]),
                         wait=[("act", v_re)])
                    v_z1 = mm(ps_s[hh][0:64, 0:512], sb["onesbf"][0:1, 0:64],
                              zbf[0:1, 0:512], True, True, wait=[("dve", p.val("dve"))])
                    v_z2 = mm(ps_s[hh][0:64, 512:1024], sb["onesbf"][0:1, 0:64],
                              zbf[0:1, 512:1024], True, True, inc="pe")
                    p.wait("vector", [("pe", v_z2)])
                    v_zc = p.vop(lambda v, hh=hh: v.tensor_copy(zrb[:],
                                                                ps_s[hh][0:64, :]))
                    base = 64 * hh
                    p.vop(lambda v, hh=hh, hp=hp, base=base, qsl=qsl: v.tensor_tensor(
                        abf[hp][base:base + 64, qsl], ps_av[hh][0:64, :], zrb[:],
                        ALU.mult))
                v_prev_ab = p.val("dve")
        pss.close()
        if "a12" in DBG:
            p.barrier()
            return

        pss = PsumScope()
        po = [pss.get([128, 512]) for _ in range(2)]
        ep_done = {}
        for oc in range(NC):
            for tg in range(4):
                tsl = slice(512 * tg, 512 * (tg + 1))
                idx = oc * 4 + tg
                pk = po[idx % 2]
                wv = [("dve", p.val("dve"))] if idx == 0 else ()
                if idx >= 2:
                    wv = [("dve", ep_done[idx - 2])]
                mm(pk[:], outwg[0][:, 128 * oc:128 * (oc + 1)], abf[0][:, tsl],
                   True, False, wait=wv)
                v_mm = mm(pk[:], outwg[1][:, 128 * oc:128 * (oc + 1)], abf[1][:, tsl],
                          False, True, inc="pe")
                if g == 0:
                    v_ep = p.op("vector", lambda v, oc=oc, tsl=tsl, pk=pk:
                                v.scalar_tensor_tensor(X[oc][:, tsl], pk[:],
                                                       outbc[:, oc:oc + 1],
                                                       X[oc][:, tsl], ALU.add, ALU.add),
                                wait=[("pe", v_mm)], inc="dve")
                else:
                    v_ep = p.op("vector", lambda v, oc=oc, tsl=tsl, pk=pk:
                                v.tensor_add(X[oc][:, tsl], X[oc][:, tsl], pk[:]),
                                wait=[("pe", v_mm)], inc="dve")
                ep_done[idx] = v_ep
        p.barrier()
        pss.close()


    def prep_attb():
        pss = PsumScope()
        ps_cs = pss.get([1, 2048])
        wbuf = wview(0, [128, 2048], F32, "pawbuf")
        wbf = wview(8192, [128, 2080], BF16, "pawbf")
        rowbf2 = wview(20544, [1, 4160], BF16, "parowbf")
        casts = [("ab_wq", "st_wq", D, NC), ("ab_wk", "st_wk", D, NC),
                 ("ab_wv", "st_wv", D, NC), ("ab_wgq", "st_wgq", LDIM, NC),
                 ("ab_wgk", "st_wgk", LDIM, NC), ("ab_wgv", "st_wgv", LDIM, NC),
                 ("ab_w_out", "st_wout", D, NC), ("ab_r_out_w", "st_rout", D, LDIM // 128)]
        for wname, stname, wide, nchunk in casts:
            for c in range(nchunk):
                v_l = p.dma("sync", lambda e, c=c, wname=wname, wide=wide: e.dma_start(
                    out=wbuf[:, 0:wide],
                    in_=win[wname].rearrange("(c p) o -> c p o", p=128)[c]),
                    wait=[("dve", p.val("dve"))])
                v_c = p.op("vector", lambda v, wide=wide: v.tensor_copy(
                    wbf[:, 0:wide], wbuf[:, 0:wide]), wait=[v_l])
                v_st = p.dma("sync", lambda e, c=c, stname=stname, wide=wide:
                             e.dma_start(out=st[stname][c][:, 0:wide],
                                         in_=wbf[:, 0:wide]), wait=[("dve", v_c)])
                p.wait("vector", [v_st])
        gbuf = wview(24576, [HD, 2 * NH], F32, "pglw")
        v_l = p.dma("sync", lambda e: e.dma_start(out=gbuf[:], in_=tabs["tab_glwT"]))
        p.op("vector", lambda v: v.tensor_copy(sb["glwt"][0:64, :], gbuf[:]),
             wait=[v_l])
        p.op("vector", lambda v: v.tensor_copy(sb["glwt"][64:128, :], gbuf[:]))
        p.wait("sync", p.dmark())
        for stname, ro, wide in [("st_wq", "colsum_wq", D), ("st_wk", "colsum_wk", D),
                                 ("st_wv", "colsum_wv", D),
                                 ("st_wgq", "colsum_wgq", LDIM)]:
            v_cl = None
            for c in range(NC):
                wv = [("pe", v_cl)] if v_cl else [("pe", p.val("pe")),
                                                 ("dve", p.val("dve"))]
                v_l = p.dma("sync", lambda e, c=c, stname=stname, wide=wide:
                            e.dma_start(out=wbf[:, 0:wide],
                                        in_=st[stname][c][:, 0:wide]), wait=wv)
                ng = wide // 512
                for gg in range(ng):
                    v_cl = mm(ps_cs[0:1, 512 * gg:512 * (gg + 1)], sb["onecolbf"][:],
                              wbf[:, 512 * gg:512 * (gg + 1)], c == 0, c == NC - 1,
                              wait=[v_l] if gg == 0 else (),
                              inc="pe" if gg == ng - 1 else None)
            p.wait("vector", [("pe", v_cl)])
            v_n = p.op("vector", lambda v, wide=wide: v.tensor_scalar(
                rowbf2[0:1, 0:wide], ps_cs[0:1, 0:wide], -1.0, None, ALU.mult))
            v_s = p.dma("sync", lambda e, ro=ro, wide=wide: e.dma_start(
                out=st["st_rows"][:, RO[ro]:RO[ro] + wide], in_=rowbf2[0:1, 0:wide]),
                wait=[("dve", v_n)])
            p.wait("vector", [v_s])
        p.wait("sync", p.dmark())
        p.wait("gpsimd", p.dmark())
        p.barrier()
        pss.close()

    BB_VBL = 0
    BB_QLF = 4096
    BB_KLF = 8192
    BB_EBL = 12288
    BB_SG = 15360
    BB_SG2 = 23552
    BB_ZRL = 31744
    BB_XSL = 39936
    BB_UBF = 44032
    BB_GRBF = 48128
    BB_LOCALF = 52224
    BB_T1 = 56320
    BB_QKVT = 60416

    def attb_sub(s):
        ln_stats(s, attn_a_mode=False, rrep_flag=False)
        for c in range(NC):
            p.dma("gpsimd", lambda e, c=c: e.dma_start(
                out=st["st_xbf"][c], in_=X[c][:]))
        rcol_all = sb["colst"][:, CO["r"]:CO["r"] + 16]
        mcol_all = sb["colst"][:, CO["tmp"]:CO["tmp"] + 16]
        mrcol_all = sb["colst"][:, CO["R2"]:CO["R2"] + 16]
        p.op("vector", lambda v: v.tensor_mul(mrcol_all, mcol_all, rcol_all))
        mi64v = wview(49152, [128, S], BF16, "mi64")
        p.dma("gpsimd", lambda e: e.dma_start(
            out=mi64v[0:64, :], in_=st["st_mrow64"][0:1, :].to_broadcast([64, S])))
        p.wait("sync", p.dmark())
        p.wait("gpsimd", p.dmark())
        p.barrier()
        if "nob0" not in DBG:
            b0_latent(s)
        if "b0only" in DBG:
            return
        npair = NC if "b11" not in DBG else 1
        for hp in range(npair):
            b1_local(s, hp)
        if "nobout" not in DBG:
            b_out(s)

    def b0_latent(s):
        rcol_all = sb["colst"][:, CO["r"]:CO["r"] + 16]
        mrcol_all = sb["colst"][:, CO["R2"]:CO["R2"] + 16]
        mi64v = wview(49152, [128, S], BF16, "mi64")
        latm2 = aview(0, [128, S], BF16, "latm2")
        gqf = [aview(4096 + j * 4096, [128, S], BF16, "gqf") for j in range(4)]
        elat = aview(20480, [128, S], BF16, "elat")
        gkbf = aview(24576, [NLAT, LDIM], BF16, "gkbf")
        gvbf = aview(25600, [128, LDIM], BF16, "gvbf")
        gkt = aview(26624, [128, LDIM], BF16, "gkt")
        poolbf = aview(27648, [NLAT, D], BF16, "poolbf")
        poolT = aview(29696, [128, LDIM], BF16, "poolT")
        rc = aview(34816, [128, 128], BF16, "rcpool")
        onesw = aview(35072, [128, 512], BF16, "onesw")
        zre = aview(36096 if False else BB_ZRL, [33, S], F32, "zre")
        remf = [aview(44288 + j * 4096, [128, S], BF16, "remf") for j in range(4)]
        gqtmp = aview(60672, [128, 512], BF16, "gqtmp")
        rem2b = aview(62720, [128, 1024], BF16, "rem2b")
        xtts = [aview(30720 + i * 2048, [128, D], BF16, "xtts") for i in range(2)]
        xslb = [aview(BB_XSL + sl * 2048, [128, 1024], BF16, "xslab2")
                for sl in range(2)]
        wgql = [wview(c * 1024, [128, LDIM], BF16, "wgql") for c in range(NC)]
        wgkl = [wview(8192 + c * 1024, [128, LDIM], BF16, "wgkl") for c in range(NC)]
        wgvl = [wview(16384 + c * 1024, [128, LDIM], BF16, "wgvl") for c in range(NC)]
        routl = [wview(40960 + j * 2048, [128, D], BF16, "routl") for j in range(4)]
        cswgq = wview(53248, [64, LDIM], BF16, "cswgq")
        rem2f = [wview(c * 4096, [128, S], BF16, "rem2f") for c in range(NC)]

        for c in range(NC):
            p.dma("sync", lambda e, c=c: e.dma_start(
                out=wgql[c][:], in_=st["st_wgq"][c]))
            p.dma("sync", lambda e, c=c: e.dma_start(
                out=wgkl[c][:], in_=st["st_wgk"][c]))
            p.dma("sync", lambda e, c=c: e.dma_start(
                out=wgvl[c][:], in_=st["st_wgv"][c]))
        for j in range(4):
            p.dma("sync", lambda e, j=j: e.dma_start(
                out=routl[j][:], in_=st["st_rout"][j]))
        p.dma("sync", lambda e: e.dma_start(
            out=latm2[0:NLAT, :], in_=tabs["tab_latmask"]))
        p.dma("sync", lambda e: e.dma_start(
            out=latm2[NLAT:128, :], in_=tabs["tab_latmask"]))
        p.dma("gpsimd", lambda e: e.dma_start(
            out=cswgq[:], in_=st["st_rows"]
            [0:1, RO["colsum_wgq"]:RO["colsum_wgq"] + LDIM].to_broadcast([64, LDIM])))
        p.op("vector", lambda v: v.memset(onesw[:], 1.0))
        gml = p.dmark() + [("dve", p.val("dve"))]
        p.wait("tensor", gml)
        p.wait("vector", gml)

        pss = PsumScope()
        ps_gq = [pss.get([128, 512]) for _ in range(2)]
        ps_pool = pss.get([NLAT, D])
        ps_trx = [pss.get([128, 128], BF16) for _ in range(2)]
        ps_trg = [pss.get([128, 128], BF16) for _ in range(2)]

        def xb_load2(t, wv):
            return p.dma("sync", lambda e, t=t: e.dma_start(
                out=xslb[t % 2][:].rearrange("p (c u) -> p c u", u=128),
                in_=st["st_xbf"][:, :, 128 * t:128 * (t + 1)]
                .rearrange("c p u -> p c u")), wait=wv)

        xtok = {0: xb_load2(0, ()), 1: xb_load2(1, ())}
        ev2 = {}
        for t in range(NT):
            bi = t % 2
            tsl = slice(128 * t, 128 * (t + 1))
            pg = ps_gq[bi]
            wv0 = [xtok[t]]
            if t - 2 in ev2:
                wv0.append(("dve", ev2[t - 2]))
            for c in range(NC):
                mm(pg[:, 0:512], xslb[bi][:, 128 * c:128 * (c + 1)],
                   wgql[c][:], c == 0, False, wait=wv0 if c == 0 else ())
            v_gq = mm(pg[:, 0:512], mi64v[0:64, tsl], cswgq[:], False, True, inc="pe")
            vtr = {}
            for c in range(NC):
                v_t = tr(ps_trx[c % 2][:], xslb[bi][:, 128 * c:128 * (c + 1)],
                         wait=[("act", p.val("act"))] if c >= 2 else (), inc="pe")
                vtr[c] = p.op("scalar", lambda s_, c=c, bi=bi: s_.copy(
                    xtts[bi][:, 128 * c:128 * (c + 1)], ps_trx[c % 2][:]),
                    wait=[("pe", v_t)])
            p.op("vector", lambda v: v.memset(rc[:], 0.0),
                 wait=[("pe", p.val("pe"))])
            p.op("vector", lambda v, t=t: v.tensor_scalar(
                rc[:, 4 * t:4 * t + 4], sb["t0"][:], rcol_all[:, t:t + 1],
                1.0 / 32.0, ALU.mult, ALU.mult))
            v_rc2 = p.op("vector", lambda v, t=t: v.tensor_scalar(
                rc[:, 64 + 4 * t:64 + 4 * t + 4], sb["t0"][:], mrcol_all[:, t:t + 1],
                -1.0 / 32.0, ALU.mult, ALU.mult))
            for c in range(NC):
                mm(ps_pool[:, 128 * c:128 * (c + 1)], rc[:, 0:64],
                   xtts[bi][:, 128 * c:128 * (c + 1)],
                   t == 0 and c in (0, 4), False,
                   wait=[("act", vtr[NC - 1]), ("dve", v_rc2)] if c == 0 else ())
            v_pl = None
            for gg in range(2):
                v_pl = mm(ps_pool[:, 512 * gg:512 * (gg + 1)], rc[:, 64:128],
                          onesw[:], False, t == NT - 1,
                          inc="pe" if (t == NT - 1 and gg == 1) else None)
            v_ev = p.op("vector", lambda v, t=t, pg=pg: v.tensor_scalar(
                gqtmp[:], pg[:], rcol_all[:, t:t + 1], None, ALU.mult),
                wait=[("pe", v_gq)])
            ev2[t] = v_ev
            for j in range(4):
                wvt = [("dve", v_ev)] if j == 0 else []
                if j >= 2 or t >= 1:
                    wvt.append(("act", p.val("act")))
                v_t = tr(ps_trg[j % 2][:], gqtmp[:, 128 * j:128 * (j + 1)],
                         wait=wvt, inc="pe")
                p.op("scalar", lambda s_, j=j, t=t: s_.copy(
                    gqf[j][:, 128 * t:128 * (t + 1)], ps_trg[j % 2][:]),
                    wait=[("pe", v_t)])
            if t + 2 < NT:
                xtok[t + 2] = xb_load2(t + 2, [("pe", p.val("pe"))])
        p.wait("vector", [("pe", p.val("pe")), ("act", p.val("act"))])
        v_pb = p.op("vector", lambda v: v.tensor_copy(poolbf[:], ps_pool[:]))
        vtp = None
        for c in range(NC):
            wvt = [("dve", v_pb)] if c == 0 else []
            if c >= 2:
                wvt.append(("act", p.val("act")))
            v_t = tr(ps_trx[c % 2][0:NLAT, :].bitcast(BF16)[:, 0:NLAT]
                     if False else ps_trx[c % 2][:, 0:NLAT],
                     poolbf[:, 128 * c:128 * (c + 1)], wait=wvt, inc="pe")
            vtp = p.op("scalar", lambda s_, c=c: s_.copy(
                poolT[:, 64 * c:64 * (c + 1)], ps_trx[c % 2][:, 0:NLAT]),
                wait=[("pe", v_t)])
        pss.close()
        pss = PsumScope()
        ps_gk = pss.get([NLAT, LDIM])
        ps_gv = pss.get([NLAT, LDIM])
        ps_trk = [pss.get([NLAT, 64], BF16) for _ in range(2)]
        for c in range(NC):
            mm(ps_gk[:], poolT[:, 64 * c:64 * (c + 1)], wgkl[c][:], c == 0,
               c == NC - 1, wait=[("act", vtp)] if c == 0 else ())
        v_gv = None
        for c in range(NC):
            v_gv = mm(ps_gv[:], poolT[:, 64 * c:64 * (c + 1)], wgvl[c][:], c == 0,
                      c == NC - 1, inc="pe" if c == NC - 1 else None)
        p.wait("vector", [("pe", v_gv)])
        v_ek = p.op("vector", lambda v: v.tensor_copy(gkbf[:], ps_gk[:]))
        p.op("vector", lambda v: v.tensor_copy(gvbf[0:64, :], ps_gv[:]))
        v_evv = p.op("vector", lambda v: v.tensor_copy(gvbf[64:128, :], ps_gv[:]))
        for h in range(8):
            wvt = [("dve", v_ek)] if h == 0 else []
            if h >= 2:
                wvt.append(("act", p.val("act")))
            v_t = tr(ps_trk[h % 2][:], gkbf[:, 64 * h:64 * (h + 1)],
                     wait=wvt, inc="pe")
            p.op("scalar", lambda s_, h=h: s_.copy(
                gkt[0:64, 64 * h:64 * (h + 1)], ps_trk[h % 2][:]),
                wait=[("pe", v_t)])
            p.op("scalar", lambda s_, h=h: s_.copy(
                gkt[64:128, 64 * h:64 * (h + 1)], ps_trk[h % 2][:]))
        p.wait("tensor", [("act", p.val("act")), ("dve", v_evv)])
        pss.close()
        for lhp in range(4):
            pss = PsumScope()
            ps_lsc = pss.get([128, S])
            ps_zl = pss.get([33, S])
            v_last = None
            for hh in range(2):
                h = 2 * lhp + hh
                for ch in range(4):
                    csl = slice(512 * ch, 512 * (ch + 1))
                    v_last = mm(ps_lsc[64 * hh:64 * hh + 64, csl],
                                gkt[64 * hh:64 * hh + 64, 64 * h:64 * (h + 1)],
                                gqf[lhp][64 * hh:64 * hh + 64, csl], True, True,
                                wait=[("dve", p.val("dve"))]
                                if hh == 0 and ch == 0 else (),
                                inc="pe" if hh == 1 and ch == 3 else None)
            v_e = None
            for ch in range(4):
                csl = slice(512 * ch, 512 * (ch + 1))
                for hh in range(2):
                    hsl = slice(64 * hh, 64 * hh + 64)
                    v_e = p.op("scalar", lambda s_, csl=csl, hsl=hsl, pl=ps_lsc:
                               s_.activation(elat[hsl, csl], pl[hsl, csl],
                                             AF.Exp, scale=0.125),
                               wait=[("pe", v_last)] if ch == 0 and hh == 0 else ())
            v_m = p.op("vector", lambda v: v.tensor_mul(elat[:], elat[:], latm2[:]),
                       wait=[("act", v_e)])
            v_z = None
            for hh in range(2):
                for ch in range(4):
                    csl = slice(512 * ch, 512 * (ch + 1))
                    v_z = mm(ps_zl[32 * hh:32 * hh + 1, csl],
                             sb["onecolbf"][64 * hh:64 * hh + 64, :],
                             elat[64 * hh:64 * hh + 64, csl], True, True,
                             wait=[("dve", v_m)] if hh == 0 and ch == 0 else (),
                             inc="pe" if hh == 1 and ch == 3 else None)
            for r0 in (0, 32):
                p.op("scalar", lambda s_, pz=ps_zl, r0=r0: s_.activation(
                    zre[r0:r0 + 1, :], pz[r0:r0 + 1, :], AF.Ln),
                    wait=[("pe", v_z)] if r0 == 0 else ())
                v_rec = p.op("scalar", lambda s_, r0=r0: s_.activation(
                    zre[r0:r0 + 1, :], zre[r0:r0 + 1, :], AF.Exp, scale=-1.0))
            pss.close()
            pss = PsumScope()
            ps_zrep = pss.get([128, S])
            ps_rem = pss.get([128, S])
            v_rp = None
            for hh in range(2):
                for ch in range(4):
                    csl = slice(512 * ch, 512 * (ch + 1))
                    v_rp = mm(ps_zrep[64 * hh:64 * hh + 64, csl],
                              sb["onesf32"][32 * hh:32 * hh + 1, 0:64],
                              zre[32 * hh:32 * hh + 1, csl], True, True,
                              wait=[("act", v_rec)] if hh == 0 and ch == 0 else (),
                              inc="pe" if hh == 1 and ch == 3 else None)
            v_nrm = p.op("vector", lambda v, pzr=ps_zrep: v.tensor_mul(
                elat[:], elat[:], pzr[:]), wait=[("pe", v_rp)])
            v_rm = None
            for hh in range(2):
                h = 2 * lhp + hh
                for ch in range(4):
                    csl = slice(512 * ch, 512 * (ch + 1))
                    v_rm = mm(ps_rem[64 * hh:64 * hh + 64, csl],
                              gvbf[64 * hh:64 * hh + 64, 64 * h:64 * (h + 1)],
                              elat[64 * hh:64 * hh + 64, csl], True, True,
                              wait=[("dve", v_nrm)] if hh == 0 and ch == 0 else (),
                              inc="pe" if hh == 1 and ch == 3 else None)
            p.op("vector", lambda v, lhp=lhp, pr=ps_rem: v.tensor_copy(
                remf[lhp][:], pr[:]), wait=[("pe", v_rm)])
            pss.close()
        pss = PsumScope()
        ps_r2 = [pss.get([128, 512]) for _ in range(2)]
        ps_trr = [pss.get([128, 128], BF16) for _ in range(2)]
        for t in range(NT):
            tsl = slice(128 * t, 128 * (t + 1))
            v_r2 = None
            for half in range(2):
                pk = ps_r2[half]
                osl = slice(512 * half, 512 * (half + 1))
                for j in range(4):
                    v_r2 = mm(pk[:], remf[j][:, tsl], routl[j][:, osl],
                              j == 0, j == 3,
                              wait=[("dve", p.val("dve"))] if j == 0 else (),
                              inc="pe" if j == 3 else None)
                p.op("vector", lambda v, pk=pk, osl=osl: v.tensor_copy(
                    rem2b[:, osl], pk[:]), wait=[("pe", v_r2)])
            for c in range(NC):
                wvt = [("dve", p.val("dve"))] if c == 0 else []
                if c >= 2 or t >= 1:
                    wvt.append(("act", p.val("act")))
                v_t = tr(ps_trr[c % 2][:], rem2b[:, 128 * c:128 * (c + 1)],
                         wait=wvt, inc="pe")
                p.op("scalar", lambda s_, c=c, t=t: s_.copy(
                    rem2f[c][:, 128 * t:128 * (t + 1)], ps_trr[c % 2][:]),
                    wait=[("pe", v_t)])
        p.wait("tensor", [("act", p.val("act")), ("dve", p.val("dve"))])
        pss.close()
        p.barrier()

    def b1_local(s, hp):
        rcol_all = sb["colst"][:, CO["r"]:CO["r"] + 16]
        mi64v = wview(49152, [128, S], BF16, "mi64")
        rem2f = [wview(c * 4096, [128, S], BF16, "rem2f") for c in range(NC)]
        vbl = aview(BB_VBL, [128, S], BF16, "latm2")
        qlf = aview(BB_QLF, [128, S], BF16, "gqf")
        klf = aview(BB_KLF, [128, S], BF16, "gqf")
        eblb = [aview(BB_EBL + i * 1536, [128, 768], BF16, "eblb") for i in range(2)]
        sg = aview(BB_SG, [33, S], F32, "sgrow")
        sg2 = aview(BB_SG2, [33, S], F32, "sgrow2")
        zrl = aview(BB_ZRL, [33, S], F32, "zre")
        ubf = aview(BB_UBF, [33, S], BF16, "ubf")
        grbf = aview(BB_GRBF, [33, S], BF16, "grbf")
        localf = aview(BB_LOCALF, [128, S], BF16, "localf")
        t1buf = aview(BB_T1, [128, S], BF16, "t1buf")
        qkvtmp = aview(BB_QKVT, [128, 384], BF16, "qkvtmp")
        xslb = [aview(BB_XSL + sl * 2048, [128, 1024], BF16, "xslab2")
                for sl in range(2)]
        wlpk = [wview(32768 + c * 768, [128, 384], BF16, "wlpk") for c in range(NC)]
        cs3 = wview(38912, [64, 384], BF16, "cs3")
        for c in range(NC):
            for k3, stname in enumerate(("st_wq", "st_wk", "st_wv")):
                p.dma("sync", lambda e, c=c, k3=k3, stname=stname:
                      e.dma_start(out=wlpk[c][:, 128 * k3:128 * (k3 + 1)],
                                  in_=st[stname][c][:, 128 * hp:128 * (hp + 1)]))
        for k3, ro in enumerate(("colsum_wq", "colsum_wk", "colsum_wv")):
            p.dma("gpsimd", lambda e, k3=k3, ro=ro: e.dma_start(
                out=cs3[:, 128 * k3:128 * (k3 + 1)],
                in_=st["st_rows"][0:1, RO[ro] + 128 * hp:RO[ro] + 128 * (hp + 1)]
                .to_broadcast([64, 128])))
        gml = p.dmark()
        p.wait("tensor", gml)
        p.wait("vector", gml)
        pss = PsumScope()
        ps_qkv = [pss.get([128, 512]) for _ in range(2)]
        ps_trq = [pss.get([128, 128], BF16) for _ in range(2)]

        def xb_load3(t, wv):
            return p.dma("sync", lambda e, t=t: e.dma_start(
                out=xslb[t % 2][:].rearrange("p (c u) -> p c u", u=128),
                in_=st["st_xbf"][:, :, 128 * t:128 * (t + 1)]
                .rearrange("c p u -> p c u")), wait=wv)

        xtok = {0: xb_load3(0, ()), 1: xb_load3(1, ())}
        ev2 = {}
        for t in range(NT):
            bi = t % 2
            tsl = slice(128 * t, 128 * (t + 1))
            pg = ps_qkv[bi]
            wv0 = [xtok[t]]
            if t - 2 in ev2:
                wv0.append(("dve", ev2[t - 2]))
            for c in range(NC):
                mm(pg[:, 0:384], xslb[bi][:, 128 * c:128 * (c + 1)],
                   wlpk[c][:], c == 0, False, wait=wv0 if c == 0 else ())
            v_pj = mm(pg[:, 0:384], mi64v[0:64, tsl], cs3[:], False, True, inc="pe")
            if t + 2 < NT:
                xtok[t + 2] = xb_load3(t + 2, [("pe", v_pj)])
            v_sc = p.op("vector", lambda v, t=t, pg=pg: v.tensor_scalar(
                qkvtmp[:], pg[:, 0:384], rcol_all[:, t:t + 1], None, ALU.mult),
                wait=[("pe", v_pj)])
            v_vv = p.op("vector", lambda v, t=t: v.tensor_copy(
                vbl[:, 128 * t:128 * (t + 1)], qkvtmp[:, 256:384]))
            ev2[t] = v_vv
            for j, dstf in ((0, qlf), (1, klf)):
                wvt = [("dve", v_vv)] if j == 0 else []
                if t >= 1:
                    wvt.append(("act", p.val("act")))
                v_t = tr(ps_trq[j][:], qkvtmp[:, 128 * j:128 * (j + 1)],
                         wait=wvt, inc="pe")
                p.op("scalar", lambda s_, dstf=dstf, t=t, j=j: s_.copy(
                    dstf[:, 128 * t:128 * (t + 1)], ps_trq[j][:]),
                    wait=[("pe", v_t)])
        p.wait("tensor", [("act", p.val("act"))])
        pss.close()
        if "b1proj" in DBG:
            p.barrier()
            return
        pss = PsumScope()
        ps_g1 = pss.get([33, S])
        ps_g2 = pss.get([33, S])
        v_g = None
        for hh in range(2):
            h = 2 * hp + hh
            for gi, (pgx, coloff) in enumerate(((ps_g1, 0), (ps_g2, NH))):
                for ch in range(4):
                    csl = slice(512 * ch, 512 * (ch + 1))
                    v_g = mm(pgx[32 * hh:32 * hh + 1, csl],
                             sb["glwt"][64 * hh:64 * hh + 64,
                                        coloff + h:coloff + h + 1],
                             qlf[64 * hh:64 * hh + 64, csl], True, True,
                             wait=[("act", p.val("act"))]
                             if hh == 0 and gi == 0 and ch == 0 else (),
                             inc="pe" if (hh == 1 and gi == 1 and ch == 3) else None)
        for r0 in (0, 32):
            p.op("scalar", lambda s_, r0=r0: s_.activation(
                sg[r0:r0 + 1, :], ps_g1[r0:r0 + 1, :], AF.Exp, scale=-1.0),
                wait=[("pe", v_g)] if r0 == 0 else ())
        v_e2 = None
        for r0 in (0, 32):
            v_e2 = p.op("scalar", lambda s_, r0=r0: s_.activation(
                sg2[r0:r0 + 1, :], ps_g2[r0:r0 + 1, :], AF.Exp, scale=-1.0))
        v_p1 = None
        for r0 in (0, 32):
            p.op("vector", lambda v, r0=r0: v.tensor_scalar(
                sg[r0:r0 + 1, :], sg[r0:r0 + 1, :], 1.0, None, ALU.add),
                wait=[("act", v_e2)] if r0 == 0 else ())
            v_p1 = p.op("vector", lambda v, r0=r0: v.tensor_scalar(
                sg2[r0:r0 + 1, :], sg2[r0:r0 + 1, :], 1.0, None, ALU.add))
        v_sig = None
        for r0 in (0, 32):
            for buf in (sg, sg2):
                p.op("scalar", lambda s_, r0=r0, buf=buf: s_.activation(
                    buf[r0:r0 + 1, :], buf[r0:r0 + 1, :], AF.Ln),
                    wait=[("dve", v_p1)] if r0 == 0 and buf is sg else ())
                v_sig = p.op("scalar", lambda s_, r0=r0, buf=buf: s_.activation(
                    buf[r0:r0 + 1, :], buf[r0:r0 + 1, :], AF.Exp, scale=-1.0))
        pss.close()
        if "b1gate" in DBG:
            p.barrier()
            return
        pss = PsumScope()
        ps_sl = [[pss.get([128, 512]) for _ in range(2)] for _ in range(2)]
        ps_av = [pss.get([128, 128]) for _ in range(2)]
        ps_zt = [pss.get([65, 128]) for _ in range(2)]
        for i in range(2):
            p.op("vector", lambda v, i=i: v.memset(eblb[i][:], 0.0))
        v_lf = {}
        v_zev = {}
        for t in range(NT):
            bi = t % 2
            tsl = slice(128 * t, 128 * (t + 1))
            kts = [t - 2, t - 1, t]
            v_s = None
            for ri, kt in enumerate(kts):
                if kt < 0:
                    continue
                ksl = slice(128 * kt, 128 * (kt + 1))
                for hh in range(2):
                    wv = []
                    if v_s is None:
                        wv = [("act", p.val("act")), ("dve", p.val("dve"))]
                        if t - 2 in v_lf:
                            wv.append(("act", v_lf[t - 2]))
                    v_s = mm(ps_sl[bi][hh][:, 128 * ri:128 * (ri + 1)],
                             klf[64 * hh:64 * hh + 64, ksl],
                             qlf[64 * hh:64 * hh + 64, tsl], True, True,
                             wait=wv, inc=None)
            if "nohack" not in DBG:
                p.q["tensor"][-1] = (p.q["tensor"][-1][0], p.q["tensor"][-1][1],
                                     "pe", 1)
                p.cnt["pe"] += 1
                v_s = p.cnt["pe"]
            if "noexp" in DBG:
                continue
            lo = 256 if t == 0 else (128 if t == 1 else 0)
            v_e = None
            for hh in range(2):
                v_e = p.op("scalar", lambda s_, bi=bi, hh=hh, lo=lo: s_.activation(
                    eblb[bi][:, 384 * hh + lo:384 * (hh + 1)],
                    ps_sl[bi][hh][:, lo:384], AF.Exp, scale=0.125),
                    wait=[("pe", v_s)] if hh == 0 else ())
            ee = eblb[bi][:].rearrange("p (h m) -> p h m", h=2)
            lmb = sb["lmask"][:].unsqueeze(1).to_broadcast([128, 2, 384])
            if "nomask" in DBG:
                v_m = p.op("vector", lambda v: v.engine_nop(),
                           wait=[("act", v_e)])
            else:
                v_m = p.op("vector", lambda v, ee=ee, lmb=lmb: v.tensor_tensor(
                    ee, ee, lmb, ALU.mult), wait=[("act", v_e)])
            v_av = None
            v_z = None
            for hh in range(2):
                if "nozmm" in DBG:
                    break
                first = True
                for ri, kt in enumerate(kts):
                    if kt < 0:
                        continue
                    rsl = slice(384 * hh + 128 * ri, 384 * hh + 128 * (ri + 1))
                    wvz = []
                    if hh == 0 and first:
                        wvz = [("dve", v_m)]
                        if t - 2 in v_zev:
                            wvz.append(("act", v_zev[t - 2]))
                    v_z = mm(ps_zt[bi][64 * hh:64 * hh + 1, 0:128],
                             sb["onecolbf"][:], eblb[bi][:, rsl], first, kt == t,
                             wait=wvz,
                             inc="pe" if (hh == 1 and kt == t) else None)
                    first = False
            for hh in range(2):
                if "noav" in DBG:
                    break
                first = True
                for ri, kt in enumerate(kts):
                    if kt < 0:
                        continue
                    rsl = slice(384 * hh + 128 * ri, 384 * hh + 128 * (ri + 1))
                    v_av = mm(ps_av[bi][64 * hh:64 * hh + 64, 0:128],
                              vbl[:, 128 * kt + 64 * hh:128 * kt + 64 * (hh + 1)],
                              eblb[bi][:, rsl], first, kt == t,
                              wait=[("dve", v_m)] if (hh == 0 and first
                                                      and "nozmm" in DBG) else (),
                              inc="pe" if (hh == 1 and kt == t) else None)
                    first = False
            if "nozmm" not in DBG:
                p.op("scalar", lambda s_, bi=bi, tsl=tsl: s_.copy(
                    zrl[0:1, tsl], ps_zt[bi][0:1, :]), wait=[("pe", v_z)])
                v_zev[t] = p.op("scalar", lambda s_, bi=bi, tsl=tsl: s_.copy(
                    zrl[32:33, tsl], ps_zt[bi][64:65, :]))
            if "noav" not in DBG:
                v_lf[t] = p.op("scalar", lambda s_, bi=bi, tsl=tsl: s_.copy(
                    localf[:, tsl], ps_av[bi][:]), wait=[("pe", v_av)])
        p.wait("vector", [("pe", p.val("pe")), ("act", p.val("act"))])
        if "b1attn" in DBG:
            pss.close()
            p.barrier()
            return
        v_zi = None
        for r0 in (0, 32):
            p.op("scalar", lambda s_, r0=r0: s_.activation(
                zrl[r0:r0 + 1, :], zrl[r0:r0 + 1, :], AF.Ln))
            v_zi = p.op("scalar", lambda s_, r0=r0: s_.activation(
                zrl[r0:r0 + 1, :], zrl[r0:r0 + 1, :], AF.Exp, scale=-1.0))
        v_gr = None
        for r0 in (0, 32):
            p.op("vector", lambda v, r0=r0: v.tensor_mul(
                ubf[r0:r0 + 1, :], sg[r0:r0 + 1, :], zrl[r0:r0 + 1, :]),
                wait=[("act", v_zi)] if r0 == 0 else ())
            v_gr = p.op("vector", lambda v, r0=r0: v.tensor_copy(
                grbf[r0:r0 + 1, :], sg2[r0:r0 + 1, :]))
        pss.close()
        pss = PsumScope()
        ps_u = pss.get([128, S])
        ps_gr = pss.get([128, S])
        v_rep = None
        for hh in range(2):
            for ch in range(4):
                csl = slice(512 * ch, 512 * (ch + 1))
                mm(ps_u[64 * hh:64 * hh + 64, csl],
                   sb["onesb32"][32 * hh:32 * hh + 1, 0:64],
                   ubf[32 * hh:32 * hh + 1, csl], True, True,
                   wait=[("dve", v_gr)] if hh == 0 and ch == 0 else ())
                v_rep = mm(ps_gr[64 * hh:64 * hh + 64, csl],
                           sb["onesb32"][32 * hh:32 * hh + 1, 0:64],
                           grbf[32 * hh:32 * hh + 1, csl], True, True,
                           inc="pe" if hh == 1 and ch == 3 else None)
        v_t1 = p.op("vector", lambda v: v.tensor_mul(t1buf[:], localf[:], ps_u[:]),
                    wait=[("pe", v_rep)])
        v_t2 = p.op("vector", lambda v, hp=hp: v.tensor_mul(
            rem2f[hp][:], rem2f[hp][:], ps_gr[:]))
        v_t3 = p.op("vector", lambda v, hp=hp: v.tensor_add(
            rem2f[hp][:], rem2f[hp][:], t1buf[:]))
        p.wait("tensor", [("dve", v_t3)])
        pss.close()
        p.barrier()

    def b_out(s):
        rem2f = [wview(c * 4096, [128, S], BF16, "rem2f") for c in range(NC)]
        woutl = [wview(32768 + i * 2048, [128, D], BF16, "woutl") for i in range(NC)]
        for i in range(NC):
            p.dma("sync", lambda e, i=i: e.dma_start(
                out=woutl[i][:], in_=st["st_wout"][i]))
        gml = p.dmark()
        p.wait("tensor", gml)
        pss = PsumScope()
        po = [pss.get([128, 512]) for _ in range(2)]
        ep_done = {}
        for oc in range(NC):
            for tg in range(4):
                tsl = slice(512 * tg, 512 * (tg + 1))
                idx = oc * 4 + tg
                pk = po[idx % 2]
                wv = [("dve", p.val("dve"))] if idx == 0 else ()
                if idx >= 2:
                    wv = [("dve", ep_done[idx - 2])]
                v_mm = None
                for i in range(NC):
                    v_mm = mm(pk[:], woutl[i][:, 128 * oc:128 * (oc + 1)],
                              rem2f[i][:, tsl], i == 0, i == NC - 1,
                              wait=wv if i == 0 else (),
                              inc="pe" if i == NC - 1 else None)
                v_ep = p.op("vector", lambda v, oc=oc, tsl=tsl, pk=pk:
                            v.tensor_add(X[oc][:, tsl], X[oc][:, tsl], pk[:]),
                            wait=[("pe", v_mm)], inc="dve")
                ep_done[idx] = v_ep
        p.barrier()
        pss.close()

    prep_consts()
    if ENABLE["atta"] and "noaprep" not in DBG:
        prep_atta()
    if ENABLE["ffn"] and "noprep" not in DBG:
        prep_ffn()
    if ENABLE["attb"] and "nobprep" not in DBG:
        prep_attb()
    for s in range(n_seq):
        load_x(s)
        if ENABLE["atta"] and "noasub" not in DBG:
            atta_sub(s)
        if ENABLE["ffn"] and "nosub" not in DBG:
            ffn_sub(s)
        if ENABLE["attb"] and "nobsub" not in DBG:
            attb_sub(s)
        if ENABLE["ffn"] and "nosub" not in DBG and "one" not in DBG:
            ffn_sub(s)
        store_y(s)
    p.wait("sync", p.dmark())
    p.wait("gpsimd", p.dmark())

    p.finalize(block, sems)
    est.close()
    return nc



WMAP = {
    "aa_qn_w": "L0_attn_a_qn_w", "aa_qn_b": "L0_attn_a_qn_b",
    "aa_q_w": "L0_attn_a_q_w", "aa_q_b": "L0_attn_a_q_b",
    "aa_kvn_w": "L0_attn_a_kvn_w", "aa_kvn_b": "L0_attn_a_kvn_b",
    "aa_kv_w": "L0_attn_a_kv_w", "aa_kv_b": "L0_attn_a_kv_b",
    "aa_out_w": "L0_attn_a_out_w", "aa_out_b": "L0_attn_a_out_b",
    "aa_ln_w": "L0_attn_a_ln_w", "aa_ln_b": "L0_attn_a_ln_b",
    "aa_rh_w": "L0_attn_a_rh_w", "aa_rh_b": "L0_attn_a_rh_b",
    "ab_wq": "L0_attn_b_wq", "ab_wk": "L0_attn_b_wk", "ab_wv": "L0_attn_b_wv",
    "ab_wgq": "L0_attn_b_wgq", "ab_wgk": "L0_attn_b_wgk", "ab_wgv": "L0_attn_b_wgv",
    "ab_r_out_w": "L0_attn_b_r_out_w", "ab_glw": "L0_attn_b_gate_local_w",
    "ab_grw": "L0_attn_b_gate_remote_w", "ab_w_out": "L0_attn_b_w_out",
    "ffn_wr": "L0_ffn_w_freq_real", "ffn_wi": "L0_ffn_w_freq_imag",
    "ffn_bias": "L0_ffn_bias",
}


def kernel(**inputs):
    ht = host_tables()
    ht["tab_glwT"] = np.ascontiguousarray(np.concatenate(
        [np.asarray(inputs["L0_attn_b_gate_local_w"]).T,
         np.asarray(inputs["L0_attn_b_gate_remote_w"]).T], axis=1).astype(np.float32))
    nc = build_program()
    x = np.ascontiguousarray(inputs["L0_attn_a_x"])
    in_maps = []
    for core in range(NCORES):
        m = {"xs": np.ascontiguousarray(x[core * NSEQ:(core + 1) * NSEQ])}
        for k, src in WMAP.items():
            m[k] = np.ascontiguousarray(inputs[src])
        m.update(ht)
        in_maps.append(m)
    res = run_bass_kernel_spmd(nc, in_maps, core_ids=list(range(NCORES)))
    return np.concatenate([r["ys"] for r in res.results], axis=0)


# version 15
# speedup vs baseline: 1.1385x; 1.0103x over previous
import math
from collections import defaultdict
from contextlib import ExitStack

import numpy as np
import ml_dtypes

import concourse.bass as bass
import concourse.mybir as mybir
from concourse.bass_utils import run_bass_kernel_spmd

F32 = mybir.dt.float32
F32R = mybir.dt.float32r
BF16 = mybir.dt.bfloat16
AF = mybir.ActivationFunctionType
ALU = mybir.AluOpType
AX = mybir.AxisListType

D = 1024
S = 2048
B = 32
NCORES = 8
NSEQ = B // NCORES
NH = 16
HD = 64
NLH = 8
LDIM = 512
NLAT = 64
POOLC = 32
WIN = 256
EPS = 1e-5
NT = S // 128
NC = D // 128
NG = 4
HPG = NH // NG
QAW = 272

import os as _os
ENABLE = {"atta": True, "ffn": True, "attb": True}
DBG = _os.environ.get("KDBG", "")



def host_tables():
    t = {}
    d2 = HD // 2
    mel_max = 2595.0 * np.log10(1.0 + 4000.0 / 200.0)
    scale = np.power(10.0, np.linspace(0.0, mel_max, d2, dtype=np.float64) / 2595.0) - 1.0
    fb = 200.0 * scale / 1000.0
    freqs = np.arange(S, dtype=np.float64)[:, None] * fb[None, :]
    c = np.cos(freqs); s = np.sin(freqs)
    C1 = np.zeros((S, HD), np.float32); C2 = np.zeros((S, HD), np.float32)
    CS = np.zeros((S, HD), np.float32)
    C1[:, 0::2] = c; C1[:, 1::2] = s
    C2[:, 0::2] = s; C2[:, 1::2] = c
    CS[:, 0::2] = c + s; CS[:, 1::2] = c - s
    t["tab_c1"] = C1; t["tab_c2"] = C2; t["tab_cs"] = CS
    ki = np.arange(128)[:, None]; qi = np.arange(128)[None, :]
    lm = np.zeros((128, 384), np.float32)
    lm[:, 0:128] = (qi < ki)
    lm[:, 128:256] = 1.0
    lm[:, 256:384] = (ki <= qi)
    t["tab_localmask"] = lm.astype(ml_dtypes.bfloat16)
    n = np.arange(NLAT)[:, None]; tt = np.arange(S)[None, :]
    t["tab_latmask"] = (n <= (tt // POOLC)).astype(np.float32).astype(ml_dtypes.bfloat16)
    t["tab_ident"] = np.eye(128, dtype=np.float32).astype(ml_dtypes.bfloat16)
    t["tab_identf"] = np.eye(128, dtype=np.float32)
    ab = np.arange(D, dtype=np.float64)
    ang = (np.outer(ab, ab) % D) * (2.0 * np.pi / D)
    t["tab_fcos"] = np.cos(ang).astype(np.float32)
    t["tab_fsin"] = np.sin(ang).astype(np.float32)
    e16 = np.zeros((16, 8 * 128), np.float32)
    for cc in range(8):
        e16[2 * cc, cc * 128: cc * 128 + 64] = 1.0
        e16[2 * cc + 1, cc * 128 + 64: cc * 128 + 128] = 1.0
    t["tab_e16"] = e16.astype(ml_dtypes.bfloat16)
    t0 = np.zeros((128, 4), np.float32)
    for j in range(4):
        t0[32 * j:32 * (j + 1), j] = 1.0
    t["tab_t0"] = t0.astype(ml_dtypes.bfloat16)
    return t



class Prog:
    ENGINES = ("tensor", "vector", "scalar", "gpsimd", "sync")
    DMA_Q = ("q0", "q1", "q2", "q3", "q4", "q5")
    DMA_G = ("g0", "g1", "g2", "g3")

    def __init__(self, nc):
        self.nc = nc
        self.q = {e: [] for e in self.ENGINES}
        self.cnt = defaultdict(int)
        self.rr = {"sync": 0, "gpsimd": 0}

    def val(self, sem):
        return self.cnt[sem]

    AUTO = {"vector": "dve", "scalar": "act", "gpsimd": "pool"}

    def op(self, eng, fn, wait=(), inc=None, amt=1, selfsync=True):
        if fn is not None and selfsync and eng in self.AUTO and inc is None:
            inc = self.AUTO[eng]
            amt = 1
        self.q[eng].append((fn, [(s, v) for (s, v) in wait if v > 0], inc, amt))
        if inc is not None:
            self.cnt[inc] += amt
            v = self.cnt[inc]
            if fn is not None and selfsync and eng in self.AUTO:
                self.q[eng].append(("W", [(inc, v)], None, 0))
            return v
        return None

    def dma(self, eng, fn, wait=()):
        pool = self.DMA_G if eng == "gpsimd" else self.DMA_Q
        sem = pool[self.rr[eng] % len(pool)]
        self.rr[eng] += 1
        prev = self.cnt[sem]
        w = [(s, v) for (s, v) in wait if v > 0]
        if prev:
            w.append((sem, prev))
        v = self.op(eng, fn, wait=w, inc=sem, amt=16, selfsync=False)
        return (sem, v)

    def dmark(self):
        return [(s, self.cnt[s]) for s in self.DMA_Q + self.DMA_G if self.cnt[s]]

    def barrier(self):
        vals = {}
        for e in ("tensor", "vector", "scalar", "gpsimd"):
            vals[e] = self.op(e, None, inc="b_" + e)
        for e in self.ENGINES:
            w = [("b_" + o, vals[o]) for o in vals if o != e]
            self.q[e].append(("W", w, None, 0))

    def vop(self, fn):
        vv = self.op("vector", fn, inc="dve")
        self.q["vector"].append(("W", [("dve", vv)], None, 0))
        return vv

    def wait(self, eng, wait):
        self.q[eng].append(("W", [(s, v) for (s, v) in wait if v > 0], None, 0))

    def finalize(self, block, sems):
        for ename in self.ENGINES:
            entries = self.q[ename]

            def run(engine, entries=entries, ename=ename):
                for fn, waits, inc, amt in entries:
                    for smn, v in waits:
                        engine.wait_ge(sems[smn], v)
                    if fn == "W":
                        continue
                    if fn is None:
                        assert inc is not None
                        engine.sem_inc(sems[inc], amt)
                        continue
                    instr = fn(engine)
                    if inc is not None:
                        assert instr is not None, f"{ename} op must return instr for {inc}"
                        instr.then_inc(sems[inc], amt)

            getattr(block, ename)(run)



def build_program(n_seq=NSEQ):
    nc = bass.Bass("TRN2", target_bir_lowering=False)

    xin = nc.dram_tensor("xs", [n_seq, S, D], F32, kind="ExternalInput").ap()
    yout = nc.dram_tensor("ys", [n_seq, S, D], F32, kind="ExternalOutput").ap()
    dbg = {}
    if "dump" in DBG:
        dbg["rows"] = nc.dram_tensor("dbg_rows", [1, 5 * S], BF16, kind="ExternalOutput").ap()
        dbg["rrep"] = nc.dram_tensor("dbg_rrep", [128, S], F32, kind="ExternalOutput").ap()
        dbg["colst"] = nc.dram_tensor("dbg_colst", [128, 64], F32, kind="ExternalOutput").ap()
        dbg["chain"] = nc.dram_tensor("dbg_chain", [128, 64], F32, kind="ExternalOutput").ap()
        dbg["r1"] = nc.dram_tensor("dbg_r1", [128, 512], F32, kind="ExternalOutput").ap()
        dbg["ops"] = nc.dram_tensor("dbg_ops", [128, 512], F32, kind="ExternalOutput").ap()
        dbg["qnf"] = nc.dram_tensor("dbg_qnf", [128, S], BF16, kind="ExternalOutput").ap()
        dbg["knf"] = nc.dram_tensor("dbg_knf", [128, S], BF16, kind="ExternalOutput").ap()
        dbg["vb"] = nc.dram_tensor("dbg_vb", [128, NT * 260], BF16, kind="ExternalOutput").ap()
        dbg["pb"] = nc.dram_tensor("dbg_pb", [128, 1024], BF16, kind="ExternalOutput").ap()
        dbg["stkv"] = nc.dram_tensor("dbg_stkv", [128, 2048], BF16, kind="ExternalOutput").ap()
        dbg["stqw"] = nc.dram_tensor("dbg_stqw", [128, 1040], BF16, kind="ExternalOutput").ap()
        dbg["strows"] = nc.dram_tensor("dbg_strows", [1, 16384], BF16, kind="ExternalOutput").ap()
        dbg["csprep"] = nc.dram_tensor("dbg_csprep", [1, D], BF16, kind="ExternalOutput").ap()

    win = {}
    for nm, shp in [
        ("aa_qn_w", [D]), ("aa_qn_b", [D]), ("aa_q_w", [D, D]), ("aa_q_b", [D]),
        ("aa_kvn_w", [D]), ("aa_kvn_b", [D]), ("aa_kv_w", [D, 2 * D]), ("aa_kv_b", [2 * D]),
        ("aa_out_w", [D, D]), ("aa_out_b", [D]), ("aa_ln_w", [HD]), ("aa_ln_b", [HD]),
        ("aa_rh_w", [D, 1]), ("aa_rh_b", [1]),
        ("ab_wq", [D, D]), ("ab_wk", [D, D]), ("ab_wv", [D, D]),
        ("ab_wgq", [D, LDIM]), ("ab_wgk", [D, LDIM]), ("ab_wgv", [D, LDIM]),
        ("ab_r_out_w", [LDIM, D]), ("ab_glw", [NH, HD]), ("ab_grw", [NH, HD]),
        ("ab_w_out", [D, D]),
        ("ffn_wr", [D, D]), ("ffn_wi", [D, D]), ("ffn_bias", [D]),
    ]:
        win[nm] = nc.dram_tensor(nm, shp, F32, kind="ExternalInput").ap()

    tabs = {}
    for nm, shp, dt_ in [
        ("tab_c1", [S, HD], F32), ("tab_c2", [S, HD], F32), ("tab_cs", [S, HD], F32),
        ("tab_localmask", [128, 384], BF16), ("tab_latmask", [NLAT, S], BF16),
        ("tab_ident", [128, 128], BF16), ("tab_identf", [128, 128], F32),
        ("tab_fcos", [D, D], F32),
        ("tab_fsin", [D, D], F32), ("tab_e16", [16, 8 * 128], BF16),
        ("tab_t0", [128, 4], BF16), ("tab_glwT", [HD, 2 * NH], F32),
    ]:
        tabs[nm] = nc.dram_tensor(nm, shp, dt_, kind="ExternalInput").ap()

    st = {}
    for nm, shp, dt_ in [
        ("st_qw", [NC, 128, 1040], BF16),
        ("st_kv", [NC, 128, 2 * D], BF16),
        ("st_outw", [NC, 128, D], BF16),
        ("st_M", [NC, 128, D], F32R),
        ("st_wq", [NC, 128, 1152], BF16),
        ("st_wk", [NC, 128, D], BF16),
        ("st_wv", [NC, 128, D], BF16),
        ("st_wgq", [NC, 128, LDIM], BF16),
        ("st_wgk", [NC, 128, LDIM], BF16),
        ("st_wgv", [NC, 128, LDIM], BF16),
        ("st_rout", [LDIM // 128, 128, D], BF16),
        ("st_wout", [NC, 128, D], BF16),
        ("st_rows", [1, 16384], BF16),
        ("st_mrow", [1, S], BF16),
        ("st_mrow64", [1, S], BF16),
        ("st_irow64", [1, S], BF16),
        ("st_xbf", [NC, 128, S], BF16),
    ]:
        st[nm] = nc.dram_tensor(nm, shp, dt_, kind="Internal").ap()

    RO = {}
    _o = [0]
    for nm, n in [("colsum_q", 1040), ("qb", 1040), ("colsum_kv", 2 * D),
                  ("kvb", 2 * D), ("colsum_M", D), ("ffn_bias", D),
                  ("colsum_wq", 1152), ("colsum_wk", D), ("colsum_wv", D),
                  ("colsum_wgq", LDIM), ("colsum_wgk", LDIM), ("colsum_wgv", LDIM)]:
        RO[nm] = _o[0]; _o[0] += n
    assert _o[0] <= 16384

    sb = {}
    def salloc(nm, shp, dt_):
        sb[nm] = nc.alloc_sbuf_tensor(nm, shp, dt_)
        return sb[nm]

    salloc("ident", [128, 128], BF16)
    salloc("identf", [128, 128], F32)
    salloc("lmask", [128, 384], BF16)
    salloc("wrep", [128, HD], F32)
    salloc("brep", [128, HD], F32)
    salloc("cbias", [128, 1], F32)
    salloc("rhbrep", [128, 1], F32)
    salloc("onesbf", [2, 128], BF16)
    salloc("onecolbf", [128, 1], BF16)
    salloc("onecolr", [128, 1], F32R)
    salloc("epscol", [128, 1], F32)
    salloc("colst", [128, 64], F32)
    salloc("glwt", [128, 2 * NH], BF16)
    salloc("t0", [128, 4], BF16)
    salloc("onesf32", [33, 64], F32)
    salloc("onesb32", [33, 64], BF16)
    CO = {"R2": 0, "r": 16, "rad": 32, "tmp": 48}
    salloc("rows", [1, 5 * S], BF16)
    RB = {"Rh": 0, "Rl": S, "m": 2 * S, "mR": 3 * S, "invR2": 4 * S}

    X = [salloc(f"X{c}", [128, S], F32) for c in range(NC)]
    AAB = 64 * 1024
    WAB = 56 * 1024
    aarena = nc.alloc_sbuf_tensor("aarena", [128, AAB], mybir.dt.uint8)
    warena = nc.alloc_sbuf_tensor("warena", [128, WAB], mybir.dt.uint8)
    AADDR = nc.lookup_mloc(aarena).addr
    WADDR = nc.lookup_mloc(warena).addr

    _vc = [0]
    _vcache = {}
    def aview(off, shp, dt_, nm="v"):
        key = ("a", off, tuple(shp), dt_)
        if key in _vcache:
            return _vcache[key]
        _vc[0] += 1
        esz = {F32: 4, F32R: 4, BF16: 2, mybir.dt.uint8: 1}[dt_]
        need = int(np.prod(shp[1:])) * esz
        assert off + need <= AAB, f"aview {nm} overflow {off}+{need}"
        t = nc.alloc_sbuf_tensor_at(f"a{_vc[0]}_{nm}", shp, dt_, offset=AADDR + off)
        _vcache[key] = t
        return t

    def wview(off, shp, dt_, nm="v"):
        key = ("w", off, tuple(shp), dt_)
        if key in _vcache:
            return _vcache[key]
        _vc[0] += 1
        esz = {F32: 4, F32R: 4, BF16: 2}[dt_]
        need = int(np.prod(shp[1:])) * esz
        assert off + need <= WAB, f"wview {nm} overflow {off}+{need}"
        t = nc.alloc_sbuf_tensor_at(f"w{_vc[0]}_{nm}", shp, dt_, offset=WADDR + off)
        _vcache[key] = t
        return t

    p = Prog(nc)
    est = ExitStack()
    sems = {}
    for smn in ["q0", "q1", "q2", "q3", "q4", "q5", "g0", "g1", "g2", "g3",
                "pe", "dve", "act", "pool", "free",
                "b_tensor", "b_vector", "b_scalar", "b_gpsimd"]:
        sems[smn] = est.enter_context(nc.semaphore(smn))
    block = est.enter_context(nc.Block())

    def mm(out_ap, lhsT_ap, rhs_ap, start, stop, wait=(), inc=None, tp=None):
        return p.op("tensor",
                    lambda t, o=out_ap, l=lhsT_ap, r=rhs_ap:
                    t.matmul(o, l, r, start=start, stop=stop, tile_position=tp),
                    wait=wait, inc=inc)

    def tr(out_ap, in_ap, wait=(), inc=None):
        np_ = in_ap.shape[0]
        base = sb["ident"] if in_ap.dtype != F32 else sb["identf"]
        idt = base[0:np_, 0:np_]
        return p.op("tensor", lambda t, o=out_ap, i=in_ap, idt=idt: t.transpose(o, i, idt),
                    wait=wait, inc=inc)

    _psn = [0]
    class PsumScope:
        def __init__(self):
            self.guards = []
            self.tensors = []
        def get(self, shape, dt_=F32):
            _psn[0] += 1
            g = nc.psum_tensor(f"ps{_psn[0]}", shape, dt_)
            t = g.__enter__()
            self.guards.append(g)
            self.tensors.append(t)
            return t
        def close(self):
            for g in reversed(self.guards):
                g.__exit__(None, None, None)
            self.guards = []

    def load_x(s):
        pss = PsumScope()
        pst = [pss.get([128, 128]) for _ in range(4)]
        tmb = [aview(0, [128, D], F32, "ldtm0"), aview(4096, [128, D], F32, "ldtm1")]
        ld_done = []
        for t in range(NT):
            bi = t % 2
            w = [("dve", ld_done[t - 2])] if t >= 2 else []
            v_l = p.dma("sync", lambda e, t=t, bi=bi: e.dma_start(
                out=tmb[bi][:], in_=xin[s].rearrange("(t p) d -> t p d", p=128)[t]),
                wait=w)
            for c in range(NC):
                pk = pst[(t * NC + c) % 4]
                wv = [v_l]
                if t * NC + c >= 4:
                    wv.append(("dve", p.val("dve") - 3))
                v_t = tr(pk[:], tmb[bi][:, 128 * c:128 * (c + 1)],
                         wait=wv, inc="pe")
                v_e = p.op("vector", lambda v, c=c, t=t, pk=pk: v.tensor_copy(
                    X[c][:, 128 * t:128 * (t + 1)], pk[:]),
                    wait=[("pe", v_t)], inc="dve")
            ld_done.append(p.val("dve"))
        for c in range(NC):
            p.dma("gpsimd", lambda e, c=c: e.dma_start(
                out=st["st_xbf"][c], in_=X[c][:]),
                wait=[("dve", p.val("dve"))] if c == 0 else ())
        p.barrier()
        pss.close()

    def store_y(s):
        p.barrier()
        pss = PsumScope()
        pst = [pss.get([128, 128]) for _ in range(4)]
        tmb = [aview(0, [128, D], F32, "sttm0"), aview(4096, [128, D], F32, "sttm1")]
        st_dx = []
        for t in range(NT):
            bi = t % 2
            for c in range(NC):
                pk = pst[(t * NC + c) % 4]
                wv = []
                if t >= 2:
                    wv.append(st_dx[t - 2])
                if t * NC + c >= 4:
                    wv.append(("dve", p.val("dve") - 3))
                v_t = tr(pk[:], X[c][:, 128 * t:128 * (t + 1)],
                         wait=wv, inc="pe")
                p.op("vector", lambda v, c=c, bi=bi, pk=pk: v.tensor_copy(
                    tmb[bi][:, 128 * c:128 * (c + 1)], pk[:]),
                    wait=[("pe", v_t)], inc="dve")
            v_s = p.dma("sync", lambda e, t=t, bi=bi: e.dma_start(
                out=yout[s].rearrange("(t p) d -> t p d", p=128)[t], in_=tmb[bi][:]),
                wait=[("dve", p.val("dve"))])
            st_dx.append(v_s)
        pss.close()


    def prep_consts():
        p.dma("sync", lambda e: e.dma_start(out=sb["ident"][:], in_=tabs["tab_ident"]))
        p.dma("sync", lambda e: e.dma_start(out=sb["identf"][:], in_=tabs["tab_identf"]))
        p.dma("sync", lambda e: e.dma_start(out=sb["lmask"][:], in_=tabs["tab_localmask"]))
        p.dma("sync", lambda e: e.dma_start(out=sb["t0"][:], in_=tabs["tab_t0"]))
        vqm = p.dmark()
        def vop(fn):
            vv = p.op("vector", fn, inc="dve")
            p.wait("vector", [("dve", vv)])
            return vv
        vop(lambda v_: v_.memset(sb["colst"][:], 0.0))
        vop(lambda v_: v_.memset(sb["onesf32"][:], 1.0))
        vop(lambda v_: v_.memset(sb["onesb32"][:], 1.0))
        vop(lambda v_: v_.memset(sb["onesbf"][:], 0.0))
        vop(lambda v_: v_.memset(sb["onesbf"][0:1, :], 1.0))
        vop(lambda v_: v_.memset(sb["onecolbf"][:], 1.0))
        vop(lambda v_: v_.memset(sb["epscol"][:], 1.0))
        vop(lambda v_: v_.tensor_copy(sb["onecolr"][:], sb["epscol"][:]))
        vop(lambda v_: v_.memset(sb["epscol"][:], EPS))
        p.op("vector", lambda v_: v_.engine_nop(), inc="dve")
        p.wait("tensor", vqm)
        p.wait("vector", vqm)
        p.barrier()

    def prep_ffn():
        pss = PsumScope()
        ps_a = pss.get([128, 512])
        ps_b = pss.get([128, 512])
        ps_row = pss.get([1, 1024])
        cosr = [wview(c * 4096, [128, D], F32R, "cos") for c in range(NC)]
        sinr = [aview(c * 4096, [128, D], F32R, "sin") for c in range(NC)]
        UTR = [aview(32768 + c * 2048, [128, 512], F32R, "utr") for c in range(NC)]
        UTI = [aview(49152 + c * 2048, [128, 512], F32R, "uti") for c in range(NC)]
        wrsl = [wview(32768 + c * 512, [128, 128], F32R, "wrsl") for c in range(NC)]
        wisl = [wview(36864 + c * 512, [128, 128], F32R, "wisl") for c in range(NC)]
        ldb = wview(40960, [128, D], F32, "mld")
        ldb2 = wview(45056, [128, 128], F32, "mld2")
        evb = wview(45568, [128, 512], F32, "mev")
        evb2 = wview(47616, [128, 512], F32, "mev2")
        mst = wview(49664, [128, 512], F32R, "mst")
        rowbf = wview(51712, [2, 1024], BF16, "mrowbf")

        for nm, dstl in (("tab_fcos", cosr), ("tab_fsin", sinr)):
            for c in range(NC):
                v_l = p.dma("sync", lambda e, c=c, nm=nm: e.dma_start(
                    out=ldb[:], in_=tabs[nm].rearrange("(c p) o -> c p o", p=128)[c]),
                    wait=[("dve", p.val("dve"))])
                p.op("vector", lambda v, c=c, dstl=dstl: v.tensor_copy(dstl[c][:], ldb[:]),
                     wait=[v_l], inc="dve")

        if "mcs" in DBG:
            p.barrier()
            pss.close()
            return
        for h in range(2 if "ms1" not in DBG else 1):
            dsl = slice(512 * h, 512 * (h + 1))
            for jc in range(NC):
                for kc in range(NC):
                    v_l = p.dma("sync", lambda e, jc=jc, kc=kc: e.dma_start(
                        out=ldb2[:], in_=win["ffn_wr"].rearrange(
                            "(kc p) o -> kc p o", p=128)[kc][:, 128 * jc:128 * (jc + 1)]),
                        wait=[("pe", p.val("pe")), ("dve", p.val("dve"))])
                    p.op("vector", lambda v, kc=kc: v.tensor_copy(wrsl[kc][:], ldb2[:]),
                         wait=[v_l], inc="dve")
                    v_l = p.dma("sync", lambda e, jc=jc, kc=kc: e.dma_start(
                        out=ldb2[:], in_=win["ffn_wi"].rearrange(
                            "(kc p) o -> kc p o", p=128)[kc][:, 128 * jc:128 * (jc + 1)]),
                        wait=[("dve", p.val("dve"))])
                    p.op("vector", lambda v, kc=kc: v.tensor_copy(wisl[kc][:], ldb2[:]),
                         wait=[v_l], inc="dve")
                v_w = p.val("dve")
                wv = [("dve", v_w)]
                for kc in range(NC):
                    mm(ps_a[:], wrsl[kc][:], cosr[kc][:, dsl], kc == 0, False, wait=wv)
                    wv = ()
                v1 = None
                for kc in range(NC):
                    v1 = mm(ps_a[:], wisl[kc][:], sinr[kc][:, dsl], False, kc == NC - 1,
                            inc="pe" if kc == NC - 1 else None)
                v_e1 = p.op("vector", lambda v, jc=jc: v.tensor_copy(
                    UTR[jc][:], ps_a[:]), wait=[("pe", v1)], inc="dve")
                v2 = None
                for kc in range(NC):
                    v2 = mm(ps_b[:], wisl[kc][:], cosr[kc][:, dsl], kc == 0, kc == NC - 1,
                            inc="pe" if kc == NC - 1 else None)
                v_e2 = p.op("vector", lambda v: v.tensor_copy(evb[:], ps_b[:]),
                            wait=[("pe", v2)], inc="dve")
                wv = [("dve", v_e1)]
                v3 = None
                for kc in range(NC):
                    v3 = mm(ps_a[:], wrsl[kc][:], sinr[kc][:, dsl], kc == 0, kc == NC - 1,
                            wait=wv, inc="pe" if kc == NC - 1 else None)
                    wv = ()
                v_u = p.op("vector", lambda v: v.scalar_tensor_tensor(
                    evb2[:], ps_a[:], -1.0, evb[:], ALU.mult, ALU.add),
                    wait=[("pe", v3), ("dve", v_e2)], inc="dve")
                p.op("vector", lambda v, jc=jc: v.tensor_copy(UTI[jc][:], evb2[:]),
                     wait=[("dve", v_u)], inc="dve")
            for dci in range(4 if "ms1" not in DBG else 0):
                dc = 4 * h + dci
                for og in range(2):
                    osl = slice(512 * og, 512 * (og + 1))
                    wv = [("dve", p.val("dve"))]
                    v1 = None
                    for jc in range(NC):
                        v1 = mm(ps_a[:], UTR[jc][:, 128 * dci:128 * (dci + 1)],
                                cosr[jc][:, osl], jc == 0, jc == NC - 1, wait=wv,
                                inc="pe" if jc == NC - 1 else None)
                        wv = ()
                    v_e = p.op("vector", lambda v: v.tensor_scalar(
                        evb[:], ps_a[:], 1.0 / D, None, ALU.mult),
                        wait=[("pe", v1)], inc="dve")
                    v2 = None
                    for jc in range(NC):
                        v2 = mm(ps_b[:], UTI[jc][:, 128 * dci:128 * (dci + 1)],
                                sinr[jc][:, osl], jc == 0, jc == NC - 1,
                                inc="pe" if jc == NC - 1 else None)
                    v_m0 = p.op("vector", lambda v: v.scalar_tensor_tensor(
                        evb2[:], ps_b[:], -1.0 / D, evb[:], ALU.mult, ALU.add),
                        wait=[("pe", v2), ("dve", v_e)], inc="dve")
                    v_m = p.op("vector", lambda v: v.tensor_copy(mst[:], evb2[:]),
                               wait=[("dve", v_m0)], inc="dve")
                    v_st = p.dma("sync", lambda e, dc=dc, osl=osl: e.dma_start(
                        out=st["st_M"][dc][:, osl], in_=mst[:]), wait=[("dve", v_m)])
                    p.wait("vector", [v_st])
        if "ms1" not in DBG:
            mld3 = wview(0, [128, D], F32R, "csmld")
            v_cs = None
            for dc in range(NC):
                v_l = p.dma("sync", lambda e, dc=dc: e.dma_start(
                    out=mld3[:], in_=st["st_M"][dc]),
                    wait=[("pe", v_cs)] if v_cs else [("pe", p.val("pe"))])
                for og in range(2):
                    osl = slice(512 * og, 512 * (og + 1))
                    v_cs = mm(ps_row[0:1, osl], sb["onecolr"][:], mld3[:, osl],
                              dc == 0, dc == NC - 1, wait=[v_l],
                              inc="pe" if og == 1 else None)
            p.wait("vector", [("pe", p.val("pe"))])
            v_n = p.op("vector", lambda v: v.tensor_scalar(
                rowbf[0:1, :], ps_row[0:1, :], -1.0 / 128.0, None, ALU.mult), inc="dve")
            p.wait("vector", [("dve", v_n)])
            v_fin = p.dma("sync", lambda e: e.dma_start(
                out=st["st_rows"][:, RO["colsum_M"]:RO["colsum_M"] + D],
                in_=rowbf[0:1, :]), wait=[("dve", v_n)])
            p.wait("gpsimd", [v_fin])
            if "dump" in DBG:
                p.dma("sync", lambda e: e.dma_start(out=dbg["csprep"],
                                                    in_=rowbf[0:1, :]))
        p.wait("sync", p.dmark())
        p.wait("gpsimd", p.dmark())
        p.barrier()
        pss.close()

    AO_RREP = 57344

    def ln_stats(s, attn_a_mode, rrep_flag=None):
        if rrep_flag is None:
            rrep_flag = not attn_a_mode
        """Compute col stats (colst) + bf16 rows (sb.rows) + RREP for sublayer."""
        pss = PsumScope()
        ps_s1 = pss.get([1, S])
        ps_s2 = pss.get([1, S])
        xbf = [aview(0, [128, 512], BF16, "stx0"), aview(1024, [128, 512], BF16, "stx1")]
        x2 = [aview(2048, [128, 512], BF16, "stq0"), aview(3072, [128, 512], BF16, "stq1")]
        rowsc = aview(4096, [1, S], F32, "strow")
        rowsc2 = aview(12288, [1, S], F32, "strow2")
        mcol = aview(20480, [128, 16], F32, "mcol")
        e2col = aview(20544, [128, 16], F32, "e2col")
        tcol = aview(20608, [128, 16], F32, "tcol")
        t2col = aview(20672, [128, 16], F32, "t2col")
        tbf = aview(20736, [128, 16], BF16, "tbfcol")
        mR2col = aview(20800, [128, 16], F32, "mR2col")
        invR2col = aview(20864, [128, 16], F32, "invR2col")
        mRcol = aview(20928, [128, 16], F32, "mRcol")
        rlcol = aview(20992, [128, 16], F32, "rlcol")
        i = 0
        prev_pe = {}
        for tg in range(4):
            tsl = slice(512 * tg, 512 * (tg + 1))
            for c in range(NC):
                bi = i % 2
                wv = []
                if i >= 2 and (i - 2) in prev_pe:
                    wv.append(("pe", prev_pe[i - 2]))
                v_cast = p.op("vector", lambda v, c=c, tsl=tsl, bi=bi: v.tensor_copy(
                    xbf[bi][:], X[c][:, tsl]), wait=wv, inc="dve")
                v_sq = p.op("gpsimd", lambda g, bi=bi: g.tensor_mul(
                    x2[bi][:], xbf[bi][:], xbf[bi][:]), wait=[("dve", v_cast)],
                    inc="pool")
                mm(ps_s1[0:1, tsl], sb["onecolbf"][:], xbf[bi][:], c == 0, c == NC - 1,
                   wait=[("dve", v_cast)])
                v_pe = mm(ps_s2[0:1, tsl], sb["onecolbf"][:], x2[bi][:],
                          c == 0, c == NC - 1, wait=[("pool", v_sq)], inc="pe")
                prev_pe[i] = v_pe
                i += 1
        v_done = p.val("pe")
        v_r1 = p.op("vector", lambda v: v.tensor_copy(rowsc[0:1, :], ps_s1[0:1, :]),
                    wait=[("pe", v_done)], inc="dve")
        v_r2 = p.op("vector", lambda v: v.tensor_copy(rowsc2[0:1, :], ps_s2[0:1, :]),
                    inc="dve")
        pss.close()
        pss = PsumScope()
        colps = pss.get([128, 16])
        colps2 = pss.get([128, 16])
        rowps = pss.get([1, S])
        v_t1 = None
        for t in range(NT):
            v_t1 = tr(colps[:, t:t + 1], rowsc[0:1, 128 * t:128 * (t + 1)],
                      wait=[("dve", v_r1)] if t == 0 else (),
                      inc="pe" if t == NT - 1 else None)
        v_t2 = None
        for t in range(NT):
            v_t2 = tr(colps2[:, t:t + 1], rowsc2[0:1, 128 * t:128 * (t + 1)],
                      wait=[("dve", v_r2)] if t == 0 else (),
                      inc="pe" if t == NT - 1 else None)
        p.op("vector", lambda v: v.tensor_scalar(mcol[:], colps[:], 1.0 / D, None,
                                                 ALU.mult), wait=[("pe", v_t1)])
        v_c2 = p.op("vector", lambda v: v.tensor_scalar(e2col[:], colps2[:], 1.0 / D,
                                                        None, ALU.mult),
                    wait=[("pe", v_t2)], inc="dve")
        p.op("vector", lambda v: v.tensor_copy(sb["colst"][:, CO["tmp"]:CO["tmp"] + 16],
                                               mcol[:]))
        if "dump" in DBG:
            chm = aview(21184, [128, 16], F32, "chainm")
            che = aview(21248, [128, 16], F32, "chaine")
            p.op("vector", lambda v: v.tensor_copy(chm[:], mcol[:]))
            p.op("vector", lambda v: v.tensor_copy(che[:], e2col[:]))
        def vchain(fn):
            vv = p.op("vector", fn, inc="dve")
            p.wait("vector", [("dve", vv)])
            return vv
        vchain(lambda v: v.tensor_mul(tcol[:], mcol[:], mcol[:]))
        v_var = vchain(lambda v: v.tensor_sub(t2col[:], e2col[:], tcol[:]))
        v_ln = p.op("scalar", lambda s_: s_.activation(tcol[:], t2col[:], AF.Ln,
                                                       bias=sb["epscol"][:]),
                    wait=[("dve", v_var)], inc="act")
        p.wait("scalar", [("act", v_ln)])
        if "dump" in DBG:
            lndump = aview(21056, [128, 16], F32, "lndump")
            vardump = aview(21120, [128, 16], F32, "vardump")
            p.op("vector", lambda v: v.tensor_copy(vardump[:], t2col[:]),
                 wait=[("dve", v_var)])
            p.op("vector", lambda v: v.tensor_copy(lndump[:], tcol[:]),
                 wait=[("act", v_ln)])
        v_r = p.op("scalar", lambda s_: s_.activation(sb["colst"][:, CO["r"]:CO["r"] + 16],
                                                      tcol[:], AF.Exp, scale=-0.5),
                   inc="act")
        p.wait("vector", [("act", v_r)])
        rcol_ap = sb["colst"][:, CO["r"]:CO["r"] + 16]
        R2_ap = sb["colst"][:, CO["R2"]:CO["R2"] + 16]
        v_mR = vchain(lambda v: v.tensor_mul(mRcol[:], mcol[:], rcol_ap))
        kinds = [("m", mcol), ("mR", mRcol)]
        if attn_a_mode:
            vchain(lambda v: v.tensor_mul(tcol[:], rcol_ap, rcol_ap))
            v_q = vchain(lambda v: v.tensor_scalar(
                tcol[:], tcol[:], -EPS, 1.0 + EPS, ALU.mult, ALU.add))
            v_l2 = p.op("scalar", lambda s_: s_.activation(t2col[:], tcol[:], AF.Ln),
                        wait=[("dve", v_q)], inc="act")
            p.wait("scalar", [("act", v_l2)])
            v_r2 = p.op("scalar", lambda s_: s_.activation(tcol[:], t2col[:], AF.Exp,
                                                           scale=-0.5), inc="act")
            p.wait("vector", [("act", v_r2)])
            vchain(lambda v: v.tensor_mul(R2_ap, rcol_ap, tcol[:]))
            v_ir = vchain(lambda v: v.reciprocal(invR2col[:], R2_ap))
            kinds += [("invR2", invR2col)]
        else:
            vchain(lambda v: v.tensor_copy(tbf[:], rcol_ap))
            vchain(lambda v: v.tensor_copy(t2col[:], tbf[:]))
            v_rl = vchain(lambda v: v.tensor_sub(rlcol[:], rcol_ap, t2col[:]))
            kinds += [("Rh", t2col), ("Rl", rlcol)]
        v_pre = p.val("dve")
        for nm, colap in kinds:
            v_tt = None
            for t in range(NT):
                v_tt = tr(rowps[0:1, 128 * t:128 * (t + 1)], colap[:, t:t + 1],
                          wait=[("dve", v_pre)] if t == 0 else (),
                          inc="pe" if t == NT - 1 else None)
            v_ev = p.op("vector", lambda v, nm=nm: v.tensor_copy(
                sb["rows"][:, RB[nm]:RB[nm] + S], rowps[0:1, :]),
                wait=[("pe", v_tt)], inc="dve")
            p.wait("tensor", [("dve", v_ev)])
            if nm == "m":
                p.dma("sync", lambda e: e.dma_start(
                    out=st["st_mrow"], in_=sb["rows"][:, RB["m"]:RB["m"] + S]),
                    wait=[("dve", v_ev)])
                if "skipst64" in DBG:
                    continue
                sc64 = aview(21504, [1, S], BF16, "sc64row")
                v_s64 = p.vop(lambda v: v.tensor_scalar(
                    sc64[:], sb["rows"][:, RB["m"]:RB["m"] + S], 1.0 / 64.0, None,
                    ALU.mult))
                v_d64 = p.dma("sync", lambda e: e.dma_start(out=st["st_mrow64"],
                                                             in_=sc64[:]),
                              wait=[("dve", v_s64)])
            if nm == "invR2" and "skipst64" not in DBG:
                sc64b = aview(21504, [1, S], BF16, "sc64rowb")
                v_s64b = p.op("vector", lambda v: v.tensor_scalar(
                    sc64b[:], sb["rows"][:, RB["invR2"]:RB["invR2"] + S], 1.0 / 64.0,
                    None, ALU.mult), wait=[v_d64])
                v_d64b = p.dma("sync", lambda e: e.dma_start(out=st["st_irow64"],
                                                              in_=sc64b[:]),
                               wait=[("dve", v_s64b)])
        if rrep_flag:
            rrep = aview(AO_RREP, [128, S], F32, "rrep")
            v_rows = p.val("dve")
            for tg in range(4):
                tsl = slice(512 * tg, 512 * (tg + 1))
                mm(colps[:, 0:16] if False else rowps[0:1, 0:1], None, None, True, True) \
                    if False else None
                mm_ps = colps
            pss.close()
            pss = PsumScope()
            reps = pss.get([128, 512])
            for tg in range(4):
                tsl = slice(512 * tg, 512 * (tg + 1))
                mm(reps[:], sb["onesbf"][0:1, :], sb["rows"][:, RB["Rh"] + 512 * tg:
                                                            RB["Rh"] + 512 * (tg + 1)],
                   True, False, wait=[("dve", v_rows)] if tg == 0 else
                   [("dve", p.val("dve"))])
                v_mm = mm(reps[:], sb["onesbf"][0:1, :],
                          sb["rows"][:, RB["Rl"] + 512 * tg:RB["Rl"] + 512 * (tg + 1)],
                          False, True, inc="pe")
                p.op("vector", lambda v, tsl=tsl: v.tensor_copy(rrep[:, tsl], reps[:]),
                     wait=[("pe", v_mm)], inc="dve")
        p.wait("sync", p.dmark())
        p.wait("gpsimd", p.dmark())
        p.barrier()
        pss.close()

    def ffn_sub(s):
        ln_stats(s, attn_a_mode=False)
        if "dump" in DBG and s == 0:
            mcolv = aview(20480, [128, 16], F32, "mcoldump")
            e2colv = aview(20544, [128, 16], F32, "e2coldump")
            p.op("vector", lambda v: v.tensor_copy(sb["colst"][:, 32:48], mcolv[:]))
            p.op("vector", lambda v: v.tensor_copy(sb["colst"][:, 48:64], e2colv[:]))
            lndumpv = aview(21056, [128, 16], F32, "lndump2")
            vardumpv = aview(21120, [128, 16], F32, "vardump2")
            p.op("vector", lambda v: v.tensor_copy(sb["colst"][:, 32:48], lndumpv[:]))
            p.op("vector", lambda v: v.tensor_copy(sb["colst"][:, 48:64], vardumpv[:]), inc="dve")
            chainv = aview(21184, [128, 128], F32, "chaindump")
            p.dma("sync", lambda e: e.dma_start(out=dbg["chain"], in_=chainv[:, 0:64]))
            rrepv = aview(AO_RREP, [128, S], F32, "rrepdump")
            p.dma("sync", lambda e: e.dma_start(out=dbg["rows"], in_=sb["rows"][:]))
            p.dma("sync", lambda e: e.dma_start(out=dbg["rrep"], in_=rrepv[:]))
            p.dma("sync", lambda e: e.dma_start(out=dbg["colst"], in_=sb["colst"][:]))
            p.barrier()
        if "statonly" in DBG:
            return
        pss = PsumScope()
        po = [pss.get([128, 512]) for _ in range(2)]
        ps_rep = pss.get([128, 512])
        rrep = aview(AO_RREP, [128, S], F32, "rrepf")
        xr = [aview(c * 2048, [128, 512], F32R, "xr") for c in range(NC)]
        tbuf = [aview(16384, [128, 512], F32, "ft0"), aview(18432, [128, 512], F32, "ft1")]
        biascol = aview(20480, [128, 8], F32, "fbias")
        csMbf = aview(20544, [1, D], BF16, "csMbf")
        mrepr = aview(22592, [128, S], F32R, "mrepr")
        csMrep = aview(30784, [128, D], F32R, "csMrep")
        Mw = [wview(c * 4096, [128, D], F32R, "Mw") for c in range(NC)]
        v_bias = p.dma("gpsimd", lambda e: e.dma_start(
            out=biascol[:], in_=win["ffn_bias"].rearrange("(c p) -> p c", p=128),
            allow_slow_non_contiguous=True))
        v_csM = p.dma("sync", lambda e: e.dma_start(
            out=csMbf[:], in_=st["st_rows"][:, RO["colsum_M"]:RO["colsum_M"] + D]))
        vw = []
        for c in range(NC):
            vw.append(p.dma("sync", lambda e, c=c: e.dma_start(out=Mw[c][:],
                                                               in_=st["st_M"][c])))
        v_rows = p.op("vector", lambda v: v.engine_nop(), wait=[v_csM], inc="dve")
        v_rep1 = p.dma("gpsimd", lambda e: e.dma_start(
            out=csMrep[:], in_=st["st_rows"][0:1, RO["colsum_M"]:RO["colsum_M"] + D]
            .to_broadcast([128, D])))
        v_rep2 = p.dma("gpsimd", lambda e: e.dma_start(
            out=mrepr[:], in_=st["st_mrow"][0:1, :].to_broadcast([128, S])))
        p.wait("tensor", [v_rep1, v_rep2])
        if "dumpr1" in DBG:
            r1ev = aview(34880, [128, 512], F32, "r1ev")
            opev = aview(36928, [128, 512], F32, "opev")
            v_o1 = p.op("vector", lambda v: v.tensor_copy(opev[:, 0:256], mrepr[:, 0:256]),
                        wait=[v_rep2, v_rep1])
            v_o2 = p.op("vector", lambda v: v.tensor_copy(opev[:, 256:512], csMrep[:, 0:256]),
                        inc="dve")
            p.dma("sync", lambda e: e.dma_start(out=dbg["ops"], in_=opev[:]),
                  wait=[("dve", v_o2)])
            v_x = mm(ps_rep[:], csMrep[:, 0:128], mrepr[:, 0:512], True, True,
                     wait=[v_rep2, v_rep1], inc="pe")
            v_xe = p.op("vector", lambda v: v.tensor_copy(r1ev[:], ps_rep[:]),
                        wait=[("pe", v_x)], inc="dve")
            p.dma("sync", lambda e: e.dma_start(out=dbg["r1"], in_=r1ev[:]),
                  wait=[("dve", v_xe)])
        ep_done = {}
        for tg in range(4):
            tsl = slice(512 * tg, 512 * (tg + 1))
            wv = [("pe", p.val("pe"))] if tg > 0 else []
            v_cast = None
            for c in range(NC):
                v_cast = p.op("vector", lambda v, c=c, tsl=tsl: v.tensor_copy(
                    xr[c][:], X[c][:, tsl]), wait=wv, inc="dve")
                wv = []
            for oc in range(NC):
                pk = po[oc % 2]
                idx = tg * NC + oc
                wv = [("dve", v_cast)] + vw
                if idx >= 2:
                    wv.append(("dve", ep_done[idx - 2]))
                skipr1 = "nor1" in DBG
                for c in range(NC):
                    v_mm = mm(pk[:], Mw[c][:, 128 * oc:128 * (oc + 1)], xr[c][:],
                              c == 0, skipr1 and c == NC - 1, wait=wv if c == 0 else (),
                              inc="pe" if (skipr1 and c == NC - 1) else None)
                if not skipr1:
                    v_mm = mm(pk[:], csMrep[:, 128 * oc:128 * (oc + 1)], mrepr[:, tsl],
                              False, True, inc="pe")
                tb = tbuf[oc % 2]
                p.op("vector", lambda v, pk=pk, tb=tb, tsl=tsl: v.tensor_mul(
                    tb[:], pk[:], rrep[:, tsl]), wait=[("pe", v_mm), ("dve", v_rows)])
                v_ep = p.op("vector", lambda v, oc=oc, tb=tb, tsl=tsl:
                            v.scalar_tensor_tensor(X[oc][:, tsl], tb[:],
                                                   biascol[:, oc:oc + 1],
                                                   X[oc][:, tsl], ALU.add, ALU.add),
                            wait=[v_bias], inc="dve")
                ep_done[idx] = v_ep
        p.barrier()
        pss.close()

    def prep_atta():
        pss = PsumScope()
        ps_rep = pss.get([128, 512])
        A = AADDR
        lnw_row = aview(0, [2, 64], F32, "plnw")
        lnb_row = aview(512, [2, 64], F32, "plnb")
        lnw_bf = aview(1024, [2, 64], BF16, "plnwb")
        lnb_bf = aview(1280, [2, 64], BF16, "plnbb")
        scr = aview(1536, [2, 64], F32, "pscr")
        scr2 = aview(2048, [2, 64], F32, "pscr2")
        rhb_row = aview(2560, [2, 1], F32, "prhb")
        rhb_bf = aview(2816, [2, 1], BF16, "prhbb")
        cv_bf = aview(3072, [2, 1], BF16, "pcv")
        p.dma("sync", lambda e: e.dma_start(out=lnw_row[0:1, :], in_=win["aa_ln_w"].unsqueeze(0)))
        p.dma("sync", lambda e: e.dma_start(out=lnb_row[0:1, :], in_=win["aa_ln_b"].unsqueeze(0)))
        v_ln = p.dma("sync", lambda e: e.dma_start(out=rhb_row[0:1, :], in_=win["aa_rh_b"].unsqueeze(0)))
        p.wait("vector", p.dmark())
        p.vop(lambda v: v.tensor_copy(lnw_bf[0:1, :], lnw_row[0:1, :]))
        p.vop(lambda v: v.tensor_copy(lnb_bf[0:1, :], lnb_row[0:1, :]))
        p.vop(lambda v: v.tensor_copy(rhb_bf[0:1, :], rhb_row[0:1, :]))
        v1 = mm(ps_rep[:, 0:64], sb["onesbf"][0:1, :], lnw_bf[0:1, :], True, True,
                wait=[("dve", p.val("dve"))])
        v2 = mm(ps_rep[:, 64:128], sb["onesbf"][0:1, :], lnb_bf[0:1, :], True, True)
        v3 = mm(ps_rep[:, 128:129], sb["onesbf"][0:1, :], rhb_bf[0:1, :], True, True,
                inc="pe")
        p.wait("vector", [("pe", v3)])
        p.vop(lambda v: v.tensor_copy(sb["wrep"][:], ps_rep[:, 0:64]))
        p.vop(lambda v: v.tensor_copy(sb["brep"][:], ps_rep[:, 64:128]))
        p.vop(lambda v: v.tensor_copy(sb["rhbrep"][:], ps_rep[:, 128:129]))
        v1b = mm(ps_rep[:, 192:256], sb["onesbf"][0:1, :], lnw_bf[0:1, :], True, True)
        p.vop(lambda v: v.reduce_max(scr[0:1, 0:1].unsqueeze(2),
                                     lnw_row[0:1, :].unsqueeze(1), AX.X,
                                     apply_absolute_value=True))
        p.vop(lambda v: v.tensor_mul(scr2[0:1, :], lnb_row[0:1, :], lnb_row[0:1, :]))
        p.vop(lambda v: v.reduce_sum(scr[0:1, 1:2].unsqueeze(2),
                                     scr2[0:1, :].unsqueeze(1), AX.X))
        v_sq = p.op("scalar", lambda s_: s_.activation(scr[0:1, 1:2], scr[0:1, 1:2],
                                                       AF.Sqrt),
                    wait=[("dve", p.val("dve"))], inc="act")
        p.wait("vector", [("act", v_sq)])
        p.vop(lambda v: v.tensor_scalar(scr[0:1, 2:3], scr[0:1, 0:1], 8.0, None, ALU.mult))
        p.vop(lambda v: v.tensor_add(scr[0:1, 2:3], scr[0:1, 2:3], scr[0:1, 1:2]))
        p.vop(lambda v: v.tensor_mul(scr[0:1, 3:4], scr[0:1, 2:3], scr[0:1, 2:3]))
        p.vop(lambda v: v.tensor_scalar(scr[0:1, 3:4], scr[0:1, 3:4], -0.125, None, ALU.mult))
        p.vop(lambda v: v.tensor_copy(cv_bf[0:1, :], scr[0:1, 3:4]))
        v4 = mm(ps_rep[:, 129:130], sb["onesbf"][0:1, :], cv_bf[0:1, :], True, True,
                wait=[("dve", p.val("dve"))], inc="pe")
        p.wait("vector", [("pe", v4)])
        p.vop(lambda v: v.tensor_copy(sb["cbias"][:], ps_rep[:, 129:130]))
        qnw_c = aview(3584, [128, 8], F32, "pqnw")
        qnb_c = aview(3616, [128, 8], F32, "pqnb")
        kvnw_c = aview(3648, [128, 8], F32, "pkvnw")
        kvnb_c = aview(3680, [128, 8], F32, "pkvnb")
        qnb_bf = aview(3712, [128, 8], BF16, "pqnbb")
        kvnb_bf = aview(3744, [128, 8], BF16, "pkvnbb")
        for nm, dst in [("aa_qn_w", qnw_c), ("aa_qn_b", qnb_c),
                        ("aa_kvn_w", kvnw_c), ("aa_kvn_b", kvnb_c)]:
            p.dma("gpsimd", lambda e, nm=nm, dst=dst: e.dma_start(
                out=dst[:], in_=win[nm].rearrange("(c p) -> p c", p=128),
                allow_slow_non_contiguous=True))
        p.wait("vector", p.dmark())
        p.vop(lambda v: v.tensor_copy(qnb_bf[:], qnb_c[:]))
        p.vop(lambda v: v.tensor_copy(kvnb_bf[:], kvnb_c[:]))
        pss.close()

        pss = PsumScope()
        ps_bias = pss.get([1, 2 * D])
        ps_cs = pss.get([1, 2 * D])
        wbuf = wview(0, [128, 2048], F32, "pawbuf")
        wbf = wview(8192, [128, 2080], BF16, "pawbf")
        rowsc = wview(16384 if False else 12352, [2, 2048], F32, "parow")
        rowbf2 = wview(20544, [1, 4160], BF16, "parowbf")

        def fold(which):
            wname = "aa_q_w" if which == "q" else "aa_kv_w"
            wide = D if which == "q" else 2 * D
            scol = qnw_c if which == "q" else kvnw_c
            bcol = qnb_bf if which == "q" else kvnb_bf
            stname = "st_qw" if which == "q" else "st_kv"
            v_mmlast = None
            for c in range(NC):
                wv = [("pe", v_mmlast)] if v_mmlast else [("pe", p.val("pe"))]
                wv += [("dve", p.val("dve"))]
                v_l = p.dma("sync", lambda e, c=c, wname=wname: e.dma_start(
                    out=wbuf[:, 0:wide],
                    in_=win[wname].rearrange("(c p) o -> c p o", p=128)[c]), wait=wv)
                ltoks = [v_l]
                if which == "q":
                    ltoks.append(p.dma("sync", lambda e, c=c: e.dma_start(
                        out=wbuf[:, D:D + 1],
                        in_=win["aa_rh_w"].rearrange("(c p) o -> c p o", p=128)[c])))
                p.wait("vector", ltoks)
                v_raw = p.vop(lambda v: v.tensor_copy(wbf[:, 0:wide], wbuf[:, 0:wide]))
                ngg = wide // 512
                v_bias = None
                for gg in range(ngg):
                    v_bias = mm(ps_bias[0:1, 512 * gg:512 * (gg + 1)], bcol[:, c:c + 1],
                                wbf[:, 512 * gg:512 * (gg + 1)], c == 0, c == NC - 1,
                                wait=[("dve", v_raw)] if gg == 0 else (),
                                inc="pe" if gg == ngg - 1 else None)
                p.wait("vector", [("pe", v_bias)])
                v_sc = p.vop(lambda v, c=c, scol=scol: v.tensor_scalar(
                    wbf[:, 0:wide], wbuf[:, 0:wide], scol[:, c:c + 1], None, ALU.mult))
                if which == "q":
                    v_sc = p.vop(lambda v: v.tensor_copy(wbf[:, D:D + 1],
                                                         wbuf[:, D:D + 1]))
                    if c == 0:
                        v_sc = p.vop(lambda v: v.memset(wbf[:, D + 1:D + 16], 0.0))
                stwide = 1040 if which == "q" else 2048
                v_st = p.dma("sync", lambda e, c=c, stname=stname, stwide=stwide:
                             e.dma_start(out=st[stname][c][:, 0:stwide],
                                         in_=wbf[:, 0:stwide]), wait=[("dve", v_sc)])
                v_mmlast = v_bias
                p.wait("sync", [v_st])
            stw0 = 1040 if which == "q" else 2048
            v_cl = None
            for c in range(NC):
                wv = [("pe", v_cl)] if v_cl else [("pe", p.val("pe"))]
                v_l = p.dma("sync", lambda e, c=c, stname=stname, stw0=stw0: e.dma_start(
                    out=wbf[:, 0:stw0], in_=st[stname][c][:, 0:stw0]), wait=wv)
                nfull = stw0 // 512
                w_ = stw0 - nfull * 512
                for gg in range(nfull):
                    v_cl = mm(ps_cs[0:1, 512 * gg:512 * (gg + 1)], sb["onecolbf"][:],
                              wbf[:, 512 * gg:512 * (gg + 1)], c == 0, c == NC - 1,
                              wait=[v_l] if gg == 0 else (),
                              inc="pe" if (gg == nfull - 1 and not w_) else None)
                if w_:
                    v_cl = mm(ps_cs[0:1, stw0 - w_:stw0], sb["onecolbf"][:],
                              wbf[:, stw0 - w_:stw0], c == 0, c == NC - 1, inc="pe")
            p.wait("vector", [("pe", p.val("pe"))])
            bname = "aa_q_b" if which == "q" else "aa_kv_b"
            v_bl = p.dma("sync", lambda e, bname=bname, wide=wide: e.dma_start(
                out=rowsc[0:1, 0:wide], in_=win[bname].unsqueeze(0)))
            p.wait("vector", [v_bl, ("pe", p.val("pe"))])
            p.vop(lambda v, wide=wide: v.tensor_add(rowsc[0:1, 0:wide],
                                                    rowsc[0:1, 0:wide],
                                                    ps_bias[0:1, 0:wide]))
            stw = 1040 if which == "q" else 2048
            if which == "q":
                p.vop(lambda v: v.memset(rowsc[0:1, D:D + 16], 0.0))
            p.vop(lambda v, stw=stw: v.tensor_copy(rowbf2[0:1, 0:stw],
                                                   rowsc[0:1, 0:stw]))
            p.vop(lambda v, stw=stw: v.tensor_scalar(rowbf2[0:1, 2080:2080 + stw],
                                                     ps_cs[0:1, 0:stw], -1.0, None,
                                                     ALU.mult))
            roq = RO["qb"] if which == "q" else RO["kvb"]
            roc = RO["colsum_q"] if which == "q" else RO["colsum_kv"]
            v_s1 = p.dma("sync", lambda e, roq=roq, stw=stw: e.dma_start(
                out=st["st_rows"][:, roq:roq + stw], in_=rowbf2[0:1, 0:stw]),
                wait=[("dve", p.val("dve"))])
            v_s2 = p.dma("sync", lambda e, roc=roc, stw=stw: e.dma_start(
                out=st["st_rows"][:, roc:roc + stw], in_=rowbf2[0:1, 2080:2080 + stw]))
            p.wait("gpsimd", [v_s1, v_s2])

        fold("q")
        p.barrier()
        fold("kv")
        p.barrier()
        if "dumpst" in DBG:
            bb = wview(24576, [128, 2048], BF16, "dumpb")
            v_l = p.dma("sync", lambda e: e.dma_start(out=bb[:], in_=st["st_kv"][0]))
            t1 = p.dma("sync", lambda e: e.dma_start(out=dbg["stkv"], in_=bb[:]),
                       wait=[v_l])
            v_l2 = p.dma("sync", lambda e: e.dma_start(out=bb[:, 0:1040], in_=st["st_qw"][0]),
                         wait=[t1])
            p.dma("sync", lambda e: e.dma_start(out=dbg["stqw"], in_=bb[:, 0:1040]),
                  wait=[v_l2])
            p.dma("sync", lambda e: e.dma_start(out=dbg["strows"], in_=st["st_rows"]))
            p.barrier()
        v_mmlast = None
        for c in range(NC):
            wv = [("dve", p.val("dve"))]
            v_l = p.dma("sync", lambda e, c=c: e.dma_start(
                out=wbuf[:, 0:D],
                in_=win["aa_out_w"].rearrange("(c p) o -> c p o", p=128)[c]), wait=wv)
            p.wait("vector", [v_l])
            v_c = p.vop(lambda v: v.tensor_copy(wbf[:, 0:D], wbuf[:, 0:D]))
            v_st = p.dma("sync", lambda e, c=c: e.dma_start(out=st["st_outw"][c],
                                                            in_=wbf[:, 0:D]),
                         wait=[("dve", v_c)])
            p.wait("vector", [v_st])
        p.wait("sync", p.dmark())
        p.wait("gpsimd", p.dmark())
        p.barrier()
        pss.close()

    AO_TAB = 0
    AO_QNF = 12288
    AO_KNF = 20480
    AO_VBF = 28672
    AO_ABF = 36992
    AO_PBF = 45184
    AO_Q32 = 49280
    AO_K32 = 50304
    AO_SCR = 51328
    AO_ZRB = 55424
    AO_COLS = 59520
    AO_XBT = 59776
    AO_ROWQ = 60416

    def atta_sub(s):
        ln_stats(s, attn_a_mode=True)
        for nm, off in [("tab_c1", 0), ("tab_c2", 4096), ("tab_cs", 8192)]:
            tv = aview(off, [128, NT * HD], F32, nm + "v")
            p.dma("sync", lambda e, nm=nm, tv=tv: e.dma_start(
                out=tv[:].rearrange("p (t d) -> p t d", d=HD),
                in_=tabs[nm].rearrange("(t p) d -> p t d", p=128)))
        c1t = aview(0, [128, NT * HD], F32, "c1t")
        c2t = aview(4096, [128, NT * HD], F32, "c2t")
        cst = aview(8192, [128, NT * HD], F32, "cst")
        vtm = p.dmark()
        p.wait("vector", vtm)
        p.wait("gpsimd", vtm)

        gtoks = {0: atta_loads(0, 0), 1: atta_loads(1, 0)}
        for g in range(NG):
            atta_group(s, g, gtoks[g])
            if g + 2 < NG:
                gtoks[g + 2] = atta_loads(g + 2, p.val("pe"))

    def atta_loads(g, war_pe):
        WS = (g % 2) * 24576
        wqkv = [wview(WS + c * 1568, [128, 784], BF16, "wqkv") for c in range(NC)]
        outwg = [wview(WS + 12800 + i * 2048, [128, 1024], BF16, "outwg")
                 for i in range(2)]
        mi64 = wview(49152, [128, S], BF16, "mi64")
        csqkv = wview(WS + 16896, [128, 784], BF16, "csqkv")
        outbc = wview(WS + 21504, [128, 8], F32, "outbc")
        toks = []
        wv = [("pe", war_pe)] if war_pe else []
        for c in range(NC):
            toks.append(p.dma("sync", lambda e, c=c: e.dma_start(
                out=wqkv[c][:, 0:256], in_=st["st_qw"][c][:, 256 * g:256 * (g + 1)]),
                wait=wv if c == 0 else ()))
            toks.append(p.dma("sync", lambda e, c=c: e.dma_start(
                out=wqkv[c][:, 256:272], in_=st["st_qw"][c][:, 1024:1040])))
            toks.append(p.dma("sync", lambda e, c=c: e.dma_start(
                out=wqkv[c][:, 272:528],
                in_=st["st_kv"][c][:, 256 * g:256 * (g + 1)])))
            toks.append(p.dma("sync", lambda e, c=c: e.dma_start(
                out=wqkv[c][:, 528:784],
                in_=st["st_kv"][c][:, 1024 + 256 * g:1024 + 256 * (g + 1)])))
        for i in range(2):
            toks.append(p.dma("sync", lambda e, i=i: e.dma_start(
                out=outwg[i][:], in_=st["st_outw"][2 * g + i])))
        if g == 0 and "skipl" not in DBG:
            toks.append(p.dma("gpsimd", lambda e: e.dma_start(
                out=mi64[0:64, :], in_=st["st_mrow64"][0:1, :].to_broadcast([64, S])),
                wait=wv))
            toks.append(p.dma("gpsimd", lambda e: e.dma_start(
                out=mi64[64:128, :],
                in_=st["st_irow64"][0:1, :].to_broadcast([64, S]))))
        for dst_off, ro, off, n in ([
            (0, "colsum_q", 256 * g, 256),
            (1, "qb", 256 * g, 256),
            (2, "colsum_q", 1024, 16),
            (3, "qb", 1024, 16),
            (4, "colsum_kv", 256 * g, 256),
            (5, "kvb", 256 * g, 256),
            (6, "colsum_kv", 1024 + 256 * g, 256),
            (7, "kvb", 1024 + 256 * g, 256),
        ]):
            half = 64 * (dst_off % 2)
            col0 = [0, 0, 256, 256, 272, 272, 528, 528][dst_off]
            dst = csqkv[half:half + 64, col0:col0 + n]
            toks.append(p.dma("gpsimd", lambda e, dst=dst, ro=ro, off=off, n=n:
                              e.dma_start(out=dst, in_=st["st_rows"]
                                          [0:1, RO[ro] + off:RO[ro] + off + n]
                                          .to_broadcast([64, n])),
                              wait=wv if dst_off == 0 and g != 0 else ()))
        if g == 0 and "skipl" not in DBG:
            toks.append(p.dma("gpsimd", lambda e: e.dma_start(
                out=outbc[:], in_=win["aa_out_b"].rearrange("(c p) -> p c", p=128),
                allow_slow_non_contiguous=True)))
        return toks

    def atta_group(s, g, ltoks):
        c1t = aview(0, [128, NT * HD], F32, "gc1")
        c2t = aview(4096, [128, NT * HD], F32, "gc2")
        cst = aview(8192, [128, NT * HD], F32, "gcs")
        qnf = [aview(AO_QNF + i * 4096, [128, S], BF16, "qnf") for i in range(2)]
        knf = [aview(AO_KNF + i * 4096, [128, S], BF16, "knf") for i in range(2)]
        vbuf = aview(AO_VBF, [128, NT * 260], BF16, "vbuf")
        abf = [aview(AO_ABF + i * 4096, [128, S], BF16, "abf") for i in range(2)]
        pbf = [aview(AO_PBF + i * 2048, [128, 1024], BF16, "pbf") for i in range(2)]
        q32 = aview(AO_Q32, [128, 256], F32, "q32")
        k32 = aview(AO_K32, [128, 256], F32, "k32")
        scr = [aview(AO_SCR + i * 1024, [128, 256], F32, "scrA") for i in range(4)]
        zrb = aview(AO_ZRB, [64, 1024], F32, "zrb")
        cols = [aview(AO_COLS + i * 32, [128, 4], F32, "colA") for i in range(12)]
        xbt = [aview(AO_XBT + i * 256, [128, 128], BF16, "xbt") for i in range(2)]
        zrow = aview(AO_ROWQ, [1, 1024], F32, "zrow")
        WS = (g % 2) * 24576
        zbf = wview(WS + 22528, [1, 1024], BF16, "zbf")
        wqkv = [wview(WS + c * 1568, [128, 784], BF16, "wqkv") for c in range(NC)]
        outwg = [wview(WS + 12800 + i * 2048, [128, 1024], BF16, "outwg")
                 for i in range(2)]
        mi64 = wview(49152, [128, S], BF16, "mi64")
        csqkv = wview(WS + 16896, [128, 784], BF16, "csqkv")
        outbc = wview(WS + 21504, [128, 8], F32, "outbc")

        gml = list(ltoks) + [("dve", p.val("dve"))]

        p.wait("tensor", gml)
        p.wait("vector", gml)

        pss = PsumScope()
        ps_p2 = [pss.get([128, 1024]) for _ in range(2)]
        ps_tr = [pss.get([128, 128], BF16) for _ in range(2)]

        p.vop(lambda v: v.memset(
            vbuf[:].rearrange("p (t h c) -> p t h c", h=4, c=65)[:, :, :, 64:65], 1.0))

        rcol_all = sb["colst"][:, CO["r"]:CO["r"] + 16]
        R2_all = sb["colst"][:, CO["R2"]:CO["R2"] + 16]
        rad_all = sb["colst"][:, CO["rad"]:CO["rad"] + 16]

        xsl = [aview(AO_ABF + sl * 2048, [128, 1024], BF16, "xslab")
               for sl in range(2)]

        def xb_load(t, wv):
            return p.dma("sync", lambda e, t=t: e.dma_start(
                out=xsl[t % 2][:].rearrange("p (c u) -> p c u", u=128),
                in_=st["st_xbf"][:, :, 128 * t:128 * (t + 1)]
                .rearrange("c p u -> p c u")), wait=wv)

        xtok = {0: xb_load(0, ()), 1: xb_load(1, ())}
        vev_done = {}
        tr_done = {}
        ap_done = {}
        pe_done = {}
        for t in range(NT):
            bi = t % 2
            psp = ps_p2[bi]
            ps_q = psp[:, 0:512]
            ps_k = psp[:, 272:784]
            ps_v = psp[:, 528:784]
            tsl = slice(128 * t, 128 * (t + 1))
            lsl = mi64[:, 128 * t:128 * (t + 1)]
            wv0 = [xtok[t]]
            if t - 2 in vev_done:
                wv0.append(("dve", vev_done[t - 2]))
            for c in range(NC):
                mm(psp[:, 0:512], xsl[bi][:, 128 * c:128 * (c + 1)],
                   wqkv[c][:, 0:512], c == 0, False, wait=wv0 if c == 0 else ())
                mm(psp[:, 512:784], xsl[bi][:, 128 * c:128 * (c + 1)],
                   wqkv[c][:, 512:784], c == 0, False)
            mm(psp[:, 0:512], lsl, csqkv[:, 0:512], False, True)
            v_proj = mm(psp[:, 512:784], lsl, csqkv[:, 512:784], False, True,
                        inc="pe")
            pe_done[t] = v_proj
            if t + 2 < NT:
                xtok[t + 2] = xb_load(t + 2, [("pe", v_proj)])
            p.wait("vector", [("pe", v_proj)])
            qk32 = aview(AO_Q32, [128, 512], F32, "qk32")
            p.vop(lambda v, t=t, psp=psp, qk32=qk32: v.tensor_scalar(
                qk32[:, 0:256], psp[:, 0:256], R2_all[:, t:t + 1], None, ALU.mult))
            p.vop(lambda v, t=t, psp=psp, qk32=qk32: v.tensor_scalar(
                qk32[:, 256:512], psp[:, 272:528], R2_all[:, t:t + 1], None,
                ALU.mult))
            if g == 0:
                p.vop(lambda v, t=t, psp=psp: v.tensor_scalar(
                    cols[8][:, 0:1], psp[:, 256:257], rcol_all[:, t:t + 1],
                    sb["rhbrep"][:], ALU.mult, ALU.add))
                p.vop(lambda v: v.tensor_scalar(cols[8][:, 0:1], cols[8][:, 0:1],
                                                0.01, 0.5, ALU.mult, ALU.max))
                p.vop(lambda v, t=t: v.tensor_scalar(
                    rad_all[:, t:t + 1], cols[8][:, 0:1], 2.0, None, ALU.min))
            v_vev = p.vop(lambda v, t=t, psp=psp: v.tensor_scalar(
                vbuf[:].rearrange("p (tt h c) -> p tt h c", h=4, c=65)[:, t, :, 0:64],
                psp[:, 528:784].rearrange("p (h d) -> p h d", d=64),
                R2_all[:, t:t + 1], None, ALU.mult))
            vev_done[t] = v_vev
            if t - 1 in tr_done:
                p.wait("vector", [("pe", tr_done[t - 1])])
            scr4 = [aview(AO_SCR + i_ * 2048, [128, 512], F32, "scr4")
                    for i_ in range(4)]
            cW = [aview(AO_COLS + i_ * 32, [128, 8], F32, "colW")
                  for i_ in range(12)]
            csv = cst[:].rearrange("p (tt d) -> p tt d", d=HD)[:, t, :] \
                .unsqueeze(1).to_broadcast([128, 8, HD])
            c1v = c1t[:].rearrange("p (tt d) -> p tt d", d=HD)[:, t, :] \
                .unsqueeze(1).to_broadcast([128, 8, HD])
            c2v = c2t[:].rearrange("p (tt d) -> p tt d", d=HD)[:, t, :] \
                .unsqueeze(1).to_broadcast([128, 8, HD])
            s3 = qk32[:].rearrange("p (h d) -> p h d", d=HD)
            wvp = [("dve", v_vev)]
            if t - 1 in ap_done:
                wvp.append(("dve", ap_done[t - 1]))
            v_p1 = p.op("gpsimd", lambda g_, s3=s3, csv=csv: g_.tensor_tensor(
                scr4[3][:].rearrange("p (h d) -> p h d", d=HD), s3, csv, ALU.mult),
                wait=wvp)
            p.vop(lambda v, s3=s3, c1v=c1v: v.tensor_tensor(
                scr4[0][:].rearrange("p (h d) -> p h d", d=HD), s3, c1v, ALU.mult))
            p.vop(lambda v, s3=s3, c2v=c2v: v.tensor_tensor(
                scr4[1][:].rearrange("p (h d) -> p h d", d=HD), s3, c2v, ALU.mult))
            v_r1 = p.op("vector", lambda v: v.reduce_sum(
                cW[0][:].unsqueeze(2),
                scr4[3][:].rearrange("p (h d) -> p h d", d=HD), AX.X),
                wait=[("pool", v_p1)])
            v_p3 = p.op("gpsimd", lambda g_, s3=s3: g_.tensor_tensor(
                scr4[3][:].rearrange("p (h d) -> p h d", d=HD), s3, s3, ALU.mult),
                wait=[("dve", v_r1)])
            s0e = scr4[0][:].rearrange("p (h j two) -> p h j two", two=2, j=32)
            s1e = scr4[1][:].rearrange("p (h j two) -> p h j two", two=2, j=32)
            u0 = scr4[2][:].rearrange("p (h j two) -> p h j two", two=2, j=32)
            p.vop(lambda v, s0e=s0e, u0=u0: v.tensor_tensor(
                u0[:, :, :, 0], s0e[:, :, :, 0], s0e[:, :, :, 1], ALU.subtract))
            p.vop(lambda v, s1e=s1e, u0=u0: v.tensor_tensor(
                u0[:, :, :, 1], s1e[:, :, :, 0], s1e[:, :, :, 1], ALU.add))
            v_r2 = p.op("vector", lambda v: v.reduce_sum(
                cW[1][:].unsqueeze(2),
                scr4[3][:].rearrange("p (h d) -> p h d", d=HD), AX.X),
                wait=[("pool", v_p3)])
            p.vop(lambda v, t=t: v.tensor_mul(
                cW[2][:, 0:1], rad_all[:, t:t + 1], rad_all[:, t:t + 1]))
            p.vop(lambda v, t=t: v.tensor_scalar(
                cW[3][:], cW[0][:], rad_all[:, t:t + 1], 1.0 / HD,
                ALU.mult, ALU.mult))
            p.vop(lambda v: v.tensor_scalar(cW[4][:], cW[1][:], cW[2][:, 0:1],
                                            1.0 / HD, ALU.mult, ALU.mult))
            p.vop(lambda v: v.tensor_mul(cW[5][:], cW[3][:], cW[3][:]))
            p.vop(lambda v: v.tensor_sub(cW[5][:], cW[4][:], cW[5][:]))
            v_lnq = p.op("scalar", lambda s_: s_.activation(
                cW[6][:], cW[5][:], AF.Ln, bias=sb["epscol"][:]),
                wait=[("dve", p.val("dve"))], inc="act")
            v_eq = p.op("scalar", lambda s_: s_.activation(
                cW[6][:], cW[6][:], AF.Exp, scale=-0.5), inc="act")
            p.wait("vector", [("act", v_eq)])
            p.vop(lambda v, t=t: v.tensor_scalar(cW[7][:], cW[6][:],
                                                 rad_all[:, t:t + 1], None,
                                                 ALU.mult))
            p.vop(lambda v: v.tensor_mul(cW[9][:], cW[3][:], cW[6][:]))
            u3 = scr4[2][:].rearrange("p (h d) -> p h d", d=HD)
            g_b = cW[7][:].unsqueeze(2).to_broadcast([128, 8, HD])
            m2_b = cW[9][:].unsqueeze(2).to_broadcast([128, 8, HD])
            w_b = sb["wrep"][:].unsqueeze(1).to_broadcast([128, 8, HD])
            b_b = sb["brep"][:].unsqueeze(1).to_broadcast([128, 8, HD])
            p.vop(lambda v, u3=u3, g_b=g_b: v.tensor_tensor(
                scr4[3][:].rearrange("p (h d) -> p h d", d=HD), u3, g_b, ALU.mult))
            p.vop(lambda v, m2_b=m2_b: v.tensor_tensor(
                scr4[3][:].rearrange("p (h d) -> p h d", d=HD),
                scr4[3][:].rearrange("p (h d) -> p h d", d=HD), m2_b, ALU.subtract))
            p.vop(lambda v, w_b=w_b: v.tensor_tensor(
                scr4[3][:].rearrange("p (h d) -> p h d", d=HD),
                scr4[3][:].rearrange("p (h d) -> p h d", d=HD), w_b, ALU.mult))
            p.vop(lambda v, b_b=b_b: v.tensor_tensor(
                scr4[0][:].rearrange("p (h d) -> p h d", d=HD),
                scr4[3][:].rearrange("p (h d) -> p h d", d=HD), b_b, ALU.add))
            v_cast = p.vop(lambda v: v.tensor_copy(
                scr4[1][:].bitcast(BF16)[:, 0:512], scr4[0][:]))
            ap_done[t] = v_cast
            cp_done = {}
            for c2 in range(4):
                dstf = qnf if c2 < 2 else knf
                wvt = [("dve", v_cast)] if c2 < 2 else [("dve", cp_done[c2 - 2])]
                v_tr = tr(ps_tr[c2 % 2][:],
                          scr4[1][:].bitcast(BF16)[:, 128 * c2:128 * (c2 + 1)],
                          wait=wvt, inc="pe")
                tr_done[t] = v_tr
                cp_done[c2] = p.op(
                    "vector", lambda v, c2=c2, t=t, dstf=dstf, pt=ps_tr[c2 % 2]:
                    v.tensor_copy(dstf[c2 % 2][:, 128 * t:128 * (t + 1)], pt[:]),
                    wait=[("pe", v_tr)], inc="dve")
        p.wait("tensor", [("dve", p.val("dve"))])
        pss.close()
        if "dumpa" in DBG and g == 0:
            p.wait("sync", p.dmark() + [("dve", p.val("dve")), ("pe", p.val("pe")),
                                        ("act", p.val("act"))])
            p.dma("sync", lambda e: e.dma_start(out=dbg["stkv"][:, 0:2048],
                                                in_=mi64[:]))
            p.dma("sync", lambda e: e.dma_start(out=dbg["stqw"][:, 0:256],
                                                in_=csqkv[:, 272:528]))
            p.dma("sync", lambda e: e.dma_start(out=dbg["colst"], in_=sb["colst"][:]))
            p.dma("sync", lambda e: e.dma_start(out=dbg["qnf"], in_=qnf[0][:]))
            p.dma("sync", lambda e: e.dma_start(out=dbg["knf"], in_=knf[0][:]))
            p.dma("sync", lambda e: e.dma_start(out=dbg["vb"], in_=vbuf[:]))
            p.barrier()
        if "a1only" in DBG:
            p.barrier()
            return

        pss = PsumScope()
        ps_s = [pss.get([128, 1024]) for _ in range(2)]
        ps_av = [pss.get([65, 1024]) for _ in range(2)]
        v_prev_ab = 0
        for hp in range(2):
            for qh in range(2):
                qsl = slice(1024 * qh, 1024 * (qh + 1))
                v_av = None
                v_exp = {}
                for kt in range(NT):
                    ksl = slice(128 * kt, 128 * (kt + 1))
                    wv = [("dve", v_prev_ab)] if v_prev_ab else []
                    if kt >= 1:
                        wv = [("act", v_exp[kt - 1])]
                    for half in range(2):
                        for hh in range(2):
                            base = 64 * hh
                            mm(ps_s[hh][:, 512 * half:512 * (half + 1)],
                               knf[hp][base:base + 64, ksl],
                               qnf[hp][base:base + 64,
                                       1024 * qh + 512 * half:
                                       1024 * qh + 512 * (half + 1)],
                               True, True,
                               wait=wv if half == 0 and hh == 0 else ())
                    v_sc = p.op("tensor", None, inc="pe") if False else None
                    p.q["tensor"][-1] = (p.q["tensor"][-1][0], p.q["tensor"][-1][1], "pe", 1)
                    p.cnt["pe"] += 1
                    v_sc = p.cnt["pe"]
                    for hh in range(2):
                        v_e = p.op("scalar", lambda s_, hh=hh: s_.activation(
                            pbf[hh][:], ps_s[hh][:], AF.Exp, bias=sb["cbias"][:],
                            scale=0.125), wait=[("pe", v_sc)] if hh == 0 else (),
                            inc="act")
                    v_exp[kt] = v_e
                    for hh in range(2):
                        h4 = 2 * hp + hh
                        vsl = vbuf[:].rearrange("p (tt h c) -> p tt h c", h=4, c=65)
                        mm(ps_av[hh][:, 0:512], vsl[:, kt, h4, :],
                           pbf[hh][:, 0:512], kt == 0, kt == NT - 1,
                           wait=[("act", v_exp[kt])] if hh == 0 else ())
                        v_av = mm(ps_av[hh][:, 512:1024],
                                  vsl[:, kt, h4, :],
                                  pbf[hh][:, 512:1024], kt == 0, kt == NT - 1,
                                  inc="pe" if (kt == NT - 1 and hh == 1) else None)
                p.wait("vector", [("pe", v_av)])
                for hh in range(2):
                    p.op("scalar", lambda s_, hh=hh: s_.activation(
                        zrow[0:1, :], ps_av[hh][64:65, :], AF.Ln),
                        wait=[("pe", v_av), ("dve", p.val("dve"))])
                    v_re = p.op("scalar", lambda s_: s_.activation(
                        zrow[0:1, :], zrow[0:1, :], AF.Exp, scale=-1.0))
                    p.op("vector", lambda v: v.tensor_copy(zbf[:], zrow[:]),
                         wait=[("act", v_re)])
                    v_z1 = mm(ps_s[hh][0:64, 0:512], sb["onesbf"][0:1, 0:64],
                              zbf[0:1, 0:512], True, True, wait=[("dve", p.val("dve"))])
                    v_z2 = mm(ps_s[hh][0:64, 512:1024], sb["onesbf"][0:1, 0:64],
                              zbf[0:1, 512:1024], True, True, inc="pe")
                    p.wait("vector", [("pe", v_z2)])
                    v_zc = p.vop(lambda v, hh=hh: v.tensor_copy(zrb[:],
                                                                ps_s[hh][0:64, :]))
                    base = 64 * hh
                    p.vop(lambda v, hh=hh, hp=hp, base=base, qsl=qsl: v.tensor_tensor(
                        abf[hp][base:base + 64, qsl], ps_av[hh][0:64, :], zrb[:],
                        ALU.mult))
                v_prev_ab = p.val("dve")
        pss.close()
        if "a12" in DBG:
            p.barrier()
            return

        pss = PsumScope()
        po = [pss.get([128, 512]) for _ in range(2)]
        ep_done = {}
        for oc in range(NC):
            for tg in range(4):
                tsl = slice(512 * tg, 512 * (tg + 1))
                idx = oc * 4 + tg
                pk = po[idx % 2]
                wv = [("dve", p.val("dve"))] if idx == 0 else ()
                if idx >= 2:
                    wv = [("dve", ep_done[idx - 2])]
                mm(pk[:], outwg[0][:, 128 * oc:128 * (oc + 1)], abf[0][:, tsl],
                   True, False, wait=wv)
                v_mm = mm(pk[:], outwg[1][:, 128 * oc:128 * (oc + 1)], abf[1][:, tsl],
                          False, True, inc="pe")
                if g == 0:
                    v_ep = p.op("vector", lambda v, oc=oc, tsl=tsl, pk=pk:
                                v.scalar_tensor_tensor(X[oc][:, tsl], pk[:],
                                                       outbc[:, oc:oc + 1],
                                                       X[oc][:, tsl], ALU.add, ALU.add),
                                wait=[("pe", v_mm)], inc="dve")
                else:
                    v_ep = p.op("vector", lambda v, oc=oc, tsl=tsl, pk=pk:
                                v.tensor_add(X[oc][:, tsl], X[oc][:, tsl], pk[:]),
                                wait=[("pe", v_mm)], inc="dve")
                ep_done[idx] = v_ep
        p.barrier()
        pss.close()


    def prep_attb():
        pss = PsumScope()
        ps_cs = pss.get([1, 2048])
        wbuf = wview(0, [128, 2048], F32, "pawbuf")
        wbf = wview(8192, [128, 2080], BF16, "pawbf")
        rowbf2 = wview(20544, [1, 4160], BF16, "parowbf")
        casts = [("ab_wq", "st_wq", D, NC), ("ab_wk", "st_wk", D, NC),
                 ("ab_wv", "st_wv", D, NC), ("ab_wgq", "st_wgq", LDIM, NC),
                 ("ab_wgk", "st_wgk", LDIM, NC), ("ab_wgv", "st_wgv", LDIM, NC),
                 ("ab_w_out", "st_wout", D, NC), ("ab_r_out_w", "st_rout", D, LDIM // 128)]
        for wname, stname, wide, nchunk in casts:
            for c in range(nchunk):
                v_l = p.dma("sync", lambda e, c=c, wname=wname, wide=wide: e.dma_start(
                    out=wbuf[:, 0:wide],
                    in_=win[wname].rearrange("(c p) o -> c p o", p=128)[c]),
                    wait=[("dve", p.val("dve"))])
                v_c = p.op("vector", lambda v, wide=wide: v.tensor_copy(
                    wbf[:, 0:wide], wbuf[:, 0:wide]), wait=[v_l])
                v_st = p.dma("sync", lambda e, c=c, stname=stname, wide=wide:
                             e.dma_start(out=st[stname][c][:, 0:wide],
                                         in_=wbf[:, 0:wide]), wait=[("dve", v_c)])
                p.wait("vector", [v_st])
        gbuf = wview(24576, [HD, 2 * NH], F32, "pglw")
        v_l = p.dma("sync", lambda e: e.dma_start(out=gbuf[:], in_=tabs["tab_glwT"]))
        p.op("vector", lambda v: v.tensor_copy(sb["glwt"][0:64, :], gbuf[:]),
             wait=[v_l])
        p.op("vector", lambda v: v.tensor_copy(sb["glwt"][64:128, :], gbuf[:]))
        p.wait("sync", p.dmark())
        for stname, ro, wide in [("st_wq", "colsum_wq", D), ("st_wk", "colsum_wk", D),
                                 ("st_wv", "colsum_wv", D),
                                 ("st_wgq", "colsum_wgq", LDIM)]:
            v_cl = None
            for c in range(NC):
                wv = [("pe", v_cl)] if v_cl else [("pe", p.val("pe")),
                                                 ("dve", p.val("dve"))]
                v_l = p.dma("sync", lambda e, c=c, stname=stname, wide=wide:
                            e.dma_start(out=wbf[:, 0:wide],
                                        in_=st[stname][c][:, 0:wide]), wait=wv)
                ng = wide // 512
                for gg in range(ng):
                    v_cl = mm(ps_cs[0:1, 512 * gg:512 * (gg + 1)], sb["onecolbf"][:],
                              wbf[:, 512 * gg:512 * (gg + 1)], c == 0, c == NC - 1,
                              wait=[v_l] if gg == 0 else (),
                              inc="pe" if gg == ng - 1 else None)
            p.wait("vector", [("pe", v_cl)])
            v_n = p.op("vector", lambda v, wide=wide: v.tensor_scalar(
                rowbf2[0:1, 0:wide], ps_cs[0:1, 0:wide], -1.0, None, ALU.mult))
            v_s = p.dma("sync", lambda e, ro=ro, wide=wide: e.dma_start(
                out=st["st_rows"][:, RO[ro]:RO[ro] + wide], in_=rowbf2[0:1, 0:wide]),
                wait=[("dve", v_n)])
            p.wait("vector", [v_s])
        p.wait("sync", p.dmark())
        p.wait("gpsimd", p.dmark())
        p.barrier()
        pss.close()

    BB_VBL = 0
    BB_QLF = 4096
    BB_KLF = 8192
    BB_EBL = 12288
    BB_SG = 15360
    BB_SG2 = 23552
    BB_ZRL = 31744
    BB_XSL = 39936
    BB_UBF = 44032
    BB_GRBF = 48128
    BB_LOCALF = 52224
    BB_T1 = 56320
    BB_QKVT = 60416

    def attb_sub(s):
        ln_stats(s, attn_a_mode=False, rrep_flag=False)
        for c in range(NC):
            p.dma("gpsimd", lambda e, c=c: e.dma_start(
                out=st["st_xbf"][c], in_=X[c][:]))
        rcol_all = sb["colst"][:, CO["r"]:CO["r"] + 16]
        mcol_all = sb["colst"][:, CO["tmp"]:CO["tmp"] + 16]
        mrcol_all = sb["colst"][:, CO["R2"]:CO["R2"] + 16]
        p.op("vector", lambda v: v.tensor_mul(mrcol_all, mcol_all, rcol_all))
        mi64v = wview(49152, [128, S], BF16, "mi64")
        p.dma("gpsimd", lambda e: e.dma_start(
            out=mi64v[0:64, :], in_=st["st_mrow64"][0:1, :].to_broadcast([64, S])))
        p.wait("sync", p.dmark())
        p.wait("gpsimd", p.dmark())
        p.barrier()
        if "nob0" not in DBG:
            b0_latent(s)
        if "b0only" in DBG:
            return
        npair = NC if "b11" not in DBG else 1
        for hp in range(npair):
            b1_local(s, hp)
        if "nobout" not in DBG:
            b_out(s)

    def b0_latent(s):
        rcol_all = sb["colst"][:, CO["r"]:CO["r"] + 16]
        mrcol_all = sb["colst"][:, CO["R2"]:CO["R2"] + 16]
        mi64v = wview(49152, [128, S], BF16, "mi64")
        latm2 = aview(0, [128, S], BF16, "latm2")
        gqf = [aview(4096 + j * 4096, [128, S], BF16, "gqf") for j in range(4)]
        elat = aview(20480, [128, S], BF16, "elat")
        gkbf = aview(24576, [NLAT, LDIM], BF16, "gkbf")
        gvbf = aview(25600, [128, LDIM], BF16, "gvbf")
        gkt = aview(26624, [128, LDIM], BF16, "gkt")
        poolbf = aview(27648, [NLAT, D], BF16, "poolbf")
        poolT = aview(29696, [128, LDIM], BF16, "poolT")
        rc = aview(34816, [128, 128], BF16, "rcpool")
        onesw = aview(35072, [128, 512], BF16, "onesw")
        zre = aview(36096 if False else BB_ZRL, [33, S], F32, "zre")
        remf = [aview(44288 + j * 4096, [128, S], BF16, "remf") for j in range(4)]
        gqtmp = aview(60672, [128, 512], BF16, "gqtmp")
        rem2b = aview(62720, [128, 1024], BF16, "rem2b")
        xtts = [aview(30720 + i * 2048, [128, D], BF16, "xtts") for i in range(2)]
        xslb = [aview(BB_XSL + sl * 2048, [128, 1024], BF16, "xslab2")
                for sl in range(2)]
        wgql = [wview(c * 1024, [128, LDIM], BF16, "wgql") for c in range(NC)]
        wgkl = [wview(8192 + c * 1024, [128, LDIM], BF16, "wgkl") for c in range(NC)]
        wgvl = [wview(16384 + c * 1024, [128, LDIM], BF16, "wgvl") for c in range(NC)]
        routl = [wview(40960 + j * 2048, [128, D], BF16, "routl") for j in range(4)]
        cswgq = wview(53248, [64, LDIM], BF16, "cswgq")
        rem2f = [wview(c * 4096, [128, S], BF16, "rem2f") for c in range(NC)]

        for c in range(NC):
            p.dma("sync", lambda e, c=c: e.dma_start(
                out=wgql[c][:], in_=st["st_wgq"][c]))
            p.dma("sync", lambda e, c=c: e.dma_start(
                out=wgkl[c][:], in_=st["st_wgk"][c]))
            p.dma("sync", lambda e, c=c: e.dma_start(
                out=wgvl[c][:], in_=st["st_wgv"][c]))
        for j in range(4):
            p.dma("sync", lambda e, j=j: e.dma_start(
                out=routl[j][:], in_=st["st_rout"][j]))
        p.dma("sync", lambda e: e.dma_start(
            out=latm2[0:NLAT, :], in_=tabs["tab_latmask"]))
        p.dma("sync", lambda e: e.dma_start(
            out=latm2[NLAT:128, :], in_=tabs["tab_latmask"]))
        p.dma("gpsimd", lambda e: e.dma_start(
            out=cswgq[:], in_=st["st_rows"]
            [0:1, RO["colsum_wgq"]:RO["colsum_wgq"] + LDIM].to_broadcast([64, LDIM])))
        p.op("vector", lambda v: v.memset(onesw[:], 1.0))
        gml = p.dmark() + [("dve", p.val("dve"))]
        p.wait("tensor", gml)
        p.wait("vector", gml)

        pss = PsumScope()
        ps_gq = [pss.get([128, 512]) for _ in range(2)]
        ps_pool = pss.get([NLAT, D])
        ps_trx = [pss.get([128, 128], BF16) for _ in range(2)]
        ps_trg = [pss.get([128, 128], BF16) for _ in range(2)]

        def xb_load2(t, wv):
            return p.dma("sync", lambda e, t=t: e.dma_start(
                out=xslb[t % 2][:].rearrange("p (c u) -> p c u", u=128),
                in_=st["st_xbf"][:, :, 128 * t:128 * (t + 1)]
                .rearrange("c p u -> p c u")), wait=wv)

        xtok = {0: xb_load2(0, ()), 1: xb_load2(1, ())}
        ev2 = {}
        for t in range(NT):
            bi = t % 2
            tsl = slice(128 * t, 128 * (t + 1))
            pg = ps_gq[bi]
            wv0 = [xtok[t]]
            if t - 2 in ev2:
                wv0.append(("dve", ev2[t - 2]))
            for c in range(NC):
                mm(pg[:, 0:512], xslb[bi][:, 128 * c:128 * (c + 1)],
                   wgql[c][:], c == 0, False, wait=wv0 if c == 0 else ())
            v_gq = mm(pg[:, 0:512], mi64v[0:64, tsl], cswgq[:], False, True, inc="pe")
            vtr = {}
            for c in range(NC):
                v_t = tr(ps_trx[c % 2][:], xslb[bi][:, 128 * c:128 * (c + 1)],
                         wait=[("act", p.val("act"))] if c >= 2 else (), inc="pe")
                vtr[c] = p.op("scalar", lambda s_, c=c, bi=bi: s_.copy(
                    xtts[bi][:, 128 * c:128 * (c + 1)], ps_trx[c % 2][:]),
                    wait=[("pe", v_t)])
            p.op("vector", lambda v: v.memset(rc[:], 0.0),
                 wait=[("pe", p.val("pe"))])
            p.op("vector", lambda v, t=t: v.tensor_scalar(
                rc[:, 4 * t:4 * t + 4], sb["t0"][:], rcol_all[:, t:t + 1],
                1.0 / 32.0, ALU.mult, ALU.mult))
            v_rc2 = p.op("vector", lambda v, t=t: v.tensor_scalar(
                rc[:, 64 + 4 * t:64 + 4 * t + 4], sb["t0"][:], mrcol_all[:, t:t + 1],
                -1.0 / 32.0, ALU.mult, ALU.mult))
            for c in range(NC):
                mm(ps_pool[:, 128 * c:128 * (c + 1)], rc[:, 0:64],
                   xtts[bi][:, 128 * c:128 * (c + 1)],
                   t == 0 and c in (0, 4), False,
                   wait=[("act", vtr[NC - 1]), ("dve", v_rc2)] if c == 0 else ())
            v_pl = None
            for gg in range(2):
                v_pl = mm(ps_pool[:, 512 * gg:512 * (gg + 1)], rc[:, 64:128],
                          onesw[:], False, t == NT - 1,
                          inc="pe" if (t == NT - 1 and gg == 1) else None)
            v_ev = p.op("vector", lambda v, t=t, pg=pg: v.tensor_scalar(
                gqtmp[:], pg[:], rcol_all[:, t:t + 1], None, ALU.mult),
                wait=[("pe", v_gq)])
            ev2[t] = v_ev
            for j in range(4):
                wvt = [("dve", v_ev)] if j == 0 else []
                if j >= 2 or t >= 1:
                    wvt.append(("act", p.val("act")))
                v_t = tr(ps_trg[j % 2][:], gqtmp[:, 128 * j:128 * (j + 1)],
                         wait=wvt, inc="pe")
                p.op("scalar", lambda s_, j=j, t=t: s_.copy(
                    gqf[j][:, 128 * t:128 * (t + 1)], ps_trg[j % 2][:]),
                    wait=[("pe", v_t)])
            if t + 2 < NT:
                xtok[t + 2] = xb_load2(t + 2, [("pe", p.val("pe"))])
        p.wait("vector", [("pe", p.val("pe")), ("act", p.val("act"))])
        v_pb = p.op("vector", lambda v: v.tensor_copy(poolbf[:], ps_pool[:]))
        vtp = None
        for c in range(NC):
            wvt = [("dve", v_pb)] if c == 0 else []
            if c >= 2:
                wvt.append(("act", p.val("act")))
            v_t = tr(ps_trx[c % 2][0:NLAT, :].bitcast(BF16)[:, 0:NLAT]
                     if False else ps_trx[c % 2][:, 0:NLAT],
                     poolbf[:, 128 * c:128 * (c + 1)], wait=wvt, inc="pe")
            vtp = p.op("scalar", lambda s_, c=c: s_.copy(
                poolT[:, 64 * c:64 * (c + 1)], ps_trx[c % 2][:, 0:NLAT]),
                wait=[("pe", v_t)])
        pss.close()
        pss = PsumScope()
        ps_gk = pss.get([NLAT, LDIM])
        ps_gv = pss.get([NLAT, LDIM])
        ps_trk = [pss.get([NLAT, 64], BF16) for _ in range(2)]
        for c in range(NC):
            mm(ps_gk[:], poolT[:, 64 * c:64 * (c + 1)], wgkl[c][:], c == 0,
               c == NC - 1, wait=[("act", vtp)] if c == 0 else ())
        v_gv = None
        for c in range(NC):
            v_gv = mm(ps_gv[:], poolT[:, 64 * c:64 * (c + 1)], wgvl[c][:], c == 0,
                      c == NC - 1, inc="pe" if c == NC - 1 else None)
        p.wait("vector", [("pe", v_gv)])
        v_ek = p.op("vector", lambda v: v.tensor_copy(gkbf[:], ps_gk[:]))
        p.op("vector", lambda v: v.tensor_copy(gvbf[0:64, :], ps_gv[:]))
        v_evv = p.op("vector", lambda v: v.tensor_copy(gvbf[64:128, :], ps_gv[:]))
        for h in range(8):
            wvt = [("dve", v_ek)] if h == 0 else []
            if h >= 2:
                wvt.append(("act", p.val("act")))
            v_t = tr(ps_trk[h % 2][:], gkbf[:, 64 * h:64 * (h + 1)],
                     wait=wvt, inc="pe")
            p.op("scalar", lambda s_, h=h: s_.copy(
                gkt[0:64, 64 * h:64 * (h + 1)], ps_trk[h % 2][:]),
                wait=[("pe", v_t)])
            p.op("scalar", lambda s_, h=h: s_.copy(
                gkt[64:128, 64 * h:64 * (h + 1)], ps_trk[h % 2][:]))
        p.wait("tensor", [("act", p.val("act")), ("dve", v_evv)])
        pss.close()
        for lhp in range(4):
            pss = PsumScope()
            ps_lsc = pss.get([128, S])
            ps_zl = pss.get([33, S])
            v_last = None
            for hh in range(2):
                h = 2 * lhp + hh
                for ch in range(4):
                    csl = slice(512 * ch, 512 * (ch + 1))
                    v_last = mm(ps_lsc[64 * hh:64 * hh + 64, csl],
                                gkt[64 * hh:64 * hh + 64, 64 * h:64 * (h + 1)],
                                gqf[lhp][64 * hh:64 * hh + 64, csl], True, True,
                                wait=[("dve", p.val("dve"))]
                                if hh == 0 and ch == 0 else (),
                                inc="pe" if hh == 1 and ch == 3 else None)
            v_e = None
            for ch in range(4):
                csl = slice(512 * ch, 512 * (ch + 1))
                for hh in range(2):
                    hsl = slice(64 * hh, 64 * hh + 64)
                    v_e = p.op("scalar", lambda s_, csl=csl, hsl=hsl, pl=ps_lsc:
                               s_.activation(elat[hsl, csl], pl[hsl, csl],
                                             AF.Exp, scale=0.125),
                               wait=[("pe", v_last)] if ch == 0 and hh == 0 else ())
            v_m = p.op("vector", lambda v: v.tensor_mul(elat[:], elat[:], latm2[:]),
                       wait=[("act", v_e)])
            v_z = None
            for hh in range(2):
                for ch in range(4):
                    csl = slice(512 * ch, 512 * (ch + 1))
                    v_z = mm(ps_zl[32 * hh:32 * hh + 1, csl],
                             sb["onecolbf"][64 * hh:64 * hh + 64, :],
                             elat[64 * hh:64 * hh + 64, csl], True, True,
                             wait=[("dve", v_m)] if hh == 0 and ch == 0 else (),
                             inc="pe" if hh == 1 and ch == 3 else None)
            for r0 in (0, 32):
                p.op("scalar", lambda s_, pz=ps_zl, r0=r0: s_.activation(
                    zre[r0:r0 + 1, :], pz[r0:r0 + 1, :], AF.Ln),
                    wait=[("pe", v_z)] if r0 == 0 else ())
                v_rec = p.op("scalar", lambda s_, r0=r0: s_.activation(
                    zre[r0:r0 + 1, :], zre[r0:r0 + 1, :], AF.Exp, scale=-1.0))
            pss.close()
            pss = PsumScope()
            ps_zrep = pss.get([128, S])
            ps_rem = pss.get([128, S])
            v_rp = None
            for hh in range(2):
                for ch in range(4):
                    csl = slice(512 * ch, 512 * (ch + 1))
                    v_rp = mm(ps_zrep[64 * hh:64 * hh + 64, csl],
                              sb["onesf32"][32 * hh:32 * hh + 1, 0:64],
                              zre[32 * hh:32 * hh + 1, csl], True, True,
                              wait=[("act", v_rec)] if hh == 0 and ch == 0 else (),
                              inc="pe" if hh == 1 and ch == 3 else None)
            v_nrm = p.op("vector", lambda v, pzr=ps_zrep: v.tensor_mul(
                elat[:], elat[:], pzr[:]), wait=[("pe", v_rp)])
            v_rm = None
            for hh in range(2):
                h = 2 * lhp + hh
                for ch in range(4):
                    csl = slice(512 * ch, 512 * (ch + 1))
                    v_rm = mm(ps_rem[64 * hh:64 * hh + 64, csl],
                              gvbf[64 * hh:64 * hh + 64, 64 * h:64 * (h + 1)],
                              elat[64 * hh:64 * hh + 64, csl], True, True,
                              wait=[("dve", v_nrm)] if hh == 0 and ch == 0 else (),
                              inc="pe" if hh == 1 and ch == 3 else None)
            p.op("vector", lambda v, lhp=lhp, pr=ps_rem: v.tensor_copy(
                remf[lhp][:], pr[:]), wait=[("pe", v_rm)])
            pss.close()
        pss = PsumScope()
        ps_r2 = [pss.get([128, 512]) for _ in range(2)]
        ps_trr = [pss.get([128, 128], BF16) for _ in range(2)]
        for t in range(NT):
            tsl = slice(128 * t, 128 * (t + 1))
            v_r2 = None
            for half in range(2):
                pk = ps_r2[half]
                osl = slice(512 * half, 512 * (half + 1))
                for j in range(4):
                    v_r2 = mm(pk[:], remf[j][:, tsl], routl[j][:, osl],
                              j == 0, j == 3,
                              wait=[("dve", p.val("dve"))] if j == 0 else (),
                              inc="pe" if j == 3 else None)
                p.op("vector", lambda v, pk=pk, osl=osl: v.tensor_copy(
                    rem2b[:, osl], pk[:]), wait=[("pe", v_r2)])
            for c in range(NC):
                wvt = [("dve", p.val("dve"))] if c == 0 else []
                if c >= 2 or t >= 1:
                    wvt.append(("act", p.val("act")))
                v_t = tr(ps_trr[c % 2][:], rem2b[:, 128 * c:128 * (c + 1)],
                         wait=wvt, inc="pe")
                p.op("scalar", lambda s_, c=c, t=t: s_.copy(
                    rem2f[c][:, 128 * t:128 * (t + 1)], ps_trr[c % 2][:]),
                    wait=[("pe", v_t)])
        p.wait("tensor", [("act", p.val("act")), ("dve", p.val("dve"))])
        pss.close()
        p.barrier()

    def b1_local(s, hp):
        rcol_all = sb["colst"][:, CO["r"]:CO["r"] + 16]
        mi64v = wview(49152, [128, S], BF16, "mi64")
        rem2f = [wview(c * 4096, [128, S], BF16, "rem2f") for c in range(NC)]
        vbl = aview(BB_VBL, [128, S], BF16, "latm2")
        qlf = aview(BB_QLF, [128, S], BF16, "gqf")
        klf = aview(BB_KLF, [128, S], BF16, "gqf")
        eblb = [aview(BB_EBL + i * 1536, [128, 768], BF16, "eblb") for i in range(2)]
        sg = aview(BB_SG, [33, S], F32, "sgrow")
        sg2 = aview(BB_SG2, [33, S], F32, "sgrow2")
        zrl = aview(BB_ZRL, [33, S], F32, "zre")
        ubf = aview(BB_UBF, [33, S], BF16, "ubf")
        grbf = aview(BB_GRBF, [33, S], BF16, "grbf")
        localf = aview(BB_LOCALF, [128, S], BF16, "localf")
        t1buf = aview(BB_T1, [128, S], BF16, "t1buf")
        qkvtmp = aview(BB_QKVT, [128, 384], BF16, "qkvtmp")
        xslb = [aview(BB_XSL + sl * 2048, [128, 1024], BF16, "xslab2")
                for sl in range(2)]
        wlpk = [wview(32768 + c * 768, [128, 384], BF16, "wlpk") for c in range(NC)]
        cs3 = wview(38912, [64, 384], BF16, "cs3")
        for c in range(NC):
            for k3, stname in enumerate(("st_wq", "st_wk", "st_wv")):
                p.dma("sync", lambda e, c=c, k3=k3, stname=stname:
                      e.dma_start(out=wlpk[c][:, 128 * k3:128 * (k3 + 1)],
                                  in_=st[stname][c][:, 128 * hp:128 * (hp + 1)]))
        for k3, ro in enumerate(("colsum_wq", "colsum_wk", "colsum_wv")):
            p.dma("gpsimd", lambda e, k3=k3, ro=ro: e.dma_start(
                out=cs3[:, 128 * k3:128 * (k3 + 1)],
                in_=st["st_rows"][0:1, RO[ro] + 128 * hp:RO[ro] + 128 * (hp + 1)]
                .to_broadcast([64, 128])))
        gml = p.dmark()
        p.wait("tensor", gml)
        p.wait("vector", gml)
        pss = PsumScope()
        ps_qkv = [pss.get([128, 512]) for _ in range(2)]
        ps_trq = [pss.get([128, 128], BF16) for _ in range(2)]

        def xb_load3(t, wv):
            return p.dma("sync", lambda e, t=t: e.dma_start(
                out=xslb[t % 2][:].rearrange("p (c u) -> p c u", u=128),
                in_=st["st_xbf"][:, :, 128 * t:128 * (t + 1)]
                .rearrange("c p u -> p c u")), wait=wv)

        xtok = {0: xb_load3(0, ()), 1: xb_load3(1, ())}
        ev2 = {}
        for t in range(NT):
            bi = t % 2
            tsl = slice(128 * t, 128 * (t + 1))
            pg = ps_qkv[bi]
            wv0 = [xtok[t]]
            if t - 2 in ev2:
                wv0.append(("dve", ev2[t - 2]))
            for c in range(NC):
                mm(pg[:, 0:384], xslb[bi][:, 128 * c:128 * (c + 1)],
                   wlpk[c][:], c == 0, False, wait=wv0 if c == 0 else ())
            v_pj = mm(pg[:, 0:384], mi64v[0:64, tsl], cs3[:], False, True, inc="pe")
            if t + 2 < NT:
                xtok[t + 2] = xb_load3(t + 2, [("pe", v_pj)])
            v_sc = p.op("vector", lambda v, t=t, pg=pg: v.tensor_scalar(
                qkvtmp[:], pg[:, 0:384], rcol_all[:, t:t + 1], None, ALU.mult),
                wait=[("pe", v_pj)])
            v_vv = p.op("vector", lambda v, t=t: v.tensor_copy(
                vbl[:, 128 * t:128 * (t + 1)], qkvtmp[:, 256:384]))
            ev2[t] = v_vv
            for j, dstf in ((0, qlf), (1, klf)):
                wvt = [("dve", v_vv)] if j == 0 else []
                if t >= 1:
                    wvt.append(("act", p.val("act")))
                v_t = tr(ps_trq[j][:], qkvtmp[:, 128 * j:128 * (j + 1)],
                         wait=wvt, inc="pe")
                p.op("scalar", lambda s_, dstf=dstf, t=t, j=j: s_.copy(
                    dstf[:, 128 * t:128 * (t + 1)], ps_trq[j][:]),
                    wait=[("pe", v_t)])
        p.wait("tensor", [("act", p.val("act"))])
        pss.close()
        if "b1proj" in DBG:
            p.barrier()
            return
        pss = PsumScope()
        ps_g1 = pss.get([33, S])
        ps_g2 = pss.get([33, S])
        v_g = None
        for hh in range(2):
            h = 2 * hp + hh
            for gi, (pgx, coloff) in enumerate(((ps_g1, 0), (ps_g2, NH))):
                for ch in range(4):
                    csl = slice(512 * ch, 512 * (ch + 1))
                    v_g = mm(pgx[32 * hh:32 * hh + 1, csl],
                             sb["glwt"][64 * hh:64 * hh + 64,
                                        coloff + h:coloff + h + 1],
                             qlf[64 * hh:64 * hh + 64, csl], True, True,
                             wait=[("act", p.val("act"))]
                             if hh == 0 and gi == 0 and ch == 0 else (),
                             inc="pe" if (hh == 1 and gi == 1 and ch == 3) else None)
        for r0 in (0, 32):
            p.op("scalar", lambda s_, r0=r0: s_.activation(
                sg[r0:r0 + 1, :], ps_g1[r0:r0 + 1, :], AF.Exp, scale=-1.0),
                wait=[("pe", v_g)] if r0 == 0 else ())
        v_e2 = None
        for r0 in (0, 32):
            v_e2 = p.op("scalar", lambda s_, r0=r0: s_.activation(
                sg2[r0:r0 + 1, :], ps_g2[r0:r0 + 1, :], AF.Exp, scale=-1.0))
        v_p1 = None
        for r0 in (0, 32):
            p.op("vector", lambda v, r0=r0: v.tensor_scalar(
                sg[r0:r0 + 1, :], sg[r0:r0 + 1, :], 1.0, None, ALU.add),
                wait=[("act", v_e2)] if r0 == 0 else ())
            v_p1 = p.op("vector", lambda v, r0=r0: v.tensor_scalar(
                sg2[r0:r0 + 1, :], sg2[r0:r0 + 1, :], 1.0, None, ALU.add))
        v_sig = None
        for r0 in (0, 32):
            for buf in (sg, sg2):
                p.op("scalar", lambda s_, r0=r0, buf=buf: s_.activation(
                    buf[r0:r0 + 1, :], buf[r0:r0 + 1, :], AF.Ln),
                    wait=[("dve", v_p1)] if r0 == 0 and buf is sg else ())
                v_sig = p.op("scalar", lambda s_, r0=r0, buf=buf: s_.activation(
                    buf[r0:r0 + 1, :], buf[r0:r0 + 1, :], AF.Exp, scale=-1.0))
        pss.close()
        if "b1gate" in DBG:
            p.barrier()
            return
        pss = PsumScope()
        ps_sl = [[pss.get([128, 512]) for _ in range(2)] for _ in range(2)]
        ps_av = [pss.get([128, 128]) for _ in range(2)]
        ps_zt = [pss.get([65, 128]) for _ in range(2)]
        for i in range(2):
            p.op("vector", lambda v, i=i: v.memset(eblb[i][:], 0.0))
        v_lf = {}
        v_zev = {}
        for t in range(NT):
            bi = t % 2
            tsl = slice(128 * t, 128 * (t + 1))
            kts = [t - 2, t - 1, t]
            v_s = None
            for ri, kt in enumerate(kts):
                if kt < 0:
                    continue
                ksl = slice(128 * kt, 128 * (kt + 1))
                for hh in range(2):
                    wv = []
                    if v_s is None:
                        wv = [("act", p.val("act")), ("dve", p.val("dve"))]
                        if t - 2 in v_lf:
                            wv.append(("act", v_lf[t - 2]))
                    v_s = mm(ps_sl[bi][hh][:, 128 * ri:128 * (ri + 1)],
                             klf[64 * hh:64 * hh + 64, ksl],
                             qlf[64 * hh:64 * hh + 64, tsl], True, True,
                             wait=wv, inc=None)
            if "nohack" not in DBG:
                p.q["tensor"][-1] = (p.q["tensor"][-1][0], p.q["tensor"][-1][1],
                                     "pe", 1)
                p.cnt["pe"] += 1
                v_s = p.cnt["pe"]
            if "noexp" in DBG:
                continue
            lo = 256 if t == 0 else (128 if t == 1 else 0)
            v_e = None
            for hh in range(2):
                v_e = p.op("scalar", lambda s_, bi=bi, hh=hh, lo=lo: s_.activation(
                    eblb[bi][:, 384 * hh + lo:384 * (hh + 1)],
                    ps_sl[bi][hh][:, lo:384], AF.Exp, scale=0.125),
                    wait=[("pe", v_s)] if hh == 0 else ())
            ee = eblb[bi][:].rearrange("p (h m) -> p h m", h=2)
            lmb = sb["lmask"][:].unsqueeze(1).to_broadcast([128, 2, 384])
            if "nomask" in DBG:
                v_m = p.op("vector", lambda v: v.engine_nop(),
                           wait=[("act", v_e)])
            else:
                v_m = p.op("vector", lambda v, ee=ee, lmb=lmb: v.tensor_tensor(
                    ee, ee, lmb, ALU.mult), wait=[("act", v_e)])
            v_av = None
            v_z = None
            for hh in range(2):
                if "nozmm" in DBG:
                    break
                first = True
                for ri, kt in enumerate(kts):
                    if kt < 0:
                        continue
                    rsl = slice(384 * hh + 128 * ri, 384 * hh + 128 * (ri + 1))
                    wvz = []
                    if hh == 0 and first:
                        wvz = [("dve", v_m)]
                        if t - 2 in v_zev:
                            wvz.append(("act", v_zev[t - 2]))
                    v_z = mm(ps_zt[bi][64 * hh:64 * hh + 1, 0:128],
                             sb["onecolbf"][:], eblb[bi][:, rsl], first, kt == t,
                             wait=wvz,
                             inc="pe" if (hh == 1 and kt == t) else None)
                    first = False
            for hh in range(2):
                if "noav" in DBG:
                    break
                first = True
                for ri, kt in enumerate(kts):
                    if kt < 0:
                        continue
                    rsl = slice(384 * hh + 128 * ri, 384 * hh + 128 * (ri + 1))
                    v_av = mm(ps_av[bi][64 * hh:64 * hh + 64, 0:128],
                              vbl[:, 128 * kt + 64 * hh:128 * kt + 64 * (hh + 1)],
                              eblb[bi][:, rsl], first, kt == t,
                              wait=[("dve", v_m)] if (hh == 0 and first
                                                      and "nozmm" in DBG) else (),
                              inc="pe" if (hh == 1 and kt == t) else None)
                    first = False
            if "nozmm" not in DBG:
                p.op("scalar", lambda s_, bi=bi, tsl=tsl: s_.copy(
                    zrl[0:1, tsl], ps_zt[bi][0:1, :]), wait=[("pe", v_z)])
                v_zev[t] = p.op("scalar", lambda s_, bi=bi, tsl=tsl: s_.copy(
                    zrl[32:33, tsl], ps_zt[bi][64:65, :]))
            if "noav" not in DBG:
                v_lf[t] = p.op("scalar", lambda s_, bi=bi, tsl=tsl: s_.copy(
                    localf[:, tsl], ps_av[bi][:]), wait=[("pe", v_av)])
        p.wait("vector", [("pe", p.val("pe")), ("act", p.val("act"))])
        if "b1attn" in DBG:
            pss.close()
            p.barrier()
            return
        v_zi = None
        for r0 in (0, 32):
            p.op("scalar", lambda s_, r0=r0: s_.activation(
                zrl[r0:r0 + 1, :], zrl[r0:r0 + 1, :], AF.Ln))
            v_zi = p.op("scalar", lambda s_, r0=r0: s_.activation(
                zrl[r0:r0 + 1, :], zrl[r0:r0 + 1, :], AF.Exp, scale=-1.0))
        v_gr = None
        for r0 in (0, 32):
            p.op("vector", lambda v, r0=r0: v.tensor_mul(
                ubf[r0:r0 + 1, :], sg[r0:r0 + 1, :], zrl[r0:r0 + 1, :]),
                wait=[("act", v_zi)] if r0 == 0 else ())
            v_gr = p.op("vector", lambda v, r0=r0: v.tensor_copy(
                grbf[r0:r0 + 1, :], sg2[r0:r0 + 1, :]))
        pss.close()
        pss = PsumScope()
        ps_u = pss.get([128, S])
        ps_gr = pss.get([128, S])
        v_rep = None
        for hh in range(2):
            for ch in range(4):
                csl = slice(512 * ch, 512 * (ch + 1))
                mm(ps_u[64 * hh:64 * hh + 64, csl],
                   sb["onesb32"][32 * hh:32 * hh + 1, 0:64],
                   ubf[32 * hh:32 * hh + 1, csl], True, True,
                   wait=[("dve", v_gr)] if hh == 0 and ch == 0 else ())
                v_rep = mm(ps_gr[64 * hh:64 * hh + 64, csl],
                           sb["onesb32"][32 * hh:32 * hh + 1, 0:64],
                           grbf[32 * hh:32 * hh + 1, csl], True, True,
                           inc="pe" if hh == 1 and ch == 3 else None)
        v_t1 = p.op("vector", lambda v: v.tensor_mul(t1buf[:], localf[:], ps_u[:]),
                    wait=[("pe", v_rep)])
        v_t2 = p.op("vector", lambda v, hp=hp: v.tensor_mul(
            rem2f[hp][:], rem2f[hp][:], ps_gr[:]))
        v_t3 = p.op("vector", lambda v, hp=hp: v.tensor_add(
            rem2f[hp][:], rem2f[hp][:], t1buf[:]))
        p.wait("tensor", [("dve", v_t3)])
        pss.close()
        p.barrier()

    def b_out(s):
        rem2f = [wview(c * 4096, [128, S], BF16, "rem2f") for c in range(NC)]
        woutl = [wview(32768 + i * 2048, [128, D], BF16, "woutl") for i in range(NC)]
        for i in range(NC):
            p.dma("sync", lambda e, i=i: e.dma_start(
                out=woutl[i][:], in_=st["st_wout"][i]))
        gml = p.dmark()
        p.wait("tensor", gml)
        pss = PsumScope()
        po = [pss.get([128, 512]) for _ in range(2)]
        ep_done = {}
        for oc in range(NC):
            for tg in range(4):
                tsl = slice(512 * tg, 512 * (tg + 1))
                idx = oc * 4 + tg
                pk = po[idx % 2]
                wv = [("dve", p.val("dve"))] if idx == 0 else ()
                if idx >= 2:
                    wv = [("dve", ep_done[idx - 2])]
                v_mm = None
                for i in range(NC):
                    v_mm = mm(pk[:], woutl[i][:, 128 * oc:128 * (oc + 1)],
                              rem2f[i][:, tsl], i == 0, i == NC - 1,
                              wait=wv if i == 0 else (),
                              inc="pe" if i == NC - 1 else None)
                v_ep = p.op("vector", lambda v, oc=oc, tsl=tsl, pk=pk:
                            v.tensor_add(X[oc][:, tsl], X[oc][:, tsl], pk[:]),
                            wait=[("pe", v_mm)], inc="dve")
                ep_done[idx] = v_ep
        p.barrier()
        pss.close()

    prep_consts()
    if ENABLE["atta"] and "noaprep" not in DBG:
        prep_atta()
    if ENABLE["ffn"] and "noprep" not in DBG:
        prep_ffn()
    if ENABLE["attb"] and "nobprep" not in DBG:
        prep_attb()
    for s in range(n_seq):
        load_x(s)
        if ENABLE["atta"] and "noasub" not in DBG:
            atta_sub(s)
        if ENABLE["ffn"] and "nosub" not in DBG:
            ffn_sub(s)
        if ENABLE["attb"] and "nobsub" not in DBG:
            attb_sub(s)
        if ENABLE["ffn"] and "nosub" not in DBG and "one" not in DBG:
            ffn_sub(s)
        store_y(s)
    p.wait("sync", p.dmark())
    p.wait("gpsimd", p.dmark())

    p.finalize(block, sems)
    est.close()
    return nc



WMAP = {
    "aa_qn_w": "L0_attn_a_qn_w", "aa_qn_b": "L0_attn_a_qn_b",
    "aa_q_w": "L0_attn_a_q_w", "aa_q_b": "L0_attn_a_q_b",
    "aa_kvn_w": "L0_attn_a_kvn_w", "aa_kvn_b": "L0_attn_a_kvn_b",
    "aa_kv_w": "L0_attn_a_kv_w", "aa_kv_b": "L0_attn_a_kv_b",
    "aa_out_w": "L0_attn_a_out_w", "aa_out_b": "L0_attn_a_out_b",
    "aa_ln_w": "L0_attn_a_ln_w", "aa_ln_b": "L0_attn_a_ln_b",
    "aa_rh_w": "L0_attn_a_rh_w", "aa_rh_b": "L0_attn_a_rh_b",
    "ab_wq": "L0_attn_b_wq", "ab_wk": "L0_attn_b_wk", "ab_wv": "L0_attn_b_wv",
    "ab_wgq": "L0_attn_b_wgq", "ab_wgk": "L0_attn_b_wgk", "ab_wgv": "L0_attn_b_wgv",
    "ab_r_out_w": "L0_attn_b_r_out_w", "ab_glw": "L0_attn_b_gate_local_w",
    "ab_grw": "L0_attn_b_gate_remote_w", "ab_w_out": "L0_attn_b_w_out",
    "ffn_wr": "L0_ffn_w_freq_real", "ffn_wi": "L0_ffn_w_freq_imag",
    "ffn_bias": "L0_ffn_bias",
}


def kernel(**inputs):
    ht = host_tables()
    ht["tab_glwT"] = np.ascontiguousarray(np.concatenate(
        [np.asarray(inputs["L0_attn_b_gate_local_w"]).T,
         np.asarray(inputs["L0_attn_b_gate_remote_w"]).T], axis=1).astype(np.float32))
    nc = build_program()
    x = np.ascontiguousarray(inputs["L0_attn_a_x"])
    in_maps = []
    for core in range(NCORES):
        m = {"xs": np.ascontiguousarray(x[core * NSEQ:(core + 1) * NSEQ])}
        for k, src in WMAP.items():
            m[k] = np.ascontiguousarray(inputs[src])
        m.update(ht)
        in_maps.append(m)
    res = run_bass_kernel_spmd(nc, in_maps, core_ids=list(range(NCORES)))
    return np.concatenate([r["ys"] for r in res.results], axis=0)
